# version 14
# baseline (speedup 1.0000x reference)
import numpy as np
import ml_dtypes
import concourse.bass as bass
import concourse.mybir as mybir
from concourse.bass_utils import run_bass_kernel_spmd
from contextlib import ExitStack

BFNP = ml_dtypes.bfloat16
F32 = mybir.dt.float32
BF16 = mybir.dt.bfloat16
U32 = mybir.dt.uint32
AF = mybir.ActivationFunctionType
ALU = mybir.AluOpType
AX = mybir.AxisListType

ENGS = ("sync", "tensor", "vector", "scalar", "gpsimd")


class Sem:
    __slots__ = ("name", "n", "h")

    def __init__(self, name):
        self.name, self.n, self.h = name, 0, None


class P:

    def __init__(self, nc):
        self.nc = nc
        self.q = {e: [] for e in ENGS}
        self.last_wait = {e: {} for e in ENGS}
        self.serialize = {}

    def op(self, eng, fn=None, w=(), inc=None, inc_n=1):
        own = self.serialize.get(eng)
        if own is not None and fn is not None:
            w = list(w) + [(own, own.n)]
            if inc is None:
                inc = own
                inc_n = 1
        if inc is not None:
            inc.n += inc_n
        self.q[eng].append((tuple(w), fn, (inc, inc_n) if inc is not None else None))
        return inc.n if inc is not None else None

    def flush(self):
        if not any(self.q[e] for e in ENGS):
            return
        with self.nc.Block() as block:
            for eng in ENGS:
                ops = self.q[eng]
                if not ops:
                    continue
                lastw = self.last_wait[eng]

                def body(E_, ops=ops, lastw=lastw):
                    for waits, fn, inc in ops:
                        for (s, v) in waits:
                            if lastw.get(id(s), 0) < v:
                                E_.wait_ge(s.h, v)
                                lastw[id(s)] = v
                        ins = fn(E_) if fn is not None else None
                        if inc is not None:
                            ins.then_inc(inc[0].h, inc[1])
                getattr(block, eng)(body)
        self.q = {e: [] for e in ENGS}


class CFG:
    def __init__(self, TL=2048, SEQ=8192, QW=512):
        self.TL, self.SEQ, self.QW = TL, SEQ, QW
        self.NTB = TL // QW
        self.NTT = TL // 128
        self.NQB = SEQ // QW
        self.DIAG = QW // 128
        self.D, self.F, self.E, self.DC, self.HL = 1024, 4096, 8, 8, 4
        self.DBG = False
        self.FH = self.F // 2
        self.NFK = self.FH // 128


def build_program(cfg: CFG):
    nc = bass.Bass("TRN2", target_bir_lowering=False)
    TL, SEQ, QW, D, E, DC = cfg.TL, cfg.SEQ, cfg.QW, cfg.D, cfg.E, cfg.DC
    NTB, NTT, NQB, DIAG = cfg.NTB, cfg.NTT, cfg.NQB, cfg.DIAG
    NFK, FH, HL, F = cfg.NFK, cfg.FH, cfg.HL, cfg.F
    NKT = SEQ // 128
    HVS = 2 if cfg.NTB >= 2 else 1

    xin = nc.dram_tensor("x", [D, TL], F32, kind="ExternalInput").ap()
    w1in = nc.dram_tensor("w1", [E, D, F], F32, kind="ExternalInput").ap()
    w2in = nc.dram_tensor("w2", [E, F, D], F32, kind="ExternalInput").ap()
    b1in = nc.dram_tensor("b1l", [128, 2 * E * NFK], F32, kind="ExternalInput").ap()
    b2in = nc.dram_tensor("b2", [E, D], F32, kind="ExternalInput").ap()
    gwin = nc.dram_tensor("gwl", [128, DC * E], F32, kind="ExternalInput").ap()
    gbin = nc.dram_tensor("gb", [1, E], F32, kind="ExternalInput").ap()
    wqkvin = nc.dram_tensor("wqkvl", [128, DC * 768], F32, kind="ExternalInput").ap()
    wpjin = nc.dram_tensor("wpjt", [128, DC * D], F32, kind="ExternalInput").ap()
    bprin = nc.dram_tensor("bpr", [1, D], F32, kind="ExternalInput").ap()
    idbin = nc.dram_tensor("idb", [128, 128], BF16, kind="ExternalInput").ap()
    mskin = nc.dram_tensor("masks", [128, DIAG * QW], BF16, kind="ExternalInput").ap()
    ebin = nc.dram_tensor("ebasis", [E, E * 128], BF16, kind="ExternalInput").ap()
    rkin = nc.dram_tensor("rank", [1, 1], U32, kind="ExternalInput").ap()
    yout = nc.dram_tensor("out", [D, TL], F32, kind="ExternalOutput").ap()
    if cfg.DBG:
        dbg1 = nc.dram_tensor("dbg1", [D, TL], F32, kind="ExternalOutput").ap()
        dbg2 = nc.dram_tensor("dbg2", [D, TL], F32, kind="ExternalOutput").ap()
        dbg3 = nc.dram_tensor("dbg3", [HL * 64, SEQ], BF16, kind="ExternalOutput").ap()
        dbg4 = nc.dram_tensor("dbg4", [E, TL], BF16, kind="ExternalOutput").ap()
        dbg5 = nc.dram_tensor("dbg5", [128, 128], F32, kind="ExternalOutput").ap()
        dbg6 = nc.dram_tensor("dbg6", [128, TL], BF16, kind="ExternalOutput").ap()

    w1bf = nc.dram_tensor("w1bf", [E, 128, DC * F], BF16)
    w2bf = nc.dram_tensor("w2bf", [E, 128, (F // 128) * D], BF16)
    wqkvbf = nc.dram_tensor("wqkvbf", [128, DC * 768], BF16)
    wpjbf = nc.dram_tensor("wpjbf", [128, DC * D], BF16)
    ag_in2 = [nc.dram_tensor(f"ag_in{i}", [D, TL // HVS], BF16) for i in range(HVS)]
    ag_out2 = [nc.dram_tensor(f"ag_out{i}", [4 * D, TL // HVS], BF16)
               for i in range(HVS)]
    o_dram = nc.dram_tensor("o_dram", [HL * 64, SEQ], BF16)
    oag_h = [nc.dram_tensor(f"oag_h{i}", [4 * 64, SEQ], BF16) for i in range(HL)]

    p = P(nc)
    dS, dG = Sem("dS"), Sem("dG")
    tE, vE, sE = Sem("tE"), Sem("vE"), Sem("sE")
    p.serialize = {"vector": vE, "scalar": sE}
    ccA, ccB = Sem("ccA"), Sem("ccB")
    ccA2 = [Sem("ccA0"), Sem("ccA1")]
    ccB4 = [Sem(f"ccB{i}") for i in range(4)]
    dP = [Sem("dP0"), Sem("dP1")]
    dQ = [Sem("dQ0"), Sem("dQ1")]
    gE = Sem("gE")
    sems = [dS, dG, tE, vE, sE, ccA, ccB, gE] + dP + dQ + ccA2 + ccB4
    own = {"sync": {dS, dP[0], dP[1]},
           "gpsimd": {dG, ccA, ccB, dQ[0], dQ[1], gE, *ccA2, *ccB4},
           "tensor": {tE}, "vector": {vE}, "scalar": {sE}}

    def barrier():
        snap = [(s, s.n) for s in sems if s.n > 0]
        for eng in ENGS:
            w = tuple((s, v) for (s, v) in snap if s not in own[eng])
            if w:
                p.q[eng].append((w, None, None))

    def dma(eng, sem, out, in_, w=(), fence=False):
        if fence and sem.n:
            w = [(sem, sem.n)] + list(w)
        return p.op(eng, lambda E_, o=out, i=in_: E_.dma_start(out=o, in_=i),
                    w=w, inc=sem, inc_n=16)

    st = ExitStack()
    with st:
        for s in sems:
            s.h = st.enter_context(nc.semaphore(s.name))
        SB = lambda name, shape, dt: st.enter_context(nc.sbuf_tensor(name, shape, dt))
        x_sb = SB("x_sb", [128, DC * TL], F32)
        ones_cf = SB("ones_cf", [128, 1], F32)
        ones_cb = SB("ones_cb", [128, 1], BF16)
        ones_rf = SB("ones_rf", [1, 128], F32)
        ones_rb = SB("ones_rb", [1, 128], BF16)
        id_sb = SB("id_sb", [128, 128], BF16)
        eb_sb = SB("eb_sb", [E, E * 128], BF16)
        b1_sb = SB("b1_sb", [128, 2 * E * NFK], F32)
        b2_sb = SB("b2_sb", [E, D], BF16)
        gw_sb = SB("gw_sb", [128, DC * E], BF16)
        gb_sb = SB("gb_sb", [1, E], BF16)
        rk_sb = SB("rk_sb", [1, 2], U32)

        p.op("vector", lambda E_: E_.memset(ones_cf[:, :], 1.0))
        p.op("vector", lambda E_: E_.memset(ones_cb[:, :], 1.0))
        p.op("vector", lambda E_: E_.memset(ones_rf[:, :], 1.0))
        p.op("vector", lambda E_: E_.memset(ones_rb[:, :], 1.0), inc=vE)

        for dc in range(DC):
            dma("sync", dS, x_sb[:, dc * TL:(dc + 1) * TL],
                xin[dc * 128:(dc + 1) * 128, :])
        dma("sync", dS, id_sb[:, :], idbin)
        dma("sync", dS, eb_sb[:, :], ebin)
        dma("sync", dS, b1_sb[:, :], b1in)
        dma("sync", dS, rk_sb[:, 0:1], rkin)

        stageA = SB("stageA", [128, 1024], F32)
        stageB = SB("stageB", [128, 1024], F32)
        chkA = SB("chkA", [128, 1024], BF16)
        chkB = SB("chkB", [128, 1024], BF16)
        stg, chks = [stageA, stageB], [chkA, chkB]
        cstate = {"i": 0, "conv_v": [0, 0]}

        def conv(dst_dram, src_dram, ncols, scales=None):
            i = cstate["i"] % 2
            cstate["i"] += 1
            w = [(gE, cstate["conv_v"][i])] if cstate["conv_v"][i] else []
            dv = p.op("sync", lambda E_, o=stg[i][:, 0:ncols], s=src_dram:
                      E_.dma_start(out=o, in_=s), w=w, inc=dP[i], inc_n=16)
            w2_ = [(dP[i], dv), (dQ[i], dQ[i].n)]
            parts = scales or [(0, ncols, None)]
            for k, (lo, hi, fac) in enumerate(parts):
                lastp = (k == len(parts) - 1)
                if fac is None:
                    p.op("gpsimd", lambda E_, i=i, lo=lo, hi=hi: E_.tensor_copy(
                        chks[i][:, lo:hi], stg[i][:, lo:hi]),
                        w=(w2_ if k == 0 else ()), inc=(gE if lastp else None))
                else:
                    p.op("gpsimd", lambda E_, i=i, lo=lo, hi=hi, fac=fac:
                         E_.tensor_scalar_mul(chks[i][:, lo:hi], stg[i][:, lo:hi], fac),
                         w=(w2_ if k == 0 else ()), inc=(gE if lastp else None))
            cstate["conv_v"][i] = gE.n
            p.op("gpsimd", lambda E_, o=dst_dram, s=chks[i][:, 0:ncols]:
                 E_.dma_start(out=o, in_=s), w=[(gE, gE.n)], inc=dQ[i], inc_n=16)

        def conv_expert(e, part=0, nparts=1):
            jobs = []
            for dc in range(DC):
                for hf in range(4):
                    jobs.append((w1bf.ap()[e, :, dc * F + hf * 1024:
                                           dc * F + (hf + 1) * 1024],
                                 w1in[e, dc * 128:(dc + 1) * 128,
                                      hf * 1024:(hf + 1) * 1024]))
            for fk in range(F // 128):
                jobs.append((w2bf.ap()[e, :, fk * D:(fk + 1) * D],
                             w2in[e, fk * 128:(fk + 1) * 128, :]))
            n = len(jobs)
            lo = n * part // nparts
            hi = n * (part + 1) // nparts
            for dst, s_ in jobs[lo:hi]:
                conv(dst, s_, 1024)
            if hi == n:
                conv_done[e] = (dQ[0].n, dQ[1].n)

        conv_done = {}
        d1 = dma("sync", dS, stageA[0:1, 0:E], gbin, fence=True)
        g1 = p.op("gpsimd", lambda E_: E_.tensor_copy(gb_sb[:, :], stageA[0:1, 0:E]),
                  w=[(dS, d1)], inc=gE)
        cstate["conv_v"][0] = g1
        d2 = dma("sync", dS, stageB[:, 0:DC * E], gwin, fence=True)
        g2 = p.op("gpsimd", lambda E_: E_.tensor_copy(gw_sb[:, :], stageB[:, 0:DC * E]),
                  w=[(dS, d2)], inc=gE)
        cstate["conv_v"][1] = g2
        d3 = dma("sync", dS, stageA[0:E, 0:D], b2in,
                 w=[(gE, cstate["conv_v"][0])], fence=True)
        g3 = p.op("gpsimd", lambda E_: E_.tensor_copy(b2_sb[:, :], stageA[0:E, 0:D]),
                  w=[(dS, d3)], inc=gE)
        cstate["conv_v"][0] = g3
        conv_expert(0)
        for dc in range(DC):
            conv(wqkvbf.ap()[:, dc * 768:(dc + 1) * 768],
                 wqkvin[:, dc * 768:(dc + 1) * 768], 768,
                 scales=[(0, 256, 0.125), (256, 768, None)])
        for j in range(DC * D // 1024):
            conv(wpjbf.ap()[:, j * 1024:(j + 1) * 1024],
                 wpjin[:, j * 1024:(j + 1) * 1024], 1024)
        barrier()
        p.flush()

        def emit_ln(xn_sb, s_):
            (ps_sx, ps_sq, ps_bc, rowm, rowa, rowb, rowc, Rb, C2b, sq_sb) = s_
            last_sq_t = [0, 0]
            last_norm_v = 0
            tb_done = []
            for tb in range(NTB):
                for dc in range(DC):
                    i = dc % 2
                    xs = x_sb[:, dc * TL + tb * QW: dc * TL + (tb + 1) * QW]
                    w = [(tE, last_sq_t[i])] if last_sq_t[i] else []
                    sv = p.op("scalar", lambda E_, i=i, xs=xs: E_.activation(
                        sq_sb[:, i * QW:(i + 1) * QW], xs, AF.Square), w=w, inc=sE)
                    p.op("tensor", lambda E_, xs=xs, dc=dc: E_.matmul(
                        ps_sx[:, :], ones_cf[:, :], xs,
                        start=(dc == 0), stop=(dc == DC - 1)))
                    tv = p.op("tensor", lambda E_, i=i, dc=dc: E_.matmul(
                        ps_sq[:, :], ones_cb[:, :], sq_sb[:, i * QW:(i + 1) * QW],
                        start=(dc == 0), stop=(dc == DC - 1)),
                        w=[(sE, sv)], inc=tE)
                    last_sq_t[i] = tv
                t_stats = tE.n
                p.op("scalar", lambda E_: E_.mul(rowm[:, :], ps_sx[0:1, :], 1.0 / D),
                     w=[(tE, t_stats)])
                s_rows = p.op("scalar", lambda E_: E_.mul(rowa[:, :], ps_sq[0:1, :],
                                                          1.0 / D), inc=sE)
                p.op("vector", lambda E_: E_.tensor_mul(rowb[:, :], rowm[:, :],
                                                        rowm[:, :]), w=[(sE, s_rows)])
                p.op("vector", lambda E_: E_.tensor_sub(rowc[:, :], rowa[:, :],
                                                        rowb[:, :]))
                v_var = p.op("vector", lambda E_: E_.tensor_scalar_add(
                    rowb[:, :], rowc[:, :], 1e-5), inc=vE)
                s_std = p.op("scalar", lambda E_: E_.activation(
                    rowa[:, :], rowb[:, :], AF.Sqrt),
                    w=[(vE, v_var)], inc=sE)
                p.op("vector", lambda E_: E_.reciprocal(rowb[:, :], rowa[:, :]),
                     w=[(sE, s_std)])
                v_c2 = p.op("vector", lambda E_: E_.scalar_tensor_tensor(
                    rowa[:, :], rowm[:, :], -1.0, rowb[:, :], ALU.mult, ALU.mult),
                    inc=vE)
                w = [(vE, v_c2)]
                if last_norm_v:
                    w.append((vE, last_norm_v))
                t_b1 = p.op("tensor", lambda E_: E_.matmul(
                    ps_bc[:, :], ones_rf[:, :], rowb[:, :], start=True, stop=True),
                    w=w, inc=tE)
                v_rb = p.op("vector", lambda E_: E_.tensor_copy(Rb[:, :], ps_bc[:, :]),
                            w=[(tE, t_b1)], inc=vE)
                t_b2 = p.op("tensor", lambda E_: E_.matmul(
                    ps_bc[:, :], ones_rf[:, :], rowa[:, :], start=True, stop=True),
                    w=[(vE, v_rb)], inc=tE)
                v_cb2 = p.op("vector", lambda E_: E_.tensor_copy(C2b[:, :], ps_bc[:, :]),
                             w=[(tE, t_b2)], inc=vE)
                for dc in range(DC):
                    xs = x_sb[:, dc * TL + tb * QW: dc * TL + (tb + 1) * QW]
                    xns = xn_sb[:, dc * TL + tb * QW: dc * TL + (tb + 1) * QW]
                    p.op("vector", lambda E_, xs=xs: E_.tensor_mul(
                        ps_bc[:, :], xs, Rb[:, :]),
                        w=([(vE, v_cb2)] if dc == 0 else ()))
                    last_norm_v = p.op("vector", lambda E_, xns=xns: E_.tensor_add(
                        xns, ps_bc[:, :], C2b[:, :]), inc=vE)
                tb_done.append(vE.n)
            return tb_done

        def mk_ln_scope(tag, stx):
            return (stx.enter_context(nc.psum_tensor("ps_sx" + tag, [1, QW], F32)),
                    stx.enter_context(nc.psum_tensor("ps_sq" + tag, [1, QW], F32)),
                    stx.enter_context(nc.psum_tensor("ps_bc" + tag, [128, QW], F32)),
                    stx.enter_context(nc.sbuf_tensor("rowm" + tag, [1, QW], F32)),
                    stx.enter_context(nc.sbuf_tensor("rowa" + tag, [1, QW], F32)),
                    stx.enter_context(nc.sbuf_tensor("rowb" + tag, [1, QW], F32)),
                    stx.enter_context(nc.sbuf_tensor("rowc" + tag, [1, QW], F32)),
                    stx.enter_context(nc.sbuf_tensor("Rb" + tag, [128, QW], F32)),
                    stx.enter_context(nc.sbuf_tensor("C2b" + tag, [128, QW], F32)),
                    stx.enter_context(nc.sbuf_tensor("sq" + tag, [128, 2 * QW], BF16)))

        def emit_ffn(tag, pre_expert=None):
            sc = ExitStack()
            with sc:
                xn_sb = sc.enter_context(
                    nc.sbuf_tensor("xn" + tag, [128, DC * TL], BF16))
                rfm8 = sc.enter_context(nc.sbuf_tensor("rfm8" + tag, [E, TL], BF16))
                lg = sc.enter_context(nc.sbuf_tensor("lg" + tag, [128, E], F32))
                gsc = sc.enter_context(nc.sbuf_tensor("gsc" + tag, [128, 96], F32))
                rtm = sc.enter_context(nc.sbuf_tensor("rtm" + tag, [128, E], BF16))
                w1b = sc.enter_context(
                    nc.sbuf_tensor("w1b" + tag, [128, DC * FH], BF16))
                w2b = sc.enter_context(
                    nc.sbuf_tensor("w2b" + tag, [128, NFK * D], BF16))
                h_sb = sc.enter_context(
                    nc.sbuf_tensor("h" + tag, [128, NFK * QW], BF16))
                hraw = sc.enter_context(
                    nc.sbuf_tensor("hraw" + tag, [128, 2 * QW], BF16))
                web = sc.enter_context(
                    nc.sbuf_tensor("web" + tag, [128, 2 * QW], BF16))
                ps_h = [sc.enter_context(
                    nc.psum_tensor(f"ps_h{i}" + tag, [128, QW], F32))
                    for i in range(2)]
                ps_o = [sc.enter_context(
                    nc.psum_tensor(f"ps_o{i}" + tag, [128, QW], F32))
                    for i in range(4)]
                ps_w = sc.enter_context(
                    nc.psum_tensor("ps_w" + tag, [128, QW], F32))
                ps_tr = sc.enter_context(
                    nc.psum_tensor("ps_tr" + tag, [E, 128], BF16))
                hv32 = h_sb[:, :].bitcast(F32)
                Rb = hv32[:, 0:QW]
                C2b = hv32[:, QW:2 * QW]
                rowm = hv32[0:1, 2 * QW + 0 * QW:2 * QW + 1 * QW]
                rowa = hv32[0:1, 2 * QW + 1 * QW:2 * QW + 2 * QW]
                rowb = hv32[0:1, 2 * QW + 2 * QW:2 * QW + 3 * QW]
                rowc = hv32[0:1, 2 * QW + 3 * QW:2 * QW + 4 * QW]
                sq_sb = hraw[:, :]
                ps_sx = ps_o[0][0:1, 0:QW]
                ps_sq = ps_o[1][0:1, 0:QW]
                ps_bc = ps_o[2][:, 0:QW]
                ps_g = ps_w[:, 0:E]
                v_xn = emit_ln(xn_sb, (ps_sx, ps_sq, ps_bc, rowm, rowa, rowb,
                                       rowc, Rb, C2b, sq_sb))[-1]
                last_g_v = 0
                last_tr_s = 0
                for tt in range(NTT):
                    w = [(vE, v_xn)]
                    if last_g_v:
                        w.append((vE, last_g_v))
                    p.op("tensor", lambda E_: E_.matmul(
                        ps_g, ones_rb[:, :], gb_sb[:, :],
                        start=True, stop=False), w=w)
                    for dc in range(DC):
                        p.op("tensor", lambda E_, dc=dc, tt=tt: E_.matmul(
                            ps_g,
                            xn_sb[:, dc * TL + tt * 128: dc * TL + (tt + 1) * 128],
                            gw_sb[:, dc * E:(dc + 1) * E],
                            start=False, stop=(dc == DC - 1)),
                            inc=(tE if dc == DC - 1 else None))
                    t_g = tE.n
                    v_lg = p.op("vector", lambda E_: E_.tensor_copy(
                        lg[:, :], ps_g), w=[(tE, t_g)], inc=vE)
                    last_g_v = v_lg
                    mx = gsc[:, 0:8]
                    d1_, d2_ = gsc[:, 8:16], gsc[:, 16:24]
                    s1_, s2_ = gsc[:, 24:32], gsc[:, 32:40]
                    dm = gsc[:, 40:41]
                    w2c, wmc = gsc[:, 41:42], gsc[:, 42:43]
                    m1c, tm_, ngs = gsc[:, 43:51], gsc[:, 51:59], gsc[:, 59:67]
                    m2c, rmx = gsc[:, 67:75], gsc[:, 75:83]
                    p.op("vector", lambda E_: E_.max(mx, lg[:, :]))
                    p.op("vector", lambda E_: E_.tensor_scalar(
                        d1_, lg[:, :], gsc[:, 0:1], 1e-6, ALU.subtract, ALU.add))
                    p.op("vector", lambda E_: E_.tensor_scalar(
                        d2_, lg[:, :], gsc[:, 1:2], 1e-6, ALU.subtract, ALU.add))
                    v_g1 = p.op("vector", lambda E_: E_.tensor_sub(
                        dm, gsc[:, 1:2], gsc[:, 0:1]), inc=vE)
                    p.op("scalar", lambda E_: E_.activation(s1_, d1_, AF.Sign),
                         w=[(vE, v_g1)])
                    p.op("scalar", lambda E_: E_.activation(s2_, d2_, AF.Sign))
                    p.op("scalar", lambda E_: E_.activation(w2c, dm, AF.Sigmoid))
                    s_g = p.op("scalar", lambda E_: E_.activation(
                        wmc, dm, AF.Sigmoid, scale=-1.0), inc=sE)
                    p.op("vector", lambda E_: E_.tensor_scalar(
                        m1c, s1_, 1.0, 0.5, ALU.add, ALU.mult), w=[(sE, s_g)])
                    p.op("vector", lambda E_: E_.tensor_scalar(
                        tm_, s2_, 1.0, 0.5, ALU.add, ALU.mult))
                    p.op("vector", lambda E_: E_.tensor_sub(m2c, tm_, m1c))
                    p.op("vector", lambda E_: E_.tensor_scalar_mul(
                        rmx, m1c, gsc[:, 42:43]))
                    p.op("vector", lambda E_: E_.tensor_scalar_mul(
                        ngs, m2c, gsc[:, 41:42]))
                    w = [(sE, last_tr_s)] if last_tr_s else []
                    v_rtm = p.op("vector", lambda E_: E_.tensor_add(
                        rtm[:, :], rmx, ngs), w=w, inc=vE)
                    t_tr = p.op("tensor", lambda E_: E_.transpose(
                        ps_tr[:, :], rtm[:, :], id_sb[:, :]),
                        w=[(vE, v_rtm)], inc=tE)
                    last_tr_s = p.op("scalar", lambda E_, tt=tt: E_.mul(
                        rfm8[:, tt * 128:(tt + 1) * 128], ps_tr[:, :], 0.5),
                        w=[(tE, t_tr)], inc=sE)
                s_gate = last_tr_s
                v_gate = vE.n
                w1_free_t = w2_free_t = 0
                gelu_s = [0, 0]
                scale_v = [0, 0]
                oadd_v = [0, 0, 0, 0]
                ps_o_first = [True, True, True, True]
                web_v = [0, 0]
                first_b2 = [[True] * DC for _ in range(NTB)]
                v_xn_local = v_xn
                for e in range(E):
                    cw = list(conv_done.get(e, ()))
                    cw = [(dQ[0], cw[0]), (dQ[1], cw[1])] if cw else []
                    for fh in range(2):
                        w = ([(tE, w1_free_t)] if w1_free_t else []) + cw
                        src1 = w1bf.ap()[e, :, :].rearrange(
                            "p (dc f) -> p dc f", dc=DC)[:, :, fh * FH:(fh + 1) * FH]
                        dst1 = w1b[:, :].rearrange("p (dc f) -> p dc f", dc=DC)
                        if dS.n:
                            w = [(dS, dS.n)] + w
                        d_w1 = p.op("sync", lambda E_, s=src1, d=dst1:
                                    E_.dma_start(out=d, in_=s), w=w, inc=dS, inc_n=16)
                        w = ([(tE, w2_free_t)] if w2_free_t else []) + cw
                        src2 = w2bf.ap()[e, :, fh * NFK * D:(fh + 1) * NFK * D]
                        d_w2 = p.op("sync", lambda E_, s=src2, d=w2b:
                                    E_.dma_start(out=d[:, :], in_=s), w=w,
                                    inc=dS, inc_n=16)
                        for tb in range(NTB):
                            if pre_expert is not None:
                                pre_expert(e, fh * NTB + tb, 2 * NTB)
                            i = tb % 2
                            w = [(vE, web_v[i])] if web_v[i] else [(vE, v_gate)]
                            w.append((sE, s_gate))
                            t_w = p.op("tensor", lambda E_, e=e, tb=tb: E_.matmul(
                                ps_w[:, :], eb_sb[:, e * 128:(e + 1) * 128],
                                rfm8[:, tb * QW:(tb + 1) * QW],
                                start=True, stop=True), w=w, inc=tE)
                            web_v[i] = p.op("vector", lambda E_, i=i:
                                            E_.tensor_copy(
                                                web[:, i * QW:(i + 1) * QW],
                                                ps_w[:, :]),
                                            w=[(tE, t_w)], inc=vE)
                            wi = tb % 2
                            for fk in range(NFK):
                                hb = fk % 2
                                w = [(dS, d_w1), (vE, v_xn_local)]
                                if gelu_s[hb]:
                                    w.append((sE, gelu_s[hb]))
                                for dc in range(DC):
                                    p.op("tensor",
                                         lambda E_, fk=fk, dc=dc, hb=hb, tb=tb:
                                         E_.matmul(
                                             ps_h[hb][:, :],
                                             w1b[:, dc * FH + fk * 128:
                                                 dc * FH + (fk + 1) * 128],
                                             xn_sb[:, dc * TL + tb * QW:
                                                   dc * TL + (tb + 1) * QW],
                                             start=(dc == 0), stop=(dc == DC - 1)),
                                         w=(w if dc == 0 else ()),
                                         inc=(tE if dc == DC - 1 else None))
                                t_h = tE.n
                                bcol = e * 2 * NFK + fh * NFK + fk
                                w = [(tE, t_h)]
                                if scale_v[hb]:
                                    w.append((vE, scale_v[hb]))
                                g_s = p.op("scalar", lambda E_, hb=hb, bcol=bcol:
                                           E_.activation(
                                               hraw[:, hb * QW:(hb + 1) * QW],
                                               ps_h[hb][:, :], AF.Gelu,
                                               bias=b1_sb[:, bcol:bcol + 1]),
                                           w=w, inc=sE)
                                gelu_s[hb] = g_s
                                scale_v[hb] = p.op(
                                    "vector", lambda E_, hb=hb, fk=fk, wi=wi:
                                    E_.tensor_mul(
                                        h_sb[:, fk * QW:(fk + 1) * QW],
                                        hraw[:, hb * QW:(hb + 1) * QW],
                                        web[:, wi * QW:(wi + 1) * QW]),
                                    w=[(sE, g_s)], inc=vE)
                            v_h = vE.n
                            if tb == NTB - 1:
                                w1_free_t = tE.n
                            for dcb in range(DC):
                                ob = dcb % 4
                                w = [(vE, v_h), (dS, d_w2)]
                                if oadd_v[ob]:
                                    w.append((vE, oadd_v[ob]))
                                if ps_o_first[ob]:
                                    w.append((sE, s_gate))
                                    ps_o_first[ob] = False
                                use_b2 = first_b2[tb][dcb]
                                first_b2[tb][dcb] = False
                                if use_b2:
                                    p.op("tensor", lambda E_, dcb=dcb, tb=tb, ob=ob:
                                         E_.matmul(
                                             ps_o[ob][:, :],
                                             b2_sb[:, dcb * 128:(dcb + 1) * 128],
                                             rfm8[:, tb * QW:(tb + 1) * QW],
                                             start=True, stop=False), w=w)
                                for fk in range(NFK):
                                    p.op("tensor",
                                         lambda E_, dcb=dcb, fk=fk, ob=ob,
                                         use_b2=use_b2: E_.matmul(
                                             ps_o[ob][:, :],
                                             w2b[:, fk * D + dcb * 128:
                                                 fk * D + (dcb + 1) * 128],
                                             h_sb[:, fk * QW:(fk + 1) * QW],
                                             start=(fk == 0 and not use_b2),
                                             stop=(fk == NFK - 1)),
                                         w=(w if (fk == 0 and not use_b2) else ()),
                                         inc=(tE if fk == NFK - 1 else None))
                                t_o = tE.n
                                xs = x_sb[:, dcb * TL + tb * QW:
                                          dcb * TL + (tb + 1) * QW]
                                oadd_v[ob] = p.op(
                                    "vector", lambda E_, ob=ob, xs=xs:
                                    E_.tensor_add(xs, ps_o[ob][:, :], xs),
                                    w=[(tE, t_o)], inc=vE)
                        w2_free_t = tE.n
                barrier()
                p.flush()

        emit_ffn("A", pre_expert=lambda e, part, nparts: (
            conv_expert(e + 1, part, nparts) if e + 1 < E else None))
        if cfg.DBG:
            for dc in range(DC):
                dma("gpsimd", dG, dbg1[dc * 128:(dc + 1) * 128, :],
                    x_sb[:, dc * TL:(dc + 1) * TL], fence=(dc == 0))
            barrier()

        lsc2 = ExitStack()
        with lsc2:
            xn2 = lsc2.enter_context(nc.sbuf_tensor("xn2", [128, DC * TL], BF16))
            lns2 = mk_ln_scope("L2", lsc2)
            tb_done2 = emit_ln(xn2, lns2)
            TLH = TL // HVS
            for hv in range(HVS):
                v_half = tb_done2[(hv + 1) * NTB // HVS - 1]
                for dc in range(DC):
                    dma("gpsimd", dG,
                        ag_in2[hv].ap()[dc * 128:(dc + 1) * 128, :],
                        xn2[:, dc * TL + hv * TLH: dc * TL + (hv + 1) * TLH],
                        w=([(vE, v_half)] if dc == 0 else ()), fence=(dc == 0))
                d_agin = dG.n
                p.op("gpsimd", lambda E_, hv=hv: E_.collective_compute(
                    "AllGather", ALU.bypass,
                    replica_groups=[[0, 1, 2, 3], [4, 5, 6, 7]],
                    ins=[ag_in2[hv].ap().opt()], outs=[ag_out2[hv].ap().opt()]),
                    w=[(dG, d_agin)], inc=ccA2[hv])
            barrier()
            p.flush()

        asc = ExitStack()
        with asc:
            QT = asc.enter_context(nc.sbuf_tensor("QT", [128, 2 * SEQ], BF16))
            KT = asc.enter_context(nc.sbuf_tensor("KT", [128, 2 * SEQ], BF16))
            VS = asc.enter_context(nc.sbuf_tensor("VS", [128, NKT * 260], BF16))
            qsc = ExitStack()
            with qsc:
                NTBH = NTB // HVS
                TH = TL // HVS
                xg = qsc.enter_context(nc.sbuf_tensor("xg", [128, DC * TH], BF16))
                wqkv_sb = qsc.enter_context(
                    nc.sbuf_tensor("wqkv_sb", [128, DC * 768], BF16))
                ps_q = [qsc.enter_context(
                    nc.psum_tensor(f"ps_q{i}", [128, QW], F32)) for i in range(2)]
                ps_v = [qsc.enter_context(
                    nc.psum_tensor(f"ps_v{i}", [128, 256], F32)) for i in range(2)]
                d_wq = dma("sync", dS, wqkv_sb[:, :], wqkvbf.ap(), fence=True)
                v_vones = p.op("vector", lambda E_: E_.memset(
                    VS[:, :].rearrange("p (t h z) -> p t h z", h=HL, z=65)
                    [:, :, :, 64:65], 1.0), inc=vE)
                qk_s = [0, 0]
                v_v = [0, 0]
                xg_free_t = 0
                for rb in range(4):
                  for hv in range(HVS):
                    w = [(ccA2[hv], 1)]
                    if xg_free_t:
                        w.append((tE, xg_free_t))
                    for dc in range(DC):
                        dma("sync", dS, xg[:, dc * TH:(dc + 1) * TH],
                            ag_out2[hv].ap()[rb * D + dc * 128:
                                             rb * D + (dc + 1) * 128, :],
                            w=(w if dc == 0 else ()), fence=(dc == 0))
                    d_xg = dS.n
                    for pr in range(2):
                        for tbh in range(NTBH):
                            tg = rb * NTB + hv * NTBH + tbh
                            for wh, base in (("q", 0), ("k", 256)):
                                i = (pr * NTBH + tbh) % 2
                                w = [(dS, d_xg), (dS, d_wq)]
                                if qk_s[i]:
                                    w.append((sE, qk_s[i]))
                                for dc in range(DC):
                                    p.op("tensor",
                                         lambda E_, dc=dc, pr=pr, tbh=tbh, i=i, base=base:
                                         E_.matmul(
                                             ps_q[i][:, :],
                                             wqkv_sb[:, dc * 768 + base + pr * 128:
                                                     dc * 768 + base + (pr + 1) * 128],
                                             xg[:, dc * TH + tbh * QW:
                                                dc * TH + (tbh + 1) * QW],
                                             start=(dc == 0), stop=(dc == DC - 1)),
                                         w=(w if dc == 0 else ()),
                                         inc=(tE if dc == DC - 1 else None))
                                t_qk = tE.n
                                dst = QT if wh == "q" else KT
                                qk_s[i] = p.op("scalar",
                                               lambda E_, dst=dst, pr=pr, tg=tg, i=i:
                                               E_.copy(
                                                   dst[:, pr * SEQ + tg * QW:
                                                       pr * SEQ + (tg + 1) * QW],
                                                   ps_q[i][:, :]),
                                               w=[(tE, t_qk)], inc=sE)
                    for tt in range(TH // 128):
                        i = tt % 2
                        ttg = rb * (TL // 128) + hv * (TH // 128) + tt
                        w = [(dS, d_xg), (dS, d_wq), (vE, v_vones)]
                        if v_v[i]:
                            w.append((vE, v_v[i]))
                        for dc in range(DC):
                            p.op("tensor", lambda E_, dc=dc, tt=tt, i=i: E_.matmul(
                                ps_v[i][:, :],
                                xg[:, dc * TH + tt * 128: dc * TH + (tt + 1) * 128],
                                wqkv_sb[:, dc * 768 + 512: dc * 768 + 768],
                                start=(dc == 0), stop=(dc == DC - 1)),
                                w=(w if dc == 0 else ()),
                                inc=(tE if dc == DC - 1 else None))
                        t_v = tE.n
                        for hl in range(HL):
                            vv = p.op("vector", lambda E_, ttg=ttg, hl=hl, i=i:
                                      E_.tensor_copy(
                                          VS[:, ttg * 260 + hl * 65:
                                             ttg * 260 + hl * 65 + 64],
                                          ps_v[i][:, hl * 64:(hl + 1) * 64]),
                                      w=([(tE, t_v)] if hl == 0 else ()),
                                      inc=(vE if hl == HL - 1 else None))
                        v_v[i] = vv
                    xg_free_t = tE.n
                barrier()
                p.flush()
            fsc = ExitStack()
            with fsc:
                FSB = lambda n_, s_, d_: fsc.enter_context(nc.sbuf_tensor(n_, s_, d_))
                FPS = lambda n_, s_, d_: fsc.enter_context(nc.psum_tensor(n_, s_, d_))
                masks_sb = FSB("masks_sb", [128, DIAG * QW], BF16)
                cb_sb = FSB("cb_sb", [128, HL * NQB], F32)
                NB = HL * NQB
                rows_at = FSB("rows_at", [1, 2 * QW + 4 * NB], F32)
                sqn = FSB("sqn", [64, 2 * QW], BF16)
                pe = [FSB(f"pe{i}", [128, 2 * QW], BF16) for i in range(4)]
                pm = [FSB(f"pm{i}", [128, 2 * QW], BF16) for i in range(3)]
                Lb_sb = FSB("Lb_sb", [64, QW], F32)
                ost = [FSB(f"ost{i}", [64, QW], BF16) for i in range(1)]
                nsc = ExitStack()
                ps_n = [nsc.enter_context(nc.psum_tensor(f"ps_n{i}", [1, QW], F32))
                        for i in range(2)]
                ps_cb = nsc.enter_context(nc.psum_tensor("ps_cb", [128, HL * NQB], F32))
                d_msk = dma("sync", dS, masks_sb[:, :], mskin, fence=True)
                nrow = rows_at[0:1, 0:QW]
                nrow2 = rows_at[0:1, QW:2 * QW]
                bq_r = rows_at[0:1, 2 * QW:2 * QW + NB]
                bk_r = rows_at[0:1, 2 * QW + NB:2 * QW + 2 * NB]
                pk_r = rows_at[0:1, 2 * QW + 2 * NB:2 * QW + 3 * NB]
                nc_r = rows_at[0:1, 2 * QW + 3 * NB:2 * QW + 4 * NB]
                sq_done_t = [0, 0]
                red_done_v = [0, 0]
                for src_, dstr in ((QT, bq_r), (KT, bk_r)):
                    for h in range(HL):
                        pr, s2 = h // 2, h % 2
                        for qb in range(NQB):
                            it = h * NQB + qb
                            sl = it % 2
                            ssrc = src_[s2 * 64:s2 * 64 + 64,
                                        pr * SEQ + qb * QW: pr * SEQ + (qb + 1) * QW]
                            w = [(tE, sq_done_t[sl])] if sq_done_t[sl] else []
                            s_sq = p.op("scalar", lambda E_, ssrc=ssrc, sl=sl:
                                        E_.activation(sqn[:, sl * QW:(sl + 1) * QW],
                                                      ssrc, AF.Square), w=w, inc=sE)
                            w = [(sE, s_sq)]
                            if red_done_v[sl]:
                                w.append((vE, red_done_v[sl]))
                            tv = p.op("tensor", lambda E_, sl=sl: E_.matmul(
                                ps_n[sl][:, :], ones_cb[0:64, :],
                                sqn[:, sl * QW:(sl + 1) * QW],
                                start=True, stop=True), w=w, inc=tE)
                            sq_done_t[sl] = tv
                            red_done_v[sl] = p.op(
                                "vector", lambda E_, dstr=dstr, it=it, sl=sl:
                                E_.reduce_max(dstr[0:1, it:it + 1],
                                              ps_n[sl][0:1, :], axis=AX.X),
                                w=[(tE, tv)], inc=vE)
                for h in range(HL):
                    p.op("vector", lambda E_, h=h: E_.tensor_tensor_scan(
                        pk_r[0:1, h * NQB:(h + 1) * NQB],
                        bk_r[0:1, h * NQB:(h + 1) * NQB],
                        bk_r[0:1, h * NQB:(h + 1) * NQB],
                        -1e30, ALU.max, ALU.bypass))
                v_pq = p.op("vector", lambda E_: E_.tensor_mul(nc_r, bq_r, pk_r),
                            inc=vE)
                for pr_ in range(2):
                    a0 = nc_r[0:1, (2 * pr_) * NQB:(2 * pr_ + 1) * NQB]
                    a1 = nc_r[0:1, (2 * pr_ + 1) * NQB:(2 * pr_ + 2) * NQB]
                    p.op("vector", lambda E_, a0=a0, a1=a1: E_.tensor_max(a0, a0, a1))
                    p.op("vector", lambda E_, a0=a0, a1=a1: E_.tensor_copy(a1, a0))
                s_cr = p.op("scalar", lambda E_: E_.activation(
                    nrow2[0:1, 0:NB], nc_r, AF.Sqrt), w=[(vE, vE.n)], inc=sE)
                v_nc = p.op("vector", lambda E_: E_.tensor_scalar_mul(
                    nc_r, nrow2[0:1, 0:NB], -1.0), w=[(sE, s_cr)], inc=vE)
                t_cb = p.op("tensor", lambda E_: E_.matmul(
                    ps_cb[:, :], ones_rf[:, :], nc_r, start=True, stop=True),
                    w=[(vE, v_nc)], inc=tE)
                v_cb = p.op("vector", lambda E_: E_.tensor_copy(
                    cb_sb[:, :], ps_cb[:, :]), w=[(tE, t_cb)], inc=vE)
                p.flush()
                nsc.close()
                NSL = 3
                ps_sp = [FPS(f"ps_sp{i}", [128, 2 * QW], F32) for i in range(NSL)]
                ps_op = [FPS(f"ps_op{i}", [65, QW], F32) for i in range(2)]
                exp_s = [0, 0, 0]
                lb_war = [0, 0, 0]
                pv_done = [[], [], [], []]
                pm_done = [[], [], []]
                o_v = [[0, 0], [0, 0]]
                ost_d = [0, 0]
                NSL = 3
                for pr in range(2):
                    for qb in range(NQB):
                        nkb = DIAG * qb + DIAG
                        pends = []

                        def emit_pv(kb, sl, pslot, mwaits, e_s,
                                    qb_=qb, pr_=pr, nkb_=nkb):
                            for hq in range(2):
                                diag = kb >= DIAG * qb_
                                if diag:
                                    w = [(vE, mwaits[hq])]
                                    rhs = pm[kb % 3][:, hq * QW:(hq + 1) * QW]
                                else:
                                    w = [(sE, e_s)]
                                    rhs = pe[pslot][:, hq * QW:(hq + 1) * QW]
                                if kb == 0 and o_v[hq][0]:
                                    w.append((vE, o_v[hq][0]))
                                hl_ = 2 * pr_ + hq
                                tv = p.op("tensor",
                                          lambda E_, kb=kb, hl_=hl_, hq=hq,
                                          nkb_=nkb_, rhs=rhs: E_.matmul(
                                              ps_op[hq][:, :],
                                              VS[:, kb * 260 + hl_ * 65:
                                                 kb * 260 + hl_ * 65 + 65],
                                              rhs,
                                              start=(kb == 0),
                                              stop=(kb == nkb_ - 1)),
                                          w=w, inc=tE)
                                pv_done[pslot].append((tE, tv))
                                if diag:
                                    pm_done[kb % 3].append((tE, tv))

                        for kb in range(nkb):
                            sl = kb % NSL
                            pslot = kb % 4
                            w = []
                            if exp_s[sl]:
                                w.append((sE, exp_s[sl]))
                            if lb_war[sl]:
                                w.append((vE, lb_war[sl]))
                            for s2 in range(2):
                                p.op("tensor",
                                     lambda E_, s2=s2, pr=pr, kb=kb, qb=qb, sl=sl:
                                     E_.matmul(
                                         ps_sp[sl][:, s2 * QW:(s2 + 1) * QW],
                                         KT[s2 * 64:s2 * 64 + 64,
                                            pr * SEQ + kb * 128:
                                            pr * SEQ + (kb + 1) * 128],
                                         QT[s2 * 64:s2 * 64 + 64,
                                            pr * SEQ + qb * QW:
                                            pr * SEQ + (qb + 1) * QW],
                                         start=True, stop=True),
                                     w=(w if s2 == 0 else ()),
                                     inc=(tE if s2 == 1 else None))
                            t_st = tE.n
                            if len(pends) >= 3:
                                emit_pv(*pends.pop(0))
                                emit_pv(*pends.pop(0))
                            w = [(tE, t_st), (vE, v_cb)] + pv_done[pslot]
                            pv_done[pslot] = []
                            cidx = (2 * pr) * NQB + qb
                            e_s = p.op("scalar",
                                       lambda E_, sl=sl, pslot=pslot, cidx=cidx:
                                       E_.activation(
                                           pe[pslot][:, 0:2 * QW],
                                           ps_sp[sl][:, 0:2 * QW], AF.Exp,
                                           bias=cb_sb[:, cidx:cidx + 1]),
                                       w=w, inc=sE)
                            exp_s[sl] = e_s
                            mwaits = {}
                            if kb >= DIAG * qb:
                                r = kb - DIAG * qb
                                for hq in range(2):
                                    w = [(sE, e_s), (dS, d_msk)] + pm_done[kb % 3]
                                    if hq == 1:
                                        w = [(sE, e_s)]
                                    v_m = p.op("vector",
                                               lambda E_, pslot=pslot, kb=kb,
                                               hq=hq, r=r: E_.tensor_mul(
                                                   pm[kb % 3][:, hq * QW:(hq + 1) * QW],
                                                   pe[pslot][:, hq * QW:(hq + 1) * QW],
                                                   masks_sb[:, r * QW:(r + 1) * QW]),
                                               w=w, inc=vE)
                                    mwaits[hq] = v_m
                                pm_done[kb % 3] = []
                            pends.append((kb, sl, pslot, mwaits, e_s))
                        lastsl = (nkb - 1) % NSL
                        for pd in pends:
                            emit_pv(*pd)
                        t_all = tE.n
                        for hq in range(2):
                            h = 2 * pr + hq
                            bsl = (lastsl + 1 + hq) % NSL
                            p.op("vector", lambda E_, hq=hq: E_.tensor_copy(
                                nrow, ps_op[hq][64:65, :]), w=[(tE, t_all)])
                            v_li = p.op("vector", lambda E_: E_.reciprocal(
                                nrow2, nrow), inc=vE)
                            w = [(vE, v_li)]
                            if exp_s[bsl]:
                                w.append((sE, exp_s[bsl]))
                            t_lb = p.op("tensor", lambda E_, bsl=bsl: E_.matmul(
                                ps_sp[bsl][0:64, 0:QW], ones_rf[0:1, 0:64], nrow2,
                                start=True, stop=True), w=w, inc=tE)
                            v_lbc = p.op("vector", lambda E_, bsl=bsl: E_.tensor_copy(
                                Lb_sb[:, :], ps_sp[bsl][0:64, 0:QW]),
                                w=[(tE, t_lb)], inc=vE)
                            lb_war[bsl] = v_lbc
                            exp_s[bsl] = 0
                            ow = 0
                            w = [(dG, ost_d[ow])] if ost_d[ow] else []
                            v_o = p.op("vector", lambda E_, hq=hq, ow=ow:
                                       E_.tensor_mul(
                                           ost[ow][:, :], ps_op[hq][0:64, :],
                                           Lb_sb[:, :]), w=w, inc=vE)
                            o_v[hq][0] = v_o
                            ost_d[ow] = dma(
                                "gpsimd", dG,
                                o_dram.ap()[h * 64:(h + 1) * 64,
                                            qb * QW:(qb + 1) * QW],
                                ost[ow][:, :], w=[(vE, v_o)], fence=True)
                    for hq in range(2):
                        h = 2 * pr + hq
                        p.op("gpsimd", lambda E_, h=h: E_.collective_compute(
                            "AllGather", ALU.bypass,
                            replica_groups=[[0, 1, 2, 3], [4, 5, 6, 7]],
                            ins=[o_dram.ap()[h * 64:(h + 1) * 64, :].opt()],
                            outs=[oag_h[h].ap().opt()]),
                            w=[(dG, dG.n)], inc=ccB4[h])
                if cfg.DBG:
                    dma("gpsimd", dG, dbg3, o_dram.ap(), fence=True)
                barrier()
                p.flush()

        psc = ExitStack()
        with psc:
            wpj_sb = psc.enter_context(nc.sbuf_tensor("wpj_sb", [128, DC * D], BF16))
            og = psc.enter_context(nc.sbuf_tensor("og", [128, DC * TL], BF16))
            bpr_f = psc.enter_context(nc.sbuf_tensor("bpr_f", [1, D], F32))
            bpr_b = psc.enter_context(nc.sbuf_tensor("bpr_b", [1, D], BF16))
            onesq = psc.enter_context(nc.sbuf_tensor("onesq", [1, QW], BF16))
            ps_p = [psc.enter_context(nc.psum_tensor(f"ps_p{i}", [128, QW], F32))
                    for i in range(2)]
            dma("sync", dS, wpj_sb[:, :], wpjbf.ap(), fence=True)
            d_pj = dma("sync", dS, bpr_f[:, :], bprin)
            p.op("vector", lambda E_: E_.memset(onesq[:, :], 1.0))
            v_bpr = p.op("vector", lambda E_: E_.tensor_copy(bpr_b[:, :], bpr_f[:, :]),
                         w=[(dS, d_pj)], inc=vE)

            def emit_og(E_):
                last = None
                with E_.register("gr0") as gr0, E_.register("gr1") as gr1:
                    E_.reg_load(gr0, rk_sb[0:1, 0:1])
                    E_.reg_mul(gr0, gr0, TL)
                    for kc in range(DC):
                        r_, hh = kc // 2, kc % 2
                        for hq in range(2):
                            hl_ = hh * 2 + hq
                            E_.reg_mov(gr1, r_ * 64 * SEQ)
                            E_.reg_add(gr1, gr1, gr0)
                            last = E_.dma_start(
                                out=og[hq * 64:(hq + 1) * 64,
                                       kc * TL:(kc + 1) * TL],
                                in_=bass.AP(oag_h[hl_], gr1, [[SEQ, 64], [1, TL]]))
                            last.then_inc(dG.h, 16)
                return last
            fence_w = [(dG, dG.n)] if dG.n else []
            dG.n += 16 * DC * 2
            p.op("gpsimd", emit_og,
                 w=fence_w + [(s, 1) for s in ccB4] + [(dS, d_pj)])
            d_og = dG.n
            padd_v = [0, 0]
            for tb in range(NTB):
                for dcb in range(DC):
                    i = (tb * DC + dcb) % 2
                    w = [(dG, d_og), (dS, d_pj), (vE, v_bpr)]
                    if padd_v[i]:
                        w.append((vE, padd_v[i]))
                    p.op("tensor", lambda E_, dcb=dcb, i=i: E_.matmul(
                        ps_p[i][:, :], bpr_b[0:1, dcb * 128:(dcb + 1) * 128],
                        onesq[:, :], start=True, stop=False), w=w)
                    for kc in range(DC):
                        p.op("tensor", lambda E_, kc=kc, dcb=dcb, tb=tb, i=i:
                             E_.matmul(
                                 ps_p[i][:, :],
                                 wpj_sb[:, kc * D + dcb * 128: kc * D + (dcb + 1) * 128],
                                 og[:, kc * TL + tb * QW: kc * TL + (tb + 1) * QW],
                                 start=False, stop=(kc == DC - 1)),
                             inc=(tE if kc == DC - 1 else None))
                    t_p = tE.n
                    xs = x_sb[:, dcb * TL + tb * QW: dcb * TL + (tb + 1) * QW]
                    padd_v[i] = p.op("vector", lambda E_, i=i, xs=xs: E_.tensor_add(
                        xs, ps_p[i][:, :], xs), w=[(tE, t_p)], inc=vE)
            barrier()
            p.flush()
        if cfg.DBG:
            for dc in range(DC):
                dma("gpsimd", dG, dbg2[dc * 128:(dc + 1) * 128, :],
                    x_sb[:, dc * TL:(dc + 1) * TL], fence=(dc == 0))
            barrier()

        emit_ffn("B")

        for dc in range(DC):
            dma("gpsimd", dG, yout[dc * 128:(dc + 1) * 128, :],
                x_sb[:, dc * TL:(dc + 1) * TL], fence=(dc == 0))
        p.q["gpsimd"].append((((dG, dG.n),), None, None))
        p.flush()
    return nc


def make_inputs(cfg, inputs):
    TL, SEQ, QW, DIAG = cfg.TL, cfg.SEQ, cfg.QW, cfg.DIAG
    x = np.asarray(inputs["L0_attn_x"], np.float32)
    w_qkv = np.asarray(inputs["L0_attn_w_qkv"], np.float32)
    w_proj = np.asarray(inputs["L0_attn_w_proj"], np.float32)
    b_proj = np.asarray(inputs["L0_attn_b_proj"], np.float32)
    gw = np.asarray(inputs["L0_ffn_gate_w"], np.float32)
    gb = np.asarray(inputs["L0_ffn_gate_b"], np.float32)
    w1 = np.ascontiguousarray(np.asarray(inputs["L0_ffn_w1"], np.float32))
    b1 = np.asarray(inputs["L0_ffn_b1"], np.float32)
    w2 = np.ascontiguousarray(np.asarray(inputs["L0_ffn_w2"], np.float32))
    b2 = np.ascontiguousarray(np.asarray(inputs["L0_ffn_b2"], np.float32))

    b1l = np.ascontiguousarray(
        b1.reshape(8, 32, 128).transpose(2, 0, 1).reshape(128, 256))
    gwl = np.ascontiguousarray(
        gw.reshape(8, 128, 8).transpose(1, 0, 2).reshape(128, 64))
    gbl = np.ascontiguousarray(gb.reshape(1, 8))
    wpjt = np.ascontiguousarray(
        w_proj.T.reshape(8, 128, 1024).transpose(1, 0, 2).reshape(128, 8192))
    bpr = np.ascontiguousarray(b_proj.reshape(1, 1024))
    idb = np.eye(128, dtype=np.float32).astype(BFNP)
    masks = np.zeros((128, DIAG * QW), np.float32)
    ii = np.arange(128)[:, None]
    jj = np.arange(QW)[None, :]
    for ri in range(DIAG):
        masks[:, ri * QW:(ri + 1) * QW] = (jj >= ii + ri * 128)
    masks = masks.astype(BFNP)
    ebasis = np.zeros((8, 1024), np.float32)
    for e in range(8):
        ebasis[e, e * 128:(e + 1) * 128] = 1.0
    ebasis = ebasis.astype(BFNP)

    maps = []
    for c in range(8):
        g, r = c // 4, c % 4
        xl = np.ascontiguousarray(x[g, r * TL:(r + 1) * TL, :].T)
        hs = slice(4 * r, 4 * r + 4)
        wq = np.ascontiguousarray(
            w_qkv[hs, 0:64, :].transpose(2, 0, 1).reshape(1024, 256))
        wk = np.ascontiguousarray(
            w_qkv[hs, 64:128, :].transpose(2, 0, 1).reshape(1024, 256))
        wv = np.ascontiguousarray(
            w_qkv[hs, 128:192, :].transpose(2, 0, 1).reshape(1024, 256))
        wqkvl = np.concatenate([wq, wk, wv], axis=1)
        wqkvl = np.ascontiguousarray(
            wqkvl.reshape(8, 128, 768).transpose(1, 0, 2).reshape(128, 8 * 768))
        maps.append({
            "x": xl, "w1": w1, "w2": w2, "b1l": b1l, "b2": b2, "gwl": gwl,
            "gb": gbl, "wqkvl": wqkvl, "wpjt": wpjt, "bpr": bpr,
            "idb": idb, "masks": masks, "ebasis": ebasis,
            "rank": np.array([[r]], dtype=np.uint32),
        })
    return maps


def kernel(**inputs):
    cfg = CFG()
    maps = make_inputs(cfg, inputs)
    nc = build_program(cfg)
    res = run_bass_kernel_spmd(nc, maps, core_ids=list(range(8)))
    out = np.empty((2, 8192, 1024), np.float32)
    for c in range(8):
        g, r = c // 4, c % 4
        out[g, r * cfg.TL:(r + 1) * cfg.TL, :] = res.results[c]["out"].T
    return out


# version 15
# speedup vs baseline: 1.0161x; 1.0161x over previous
import numpy as np
import ml_dtypes
import concourse.bass as bass
import concourse.mybir as mybir
from concourse.bass_utils import run_bass_kernel_spmd
from contextlib import ExitStack

BFNP = ml_dtypes.bfloat16
F32 = mybir.dt.float32
BF16 = mybir.dt.bfloat16
U32 = mybir.dt.uint32
AF = mybir.ActivationFunctionType
ALU = mybir.AluOpType
AX = mybir.AxisListType

ENGS = ("sync", "tensor", "vector", "scalar", "gpsimd")


class Sem:
    __slots__ = ("name", "n", "h")

    def __init__(self, name):
        self.name, self.n, self.h = name, 0, None


class P:

    def __init__(self, nc):
        self.nc = nc
        self.q = {e: [] for e in ENGS}
        self.last_wait = {e: {} for e in ENGS}
        self.serialize = {}

    def op(self, eng, fn=None, w=(), inc=None, inc_n=1):
        own = self.serialize.get(eng)
        if own is not None and fn is not None:
            w = list(w) + [(own, own.n)]
            if inc is None:
                inc = own
                inc_n = 1
        if inc is not None:
            inc.n += inc_n
        self.q[eng].append((tuple(w), fn, (inc, inc_n) if inc is not None else None))
        return inc.n if inc is not None else None

    def flush(self):
        if not any(self.q[e] for e in ENGS):
            return
        with self.nc.Block() as block:
            for eng in ENGS:
                ops = self.q[eng]
                if not ops:
                    continue
                lastw = self.last_wait[eng]

                def body(E_, ops=ops, lastw=lastw):
                    for waits, fn, inc in ops:
                        for (s, v) in waits:
                            if lastw.get(id(s), 0) < v:
                                E_.wait_ge(s.h, v)
                                lastw[id(s)] = v
                        ins = fn(E_) if fn is not None else None
                        if inc is not None:
                            ins.then_inc(inc[0].h, inc[1])
                getattr(block, eng)(body)
        self.q = {e: [] for e in ENGS}


class CFG:
    def __init__(self, TL=2048, SEQ=8192, QW=512):
        self.TL, self.SEQ, self.QW = TL, SEQ, QW
        self.NTB = TL // QW
        self.NTT = TL // 128
        self.NQB = SEQ // QW
        self.DIAG = QW // 128
        self.D, self.F, self.E, self.DC, self.HL = 1024, 4096, 8, 8, 4
        self.DBG = False
        self.FH = self.F // 2
        self.NFK = self.FH // 128


def build_program(cfg: CFG):
    nc = bass.Bass("TRN2", target_bir_lowering=False)
    TL, SEQ, QW, D, E, DC = cfg.TL, cfg.SEQ, cfg.QW, cfg.D, cfg.E, cfg.DC
    NTB, NTT, NQB, DIAG = cfg.NTB, cfg.NTT, cfg.NQB, cfg.DIAG
    NFK, FH, HL, F = cfg.NFK, cfg.FH, cfg.HL, cfg.F
    NKT = SEQ // 128
    HVS = 2 if cfg.NTB >= 2 else 1

    xin = nc.dram_tensor("x", [D, TL], F32, kind="ExternalInput").ap()
    w1in = nc.dram_tensor("w1", [E, D, F], F32, kind="ExternalInput").ap()
    w2in = nc.dram_tensor("w2", [E, F, D], F32, kind="ExternalInput").ap()
    b1in = nc.dram_tensor("b1l", [128, 2 * E * NFK], F32, kind="ExternalInput").ap()
    b2in = nc.dram_tensor("b2", [E, D], F32, kind="ExternalInput").ap()
    gwin = nc.dram_tensor("gwl", [128, DC * E], F32, kind="ExternalInput").ap()
    gbin = nc.dram_tensor("gb", [1, E], F32, kind="ExternalInput").ap()
    wqkvin = nc.dram_tensor("wqkvl", [128, DC * 768], F32, kind="ExternalInput").ap()
    wpjin = nc.dram_tensor("wpjt", [128, DC * D], F32, kind="ExternalInput").ap()
    bprin = nc.dram_tensor("bpr", [1, D], F32, kind="ExternalInput").ap()
    idbin = nc.dram_tensor("idb", [128, 128], BF16, kind="ExternalInput").ap()
    mskin = nc.dram_tensor("masks", [128, DIAG * QW], BF16, kind="ExternalInput").ap()
    ebin = nc.dram_tensor("ebasis", [E, E * 128], BF16, kind="ExternalInput").ap()
    rkin = nc.dram_tensor("rank", [1, 1], U32, kind="ExternalInput").ap()
    yout = nc.dram_tensor("out", [D, TL], F32, kind="ExternalOutput").ap()
    if cfg.DBG:
        dbg1 = nc.dram_tensor("dbg1", [D, TL], F32, kind="ExternalOutput").ap()
        dbg2 = nc.dram_tensor("dbg2", [D, TL], F32, kind="ExternalOutput").ap()
        dbg3 = nc.dram_tensor("dbg3", [HL * 64, SEQ], BF16, kind="ExternalOutput").ap()
        dbg4 = nc.dram_tensor("dbg4", [E, TL], BF16, kind="ExternalOutput").ap()
        dbg5 = nc.dram_tensor("dbg5", [128, 128], F32, kind="ExternalOutput").ap()
        dbg6 = nc.dram_tensor("dbg6", [128, TL], BF16, kind="ExternalOutput").ap()

    w1bf = nc.dram_tensor("w1bf", [E, 128, DC * F], BF16)
    w2bf = nc.dram_tensor("w2bf", [E, 128, (F // 128) * D], BF16)
    wqkvbf = nc.dram_tensor("wqkvbf", [128, DC * 768], BF16)
    wpjbf = nc.dram_tensor("wpjbf", [128, DC * D], BF16)
    ag_in2 = [nc.dram_tensor(f"ag_in{i}", [D, TL // HVS], BF16) for i in range(HVS)]
    ag_out2 = [nc.dram_tensor(f"ag_out{i}", [4 * D, TL // HVS], BF16)
               for i in range(HVS)]
    o_dram = nc.dram_tensor("o_dram", [HL * 64, SEQ], BF16)
    oag_h = [nc.dram_tensor(f"oag_h{i}", [4 * 64, SEQ], BF16) for i in range(HL)]

    p = P(nc)
    dS, dG = Sem("dS"), Sem("dG")
    tE, vE, sE = Sem("tE"), Sem("vE"), Sem("sE")
    p.serialize = {"vector": vE, "scalar": sE}
    ccA, ccB = Sem("ccA"), Sem("ccB")
    ccA2 = [Sem("ccA0"), Sem("ccA1")]
    ccB4 = [Sem(f"ccB{i}") for i in range(4)]
    dP = [Sem("dP0"), Sem("dP1")]
    dQ = [Sem("dQ0"), Sem("dQ1")]
    gE = Sem("gE")
    sems = [dS, dG, tE, vE, sE, ccA, ccB, gE] + dP + dQ + ccA2 + ccB4
    own = {"sync": {dS},
           "gpsimd": {dG, ccA, ccB, dP[0], dP[1], dQ[0], dQ[1], gE, *ccA2, *ccB4},
           "tensor": {tE}, "vector": {vE}, "scalar": {sE}}

    def barrier():
        snap = [(s, s.n) for s in sems if s.n > 0]
        for eng in ENGS:
            w = tuple((s, v) for (s, v) in snap if s not in own[eng])
            if w:
                p.q[eng].append((w, None, None))

    def dma(eng, sem, out, in_, w=(), fence=False):
        if fence and sem.n:
            w = [(sem, sem.n)] + list(w)
        return p.op(eng, lambda E_, o=out, i=in_: E_.dma_start(out=o, in_=i),
                    w=w, inc=sem, inc_n=16)

    st = ExitStack()
    with st:
        for s in sems:
            s.h = st.enter_context(nc.semaphore(s.name))
        SB = lambda name, shape, dt: st.enter_context(nc.sbuf_tensor(name, shape, dt))
        x_sb = SB("x_sb", [128, DC * TL], F32)
        ones_cf = SB("ones_cf", [128, 1], F32)
        ones_cb = SB("ones_cb", [128, 1], BF16)
        ones_rf = SB("ones_rf", [1, 128], F32)
        ones_rb = SB("ones_rb", [1, 128], BF16)
        id_sb = SB("id_sb", [128, 128], BF16)
        eb_sb = SB("eb_sb", [E, E * 128], BF16)
        b1_sb = SB("b1_sb", [128, 2 * E * NFK], F32)
        b2_sb = SB("b2_sb", [E, D], BF16)
        gw_sb = SB("gw_sb", [128, DC * E], BF16)
        gb_sb = SB("gb_sb", [1, E], BF16)
        rk_sb = SB("rk_sb", [1, 2], U32)

        p.op("vector", lambda E_: E_.memset(ones_cf[:, :], 1.0))
        p.op("vector", lambda E_: E_.memset(ones_cb[:, :], 1.0))
        p.op("vector", lambda E_: E_.memset(ones_rf[:, :], 1.0))
        p.op("vector", lambda E_: E_.memset(ones_rb[:, :], 1.0), inc=vE)

        for dc in range(DC):
            dma("sync", dS, x_sb[:, dc * TL:(dc + 1) * TL],
                xin[dc * 128:(dc + 1) * 128, :])
        dma("sync", dS, id_sb[:, :], idbin)
        dma("sync", dS, eb_sb[:, :], ebin)
        dma("sync", dS, b1_sb[:, :], b1in)
        dma("sync", dS, rk_sb[:, 0:1], rkin)

        stageA = SB("stageA", [128, 1024], F32)
        stageB = SB("stageB", [128, 1024], F32)
        chkA = SB("chkA", [128, 1024], BF16)
        chkB = SB("chkB", [128, 1024], BF16)
        stg, chks = [stageA, stageB], [chkA, chkB]
        cstate = {"i": 0, "conv_v": [0, 0]}

        def conv(dst_dram, src_dram, ncols, scales=None):
            i = cstate["i"] % 2
            cstate["i"] += 1
            w = [(gE, cstate["conv_v"][i])] if cstate["conv_v"][i] else []
            dv = p.op("gpsimd", lambda E_, o=stg[i][:, 0:ncols], s=src_dram:
                      E_.dma_start(out=o, in_=s), w=w, inc=dP[i], inc_n=16)
            w2_ = [(dP[i], dv), (dQ[i], dQ[i].n)]
            parts = scales or [(0, ncols, None)]
            for k, (lo, hi, fac) in enumerate(parts):
                lastp = (k == len(parts) - 1)
                if fac is None:
                    p.op("gpsimd", lambda E_, i=i, lo=lo, hi=hi: E_.tensor_copy(
                        chks[i][:, lo:hi], stg[i][:, lo:hi]),
                        w=(w2_ if k == 0 else ()), inc=(gE if lastp else None))
                else:
                    p.op("gpsimd", lambda E_, i=i, lo=lo, hi=hi, fac=fac:
                         E_.tensor_scalar_mul(chks[i][:, lo:hi], stg[i][:, lo:hi], fac),
                         w=(w2_ if k == 0 else ()), inc=(gE if lastp else None))
            cstate["conv_v"][i] = gE.n
            p.op("gpsimd", lambda E_, o=dst_dram, s=chks[i][:, 0:ncols]:
                 E_.dma_start(out=o, in_=s), w=[(gE, gE.n)], inc=dQ[i], inc_n=16)

        def conv_expert(e, part=0, nparts=1):
            jobs = []
            for dc in range(DC):
                for hf in range(4):
                    jobs.append((w1bf.ap()[e, :, dc * F + hf * 1024:
                                           dc * F + (hf + 1) * 1024],
                                 w1in[e, dc * 128:(dc + 1) * 128,
                                      hf * 1024:(hf + 1) * 1024]))
            for fk in range(F // 128):
                jobs.append((w2bf.ap()[e, :, fk * D:(fk + 1) * D],
                             w2in[e, fk * 128:(fk + 1) * 128, :]))
            n = len(jobs)
            lo = n * part // nparts
            hi = n * (part + 1) // nparts
            for dst, s_ in jobs[lo:hi]:
                conv(dst, s_, 1024)
            if hi == n:
                conv_done[e] = (dQ[0].n, dQ[1].n)

        conv_done = {}
        d1 = dma("sync", dS, stageA[0:1, 0:E], gbin, fence=True)
        g1 = p.op("gpsimd", lambda E_: E_.tensor_copy(gb_sb[:, :], stageA[0:1, 0:E]),
                  w=[(dS, d1)], inc=gE)
        cstate["conv_v"][0] = g1
        d2 = dma("sync", dS, stageB[:, 0:DC * E], gwin, fence=True)
        g2 = p.op("gpsimd", lambda E_: E_.tensor_copy(gw_sb[:, :], stageB[:, 0:DC * E]),
                  w=[(dS, d2)], inc=gE)
        cstate["conv_v"][1] = g2
        d3 = dma("sync", dS, stageA[0:E, 0:D], b2in,
                 w=[(gE, cstate["conv_v"][0])], fence=True)
        g3 = p.op("gpsimd", lambda E_: E_.tensor_copy(b2_sb[:, :], stageA[0:E, 0:D]),
                  w=[(dS, d3)], inc=gE)
        cstate["conv_v"][0] = g3
        conv_expert(0)
        for dc in range(DC):
            conv(wqkvbf.ap()[:, dc * 768:(dc + 1) * 768],
                 wqkvin[:, dc * 768:(dc + 1) * 768], 768,
                 scales=[(0, 256, 0.125), (256, 768, None)])
        for j in range(DC * D // 1024):
            conv(wpjbf.ap()[:, j * 1024:(j + 1) * 1024],
                 wpjin[:, j * 1024:(j + 1) * 1024], 1024)
        barrier()
        p.flush()

        def emit_ln(xn_sb, s_):
            (ps_sx, ps_sq, ps_bc, rowm, rowa, rowb, rowc, Rb, C2b, sq_sb) = s_
            last_sq_t = [0, 0]
            last_norm_v = 0
            tb_done = []
            for tb in range(NTB):
                for dc in range(DC):
                    i = dc % 2
                    xs = x_sb[:, dc * TL + tb * QW: dc * TL + (tb + 1) * QW]
                    w = [(tE, last_sq_t[i])] if last_sq_t[i] else []
                    sv = p.op("scalar", lambda E_, i=i, xs=xs: E_.activation(
                        sq_sb[:, i * QW:(i + 1) * QW], xs, AF.Square), w=w, inc=sE)
                    p.op("tensor", lambda E_, xs=xs, dc=dc: E_.matmul(
                        ps_sx[:, :], ones_cf[:, :], xs,
                        start=(dc == 0), stop=(dc == DC - 1)))
                    tv = p.op("tensor", lambda E_, i=i, dc=dc: E_.matmul(
                        ps_sq[:, :], ones_cb[:, :], sq_sb[:, i * QW:(i + 1) * QW],
                        start=(dc == 0), stop=(dc == DC - 1)),
                        w=[(sE, sv)], inc=tE)
                    last_sq_t[i] = tv
                t_stats = tE.n
                p.op("scalar", lambda E_: E_.mul(rowm[:, :], ps_sx[0:1, :], 1.0 / D),
                     w=[(tE, t_stats)])
                s_rows = p.op("scalar", lambda E_: E_.mul(rowa[:, :], ps_sq[0:1, :],
                                                          1.0 / D), inc=sE)
                p.op("vector", lambda E_: E_.tensor_mul(rowb[:, :], rowm[:, :],
                                                        rowm[:, :]), w=[(sE, s_rows)])
                p.op("vector", lambda E_: E_.tensor_sub(rowc[:, :], rowa[:, :],
                                                        rowb[:, :]))
                v_var = p.op("vector", lambda E_: E_.tensor_scalar_add(
                    rowb[:, :], rowc[:, :], 1e-5), inc=vE)
                s_std = p.op("scalar", lambda E_: E_.activation(
                    rowa[:, :], rowb[:, :], AF.Sqrt),
                    w=[(vE, v_var)], inc=sE)
                p.op("vector", lambda E_: E_.reciprocal(rowb[:, :], rowa[:, :]),
                     w=[(sE, s_std)])
                v_c2 = p.op("vector", lambda E_: E_.scalar_tensor_tensor(
                    rowa[:, :], rowm[:, :], -1.0, rowb[:, :], ALU.mult, ALU.mult),
                    inc=vE)
                w = [(vE, v_c2)]
                if last_norm_v:
                    w.append((vE, last_norm_v))
                t_b1 = p.op("tensor", lambda E_: E_.matmul(
                    ps_bc[:, :], ones_rf[:, :], rowb[:, :], start=True, stop=True),
                    w=w, inc=tE)
                v_rb = p.op("vector", lambda E_: E_.tensor_copy(Rb[:, :], ps_bc[:, :]),
                            w=[(tE, t_b1)], inc=vE)
                t_b2 = p.op("tensor", lambda E_: E_.matmul(
                    ps_bc[:, :], ones_rf[:, :], rowa[:, :], start=True, stop=True),
                    w=[(vE, v_rb)], inc=tE)
                v_cb2 = p.op("vector", lambda E_: E_.tensor_copy(C2b[:, :], ps_bc[:, :]),
                             w=[(tE, t_b2)], inc=vE)
                for dc in range(DC):
                    xs = x_sb[:, dc * TL + tb * QW: dc * TL + (tb + 1) * QW]
                    xns = xn_sb[:, dc * TL + tb * QW: dc * TL + (tb + 1) * QW]
                    p.op("vector", lambda E_, xs=xs: E_.tensor_mul(
                        ps_bc[:, :], xs, Rb[:, :]),
                        w=([(vE, v_cb2)] if dc == 0 else ()))
                    last_norm_v = p.op("vector", lambda E_, xns=xns: E_.tensor_add(
                        xns, ps_bc[:, :], C2b[:, :]), inc=vE)
                tb_done.append(vE.n)
            return tb_done

        def mk_ln_scope(tag, stx):
            return (stx.enter_context(nc.psum_tensor("ps_sx" + tag, [1, QW], F32)),
                    stx.enter_context(nc.psum_tensor("ps_sq" + tag, [1, QW], F32)),
                    stx.enter_context(nc.psum_tensor("ps_bc" + tag, [128, QW], F32)),
                    stx.enter_context(nc.sbuf_tensor("rowm" + tag, [1, QW], F32)),
                    stx.enter_context(nc.sbuf_tensor("rowa" + tag, [1, QW], F32)),
                    stx.enter_context(nc.sbuf_tensor("rowb" + tag, [1, QW], F32)),
                    stx.enter_context(nc.sbuf_tensor("rowc" + tag, [1, QW], F32)),
                    stx.enter_context(nc.sbuf_tensor("Rb" + tag, [128, QW], F32)),
                    stx.enter_context(nc.sbuf_tensor("C2b" + tag, [128, QW], F32)),
                    stx.enter_context(nc.sbuf_tensor("sq" + tag, [128, 2 * QW], BF16)))

        def emit_ffn(tag, pre_expert=None):
            sc = ExitStack()
            with sc:
                xn_sb = sc.enter_context(
                    nc.sbuf_tensor("xn" + tag, [128, DC * TL], BF16))
                rfm8 = sc.enter_context(nc.sbuf_tensor("rfm8" + tag, [E, TL], BF16))
                lsc = ExitStack()
                with lsc:
                    lns = mk_ln_scope(tag, lsc)
                    ps_bc = lns[2]
                    ps_g = lsc.enter_context(nc.psum_tensor("ps_g" + tag, [128, E], F32))
                    ps_tr = lsc.enter_context(nc.psum_tensor("ps_tr" + tag, [E, 128], BF16))
                    lg = lsc.enter_context(nc.sbuf_tensor("lg" + tag, [128, E], F32))
                    gsc = lsc.enter_context(nc.sbuf_tensor("gsc" + tag, [128, 96], F32))
                    rtm = lsc.enter_context(nc.sbuf_tensor("rtm" + tag, [128, E], BF16))
                    v_xn = emit_ln(xn_sb, lns)[-1]
                    last_g_v = 0
                    last_tr_s = 0
                    for tt in range(NTT):
                        w = [(vE, v_xn)]
                        if last_g_v:
                            w.append((vE, last_g_v))
                        p.op("tensor", lambda E_: E_.matmul(
                            ps_g[:, :], ones_rb[:, :], gb_sb[:, :],
                            start=True, stop=False), w=w)
                        for dc in range(DC):
                            p.op("tensor", lambda E_, dc=dc, tt=tt: E_.matmul(
                                ps_g[:, :],
                                xn_sb[:, dc * TL + tt * 128: dc * TL + (tt + 1) * 128],
                                gw_sb[:, dc * E:(dc + 1) * E],
                                start=False, stop=(dc == DC - 1)),
                                inc=(tE if dc == DC - 1 else None))
                        t_g = tE.n
                        v_lg = p.op("vector", lambda E_: E_.tensor_copy(
                            lg[:, :], ps_g[:, :]), w=[(tE, t_g)], inc=vE)
                        last_g_v = v_lg
                        mx = gsc[:, 0:8]
                        d1_, d2_ = gsc[:, 8:16], gsc[:, 16:24]
                        s1_, s2_ = gsc[:, 24:32], gsc[:, 32:40]
                        dm = gsc[:, 40:41]
                        w2c, wmc = gsc[:, 41:42], gsc[:, 42:43]
                        m1c, tm_, ngs = gsc[:, 43:51], gsc[:, 51:59], gsc[:, 59:67]
                        m2c, rmx = gsc[:, 67:75], gsc[:, 75:83]
                        p.op("vector", lambda E_: E_.max(mx, lg[:, :]))
                        p.op("vector", lambda E_: E_.tensor_scalar(
                            d1_, lg[:, :], gsc[:, 0:1], 1e-6, ALU.subtract, ALU.add))
                        p.op("vector", lambda E_: E_.tensor_scalar(
                            d2_, lg[:, :], gsc[:, 1:2], 1e-6, ALU.subtract, ALU.add))
                        v_g1 = p.op("vector", lambda E_: E_.tensor_sub(
                            dm, gsc[:, 1:2], gsc[:, 0:1]), inc=vE)
                        p.op("scalar", lambda E_: E_.activation(s1_, d1_, AF.Sign),
                             w=[(vE, v_g1)])
                        p.op("scalar", lambda E_: E_.activation(s2_, d2_, AF.Sign))
                        p.op("scalar", lambda E_: E_.activation(w2c, dm, AF.Sigmoid))
                        s_g = p.op("scalar", lambda E_: E_.activation(
                            wmc, dm, AF.Sigmoid, scale=-1.0), inc=sE)
                        p.op("vector", lambda E_: E_.tensor_scalar(
                            m1c, s1_, 1.0, 0.5, ALU.add, ALU.mult), w=[(sE, s_g)])
                        p.op("vector", lambda E_: E_.tensor_scalar(
                            tm_, s2_, 1.0, 0.5, ALU.add, ALU.mult))
                        p.op("vector", lambda E_: E_.tensor_sub(m2c, tm_, m1c))
                        p.op("vector", lambda E_: E_.tensor_scalar_mul(
                            rmx, m1c, gsc[:, 42:43]))
                        p.op("vector", lambda E_: E_.tensor_scalar_mul(
                            ngs, m2c, gsc[:, 41:42]))
                        w = [(sE, last_tr_s)] if last_tr_s else []
                        v_rtm = p.op("vector", lambda E_: E_.tensor_add(
                            rtm[:, :], rmx, ngs), w=w, inc=vE)
                        t_tr = p.op("tensor", lambda E_: E_.transpose(
                            ps_tr[:, :], rtm[:, :], id_sb[:, :]),
                            w=[(vE, v_rtm)], inc=tE)
                        last_tr_s = p.op("scalar", lambda E_, tt=tt: E_.mul(
                            rfm8[:, tt * 128:(tt + 1) * 128], ps_tr[:, :], 0.5),
                            w=[(tE, t_tr)], inc=sE)
                    if cfg.DBG and tag == "A":
                        dma("gpsimd", dG, dbg4, rfm8[:, :],
                            w=[(sE, last_tr_s)], fence=True)
                        dma("gpsimd", dG, dbg5[:, 0:8], lg[:, :], w=[(vE, vE.n)])
                        dma("gpsimd", dG, dbg5[:, 8:104], gsc[:, :], w=[(sE, sE.n)])
                        dma("gpsimd", dG, dbg6, xn_sb[:, 0:TL], w=[(vE, vE.n)])
                    barrier()
                    p.flush()
                esc = ExitStack()
                with esc:
                    w1b = esc.enter_context(
                        nc.sbuf_tensor("w1b" + tag, [128, DC * FH], BF16))
                    w2b = esc.enter_context(
                        nc.sbuf_tensor("w2b" + tag, [128, NFK * D], BF16))
                    h_sb = esc.enter_context(
                        nc.sbuf_tensor("h" + tag, [128, NFK * QW], BF16))
                    hraw = esc.enter_context(
                        nc.sbuf_tensor("hraw" + tag, [128, 2 * QW], BF16))
                    web = esc.enter_context(
                        nc.sbuf_tensor("web" + tag, [128, 2 * QW], BF16))
                    ps_h = [esc.enter_context(
                        nc.psum_tensor(f"ps_h{i}" + tag, [128, QW], F32))
                        for i in range(2)]
                    ps_o = [esc.enter_context(
                        nc.psum_tensor(f"ps_o{i}" + tag, [128, QW], F32))
                        for i in range(4)]
                    ps_w = esc.enter_context(
                        nc.psum_tensor("ps_w" + tag, [128, QW], F32))
                    w1_free_t = w2_free_t = 0
                    gelu_s = [0, 0]
                    scale_v = [0, 0]
                    oadd_v = [0, 0, 0, 0]
                    web_v = [0, 0]
                    first_b2 = [[True] * DC for _ in range(NTB)]
                    v_xn_local = vE.n
                    for e in range(E):
                        cw = list(conv_done.get(e, ()))
                        cw = [(dQ[0], cw[0]), (dQ[1], cw[1])] if cw else []
                        for fh in range(2):
                            w = ([(tE, w1_free_t)] if w1_free_t else []) + cw
                            src1 = w1bf.ap()[e, :, :].rearrange(
                                "p (dc f) -> p dc f", dc=DC)[:, :, fh * FH:(fh + 1) * FH]
                            dst1 = w1b[:, :].rearrange("p (dc f) -> p dc f", dc=DC)
                            if dS.n:
                                w = [(dS, dS.n)] + w
                            d_w1 = p.op("sync", lambda E_, s=src1, d=dst1:
                                        E_.dma_start(out=d, in_=s), w=w, inc=dS, inc_n=16)
                            w = ([(tE, w2_free_t)] if w2_free_t else []) + cw
                            src2 = w2bf.ap()[e, :, fh * NFK * D:(fh + 1) * NFK * D]
                            d_w2 = p.op("sync", lambda E_, s=src2, d=w2b:
                                        E_.dma_start(out=d[:, :], in_=s), w=w,
                                        inc=dS, inc_n=16)

                            for tb in range(NTB):
                                if pre_expert is not None:
                                    pre_expert(e, fh * NTB + tb, 2 * NTB)
                                i = tb % 2
                                w = [(vE, web_v[i])] if web_v[i] else []
                                t_w = p.op("tensor", lambda E_, e=e, tb=tb: E_.matmul(
                                    ps_w[:, :], eb_sb[:, e * 128:(e + 1) * 128],
                                    rfm8[:, tb * QW:(tb + 1) * QW],
                                    start=True, stop=True), w=w, inc=tE)
                                web_v[i] = p.op("vector", lambda E_, i=i:
                                                E_.tensor_copy(
                                                    web[:, i * QW:(i + 1) * QW],
                                                    ps_w[:, :]),
                                                w=[(tE, t_w)], inc=vE)
                                wi = tb % 2
                                for fk in range(NFK):
                                    hb = fk % 2
                                    w = [(dS, d_w1), (vE, v_xn_local)]
                                    if gelu_s[hb]:
                                        w.append((sE, gelu_s[hb]))
                                    for dc in range(DC):
                                        p.op("tensor",
                                             lambda E_, fk=fk, dc=dc, hb=hb, tb=tb:
                                             E_.matmul(
                                                 ps_h[hb][:, :],
                                                 w1b[:, dc * FH + fk * 128:
                                                     dc * FH + (fk + 1) * 128],
                                                 xn_sb[:, dc * TL + tb * QW:
                                                       dc * TL + (tb + 1) * QW],
                                                 start=(dc == 0), stop=(dc == DC - 1)),
                                             w=(w if dc == 0 else ()),
                                             inc=(tE if dc == DC - 1 else None))
                                    t_h = tE.n
                                    bcol = e * 2 * NFK + fh * NFK + fk
                                    w = [(tE, t_h)]
                                    if scale_v[hb]:
                                        w.append((vE, scale_v[hb]))
                                    g_s = p.op("scalar", lambda E_, hb=hb, bcol=bcol:
                                               E_.activation(
                                                   hraw[:, hb * QW:(hb + 1) * QW],
                                                   ps_h[hb][:, :], AF.Gelu,
                                                   bias=b1_sb[:, bcol:bcol + 1]),
                                               w=w, inc=sE)
                                    gelu_s[hb] = g_s
                                    scale_v[hb] = p.op(
                                        "vector", lambda E_, hb=hb, fk=fk, wi=wi:
                                        E_.tensor_mul(
                                            h_sb[:, fk * QW:(fk + 1) * QW],
                                            hraw[:, hb * QW:(hb + 1) * QW],
                                            web[:, wi * QW:(wi + 1) * QW]),
                                        w=[(sE, g_s)], inc=vE)
                                v_h = vE.n
                                if tb == NTB - 1:
                                    w1_free_t = tE.n
                                for dcb in range(DC):
                                    ob = dcb % 4
                                    w = [(vE, v_h), (dS, d_w2)]
                                    if oadd_v[ob]:
                                        w.append((vE, oadd_v[ob]))
                                    use_b2 = first_b2[tb][dcb]
                                    first_b2[tb][dcb] = False
                                    if use_b2:
                                        p.op("tensor", lambda E_, dcb=dcb, tb=tb, ob=ob:
                                             E_.matmul(
                                                 ps_o[ob][:, :],
                                                 b2_sb[:, dcb * 128:(dcb + 1) * 128],
                                                 rfm8[:, tb * QW:(tb + 1) * QW],
                                                 start=True, stop=False), w=w)
                                    for fk in range(NFK):
                                        p.op("tensor",
                                             lambda E_, dcb=dcb, fk=fk, ob=ob,
                                             use_b2=use_b2: E_.matmul(
                                                 ps_o[ob][:, :],
                                                 w2b[:, fk * D + dcb * 128:
                                                     fk * D + (dcb + 1) * 128],
                                                 h_sb[:, fk * QW:(fk + 1) * QW],
                                                 start=(fk == 0 and not use_b2),
                                                 stop=(fk == NFK - 1)),
                                             w=(w if (fk == 0 and not use_b2) else ()),
                                             inc=(tE if fk == NFK - 1 else None))
                                    t_o = tE.n
                                    xs = x_sb[:, dcb * TL + tb * QW:
                                              dcb * TL + (tb + 1) * QW]
                                    oadd_v[ob] = p.op(
                                        "vector", lambda E_, ob=ob, xs=xs:
                                        E_.tensor_add(xs, ps_o[ob][:, :], xs),
                                        w=[(tE, t_o)], inc=vE)
                            w2_free_t = tE.n
                    barrier()
                    p.flush()

        emit_ffn("A", pre_expert=lambda e, part, nparts: (
            conv_expert(e + 1, part, nparts) if e + 1 < E else None))
        if cfg.DBG:
            for dc in range(DC):
                dma("gpsimd", dG, dbg1[dc * 128:(dc + 1) * 128, :],
                    x_sb[:, dc * TL:(dc + 1) * TL], fence=(dc == 0))
            barrier()

        lsc2 = ExitStack()
        with lsc2:
            xn2 = lsc2.enter_context(nc.sbuf_tensor("xn2", [128, DC * TL], BF16))
            lns2 = mk_ln_scope("L2", lsc2)
            tb_done2 = emit_ln(xn2, lns2)
            TLH = TL // HVS
            for hv in range(HVS):
                v_half = tb_done2[(hv + 1) * NTB // HVS - 1]
                for dc in range(DC):
                    dma("gpsimd", dG,
                        ag_in2[hv].ap()[dc * 128:(dc + 1) * 128, :],
                        xn2[:, dc * TL + hv * TLH: dc * TL + (hv + 1) * TLH],
                        w=([(vE, v_half)] if dc == 0 else ()), fence=(dc == 0))
                d_agin = dG.n
                p.op("gpsimd", lambda E_, hv=hv: E_.collective_compute(
                    "AllGather", ALU.bypass,
                    replica_groups=[[0, 1, 2, 3], [4, 5, 6, 7]],
                    ins=[ag_in2[hv].ap().opt()], outs=[ag_out2[hv].ap().opt()]),
                    w=[(dG, d_agin)], inc=ccA2[hv])
            barrier()
            p.flush()

        asc = ExitStack()
        with asc:
            QT = asc.enter_context(nc.sbuf_tensor("QT", [128, 2 * SEQ], BF16))
            KT = asc.enter_context(nc.sbuf_tensor("KT", [128, 2 * SEQ], BF16))
            VS = asc.enter_context(nc.sbuf_tensor("VS", [128, NKT * 260], BF16))
            qsc = ExitStack()
            with qsc:
                NTBH = NTB // HVS
                TH = TL // HVS
                xg = qsc.enter_context(nc.sbuf_tensor("xg", [128, DC * TH], BF16))
                wqkv_sb = qsc.enter_context(
                    nc.sbuf_tensor("wqkv_sb", [128, DC * 768], BF16))
                ps_q = [qsc.enter_context(
                    nc.psum_tensor(f"ps_q{i}", [128, QW], F32)) for i in range(2)]
                ps_v = [qsc.enter_context(
                    nc.psum_tensor(f"ps_v{i}", [128, 256], F32)) for i in range(2)]
                d_wq = dma("sync", dS, wqkv_sb[:, :], wqkvbf.ap(), fence=True)
                v_vones = p.op("vector", lambda E_: E_.memset(
                    VS[:, :].rearrange("p (t h z) -> p t h z", h=HL, z=65)
                    [:, :, :, 64:65], 1.0), inc=vE)
                qk_s = [0, 0]
                v_v = [0, 0]
                xg_free_t = 0
                for rb in range(4):
                  for hv in range(HVS):
                    w = [(ccA2[hv], 1)]
                    if xg_free_t:
                        w.append((tE, xg_free_t))
                    for dc in range(DC):
                        dma("sync", dS, xg[:, dc * TH:(dc + 1) * TH],
                            ag_out2[hv].ap()[rb * D + dc * 128:
                                             rb * D + (dc + 1) * 128, :],
                            w=(w if dc == 0 else ()), fence=(dc == 0))
                    d_xg = dS.n
                    for pr in range(2):
                        for tbh in range(NTBH):
                            tg = rb * NTB + hv * NTBH + tbh
                            for wh, base in (("q", 0), ("k", 256)):
                                i = (pr * NTBH + tbh) % 2
                                w = [(dS, d_xg), (dS, d_wq)]
                                if qk_s[i]:
                                    w.append((sE, qk_s[i]))
                                for dc in range(DC):
                                    p.op("tensor",
                                         lambda E_, dc=dc, pr=pr, tbh=tbh, i=i, base=base:
                                         E_.matmul(
                                             ps_q[i][:, :],
                                             wqkv_sb[:, dc * 768 + base + pr * 128:
                                                     dc * 768 + base + (pr + 1) * 128],
                                             xg[:, dc * TH + tbh * QW:
                                                dc * TH + (tbh + 1) * QW],
                                             start=(dc == 0), stop=(dc == DC - 1)),
                                         w=(w if dc == 0 else ()),
                                         inc=(tE if dc == DC - 1 else None))
                                t_qk = tE.n
                                dst = QT if wh == "q" else KT
                                qk_s[i] = p.op("scalar",
                                               lambda E_, dst=dst, pr=pr, tg=tg, i=i:
                                               E_.copy(
                                                   dst[:, pr * SEQ + tg * QW:
                                                       pr * SEQ + (tg + 1) * QW],
                                                   ps_q[i][:, :]),
                                               w=[(tE, t_qk)], inc=sE)
                    for tt in range(TH // 128):
                        i = tt % 2
                        ttg = rb * (TL // 128) + hv * (TH // 128) + tt
                        w = [(dS, d_xg), (dS, d_wq), (vE, v_vones)]
                        if v_v[i]:
                            w.append((vE, v_v[i]))
                        for dc in range(DC):
                            p.op("tensor", lambda E_, dc=dc, tt=tt, i=i: E_.matmul(
                                ps_v[i][:, :],
                                xg[:, dc * TH + tt * 128: dc * TH + (tt + 1) * 128],
                                wqkv_sb[:, dc * 768 + 512: dc * 768 + 768],
                                start=(dc == 0), stop=(dc == DC - 1)),
                                w=(w if dc == 0 else ()),
                                inc=(tE if dc == DC - 1 else None))
                        t_v = tE.n
                        for hl in range(HL):
                            vv = p.op("vector", lambda E_, ttg=ttg, hl=hl, i=i:
                                      E_.tensor_copy(
                                          VS[:, ttg * 260 + hl * 65:
                                             ttg * 260 + hl * 65 + 64],
                                          ps_v[i][:, hl * 64:(hl + 1) * 64]),
                                      w=([(tE, t_v)] if hl == 0 else ()),
                                      inc=(vE if hl == HL - 1 else None))
                        v_v[i] = vv
                    xg_free_t = tE.n
                barrier()
                p.flush()
            fsc = ExitStack()
            with fsc:
                FSB = lambda n_, s_, d_: fsc.enter_context(nc.sbuf_tensor(n_, s_, d_))
                FPS = lambda n_, s_, d_: fsc.enter_context(nc.psum_tensor(n_, s_, d_))
                masks_sb = FSB("masks_sb", [128, DIAG * QW], BF16)
                cb_sb = FSB("cb_sb", [128, HL * NQB], F32)
                NB = HL * NQB
                rows_at = FSB("rows_at", [1, 2 * QW + 4 * NB], F32)
                sqn = FSB("sqn", [64, 2 * QW], BF16)
                pe = [FSB(f"pe{i}", [128, 2 * QW], BF16) for i in range(3)]
                pm = [FSB(f"pm{i}", [128, 2 * QW], BF16) for i in range(3)]
                Lb_sb = FSB("Lb_sb", [64, QW], F32)
                ost = [FSB(f"ost{i}", [64, QW], BF16) for i in range(2)]
                nsc = ExitStack()
                ps_n = [nsc.enter_context(nc.psum_tensor(f"ps_n{i}", [1, QW], F32))
                        for i in range(2)]
                ps_cb = nsc.enter_context(nc.psum_tensor("ps_cb", [128, HL * NQB], F32))
                d_msk = dma("sync", dS, masks_sb[:, :], mskin, fence=True)
                nrow = rows_at[0:1, 0:QW]
                nrow2 = rows_at[0:1, QW:2 * QW]
                bq_r = rows_at[0:1, 2 * QW:2 * QW + NB]
                bk_r = rows_at[0:1, 2 * QW + NB:2 * QW + 2 * NB]
                pk_r = rows_at[0:1, 2 * QW + 2 * NB:2 * QW + 3 * NB]
                nc_r = rows_at[0:1, 2 * QW + 3 * NB:2 * QW + 4 * NB]
                sq_done_t = [0, 0]
                red_done_v = [0, 0]
                for src_, dstr in ((QT, bq_r), (KT, bk_r)):
                    for h in range(HL):
                        pr, s2 = h // 2, h % 2
                        for qb in range(NQB):
                            it = h * NQB + qb
                            sl = it % 2
                            ssrc = src_[s2 * 64:s2 * 64 + 64,
                                        pr * SEQ + qb * QW: pr * SEQ + (qb + 1) * QW]
                            w = [(tE, sq_done_t[sl])] if sq_done_t[sl] else []
                            s_sq = p.op("scalar", lambda E_, ssrc=ssrc, sl=sl:
                                        E_.activation(sqn[:, sl * QW:(sl + 1) * QW],
                                                      ssrc, AF.Square), w=w, inc=sE)
                            w = [(sE, s_sq)]
                            if red_done_v[sl]:
                                w.append((vE, red_done_v[sl]))
                            tv = p.op("tensor", lambda E_, sl=sl: E_.matmul(
                                ps_n[sl][:, :], ones_cb[0:64, :],
                                sqn[:, sl * QW:(sl + 1) * QW],
                                start=True, stop=True), w=w, inc=tE)
                            sq_done_t[sl] = tv
                            red_done_v[sl] = p.op(
                                "vector", lambda E_, dstr=dstr, it=it, sl=sl:
                                E_.reduce_max(dstr[0:1, it:it + 1],
                                              ps_n[sl][0:1, :], axis=AX.X),
                                w=[(tE, tv)], inc=vE)
                for h in range(HL):
                    p.op("vector", lambda E_, h=h: E_.tensor_tensor_scan(
                        pk_r[0:1, h * NQB:(h + 1) * NQB],
                        bk_r[0:1, h * NQB:(h + 1) * NQB],
                        bk_r[0:1, h * NQB:(h + 1) * NQB],
                        -1e30, ALU.max, ALU.bypass))
                v_pq = p.op("vector", lambda E_: E_.tensor_mul(nc_r, bq_r, pk_r),
                            inc=vE)
                for pr_ in range(2):
                    a0 = nc_r[0:1, (2 * pr_) * NQB:(2 * pr_ + 1) * NQB]
                    a1 = nc_r[0:1, (2 * pr_ + 1) * NQB:(2 * pr_ + 2) * NQB]
                    p.op("vector", lambda E_, a0=a0, a1=a1: E_.tensor_max(a0, a0, a1))
                    p.op("vector", lambda E_, a0=a0, a1=a1: E_.tensor_copy(a1, a0))
                s_cr = p.op("scalar", lambda E_: E_.activation(
                    nrow2[0:1, 0:NB], nc_r, AF.Sqrt), w=[(vE, vE.n)], inc=sE)
                v_nc = p.op("vector", lambda E_: E_.tensor_scalar_mul(
                    nc_r, nrow2[0:1, 0:NB], -1.0), w=[(sE, s_cr)], inc=vE)
                t_cb = p.op("tensor", lambda E_: E_.matmul(
                    ps_cb[:, :], ones_rf[:, :], nc_r, start=True, stop=True),
                    w=[(vE, v_nc)], inc=tE)
                v_cb = p.op("vector", lambda E_: E_.tensor_copy(
                    cb_sb[:, :], ps_cb[:, :]), w=[(tE, t_cb)], inc=vE)
                p.flush()
                nsc.close()
                NSL = 3
                ps_sp = [FPS(f"ps_sp{i}", [128, 2 * QW], F32) for i in range(NSL)]
                ps_op = [FPS(f"ps_op{i}", [65, QW], F32) for i in range(2)]
                exp_s = [0, 0, 0]
                lb_war = [0, 0, 0]
                pv_done = [[], [], []]
                o_v = [[0, 0], [0, 0]]
                ost_d = [0, 0]
                NSL = 3
                for pr in range(2):
                    for qb in range(NQB):
                        nkb = DIAG * qb + DIAG
                        pends = []

                        def emit_pv(kb, sl, mwaits, e_s, qb_=qb, pr_=pr, nkb_=nkb):
                            for hq in range(2):
                                diag = kb >= DIAG * qb_
                                if diag:
                                    w = [(vE, mwaits[hq])]
                                    rhs = pm[sl][:, hq * QW:(hq + 1) * QW]
                                else:
                                    w = [(sE, e_s)]
                                    rhs = pe[sl][:, hq * QW:(hq + 1) * QW]
                                if kb == 0 and o_v[hq][0]:
                                    w.append((vE, o_v[hq][0]))
                                hl_ = 2 * pr_ + hq
                                tv = p.op("tensor",
                                          lambda E_, kb=kb, hl_=hl_, hq=hq,
                                          nkb_=nkb_, rhs=rhs: E_.matmul(
                                              ps_op[hq][:, :],
                                              VS[:, kb * 260 + hl_ * 65:
                                                 kb * 260 + hl_ * 65 + 65],
                                              rhs,
                                              start=(kb == 0),
                                              stop=(kb == nkb_ - 1)),
                                          w=w, inc=tE)
                                pv_done[sl].append((tE, tv))

                        for kb in range(nkb):
                            sl = kb % NSL
                            w = []
                            if exp_s[sl]:
                                w.append((sE, exp_s[sl]))
                            if lb_war[sl]:
                                w.append((vE, lb_war[sl]))
                            for s2 in range(2):
                                p.op("tensor",
                                     lambda E_, s2=s2, pr=pr, kb=kb, qb=qb, sl=sl:
                                     E_.matmul(
                                         ps_sp[sl][:, s2 * QW:(s2 + 1) * QW],
                                         KT[s2 * 64:s2 * 64 + 64,
                                            pr * SEQ + kb * 128:
                                            pr * SEQ + (kb + 1) * 128],
                                         QT[s2 * 64:s2 * 64 + 64,
                                            pr * SEQ + qb * QW:
                                            pr * SEQ + (qb + 1) * QW],
                                         start=True, stop=True),
                                     w=(w if s2 == 0 else ()),
                                     inc=(tE if s2 == 1 else None))
                            t_st = tE.n
                            if len(pends) >= 2:
                                kb_, sl_, mw_, es_ = pends.pop(0)
                                emit_pv(kb_, sl_, mw_, es_)
                            w = [(tE, t_st), (vE, v_cb)] + pv_done[sl]
                            pv_done[sl] = []
                            cidx = (2 * pr) * NQB + qb
                            e_s = p.op("scalar",
                                       lambda E_, sl=sl, cidx=cidx:
                                       E_.activation(
                                           pe[sl][:, 0:2 * QW],
                                           ps_sp[sl][:, 0:2 * QW], AF.Exp,
                                           bias=cb_sb[:, cidx:cidx + 1]),
                                       w=w, inc=sE)
                            exp_s[sl] = e_s
                            mwaits = {}
                            if kb >= DIAG * qb:
                                r = kb - DIAG * qb
                                for hq in range(2):
                                    v_m = p.op("vector",
                                               lambda E_, sl=sl, hq=hq, r=r:
                                               E_.tensor_mul(
                                                   pm[sl][:, hq * QW:(hq + 1) * QW],
                                                   pe[sl][:, hq * QW:(hq + 1) * QW],
                                                   masks_sb[:, r * QW:(r + 1) * QW]),
                                               w=[(sE, e_s), (dS, d_msk)], inc=vE)
                                    mwaits[hq] = v_m
                            pends.append((kb, sl, mwaits, e_s))
                        lastsl = (nkb - 1) % NSL
                        for (kb_, sl_, mw_, es_) in pends:
                            emit_pv(kb_, sl_, mw_, es_)
                        t_all = tE.n
                        for hq in range(2):
                            h = 2 * pr + hq
                            bsl = (lastsl + 1 + hq) % NSL
                            p.op("vector", lambda E_, hq=hq: E_.tensor_copy(
                                nrow, ps_op[hq][64:65, :]), w=[(tE, t_all)])
                            v_li = p.op("vector", lambda E_: E_.reciprocal(
                                nrow2, nrow), inc=vE)
                            w = [(vE, v_li)]
                            if exp_s[bsl]:
                                w.append((sE, exp_s[bsl]))
                            t_lb = p.op("tensor", lambda E_, bsl=bsl: E_.matmul(
                                ps_sp[bsl][0:64, 0:QW], ones_rf[0:1, 0:64], nrow2,
                                start=True, stop=True), w=w, inc=tE)
                            v_lbc = p.op("vector", lambda E_, bsl=bsl: E_.tensor_copy(
                                Lb_sb[:, :], ps_sp[bsl][0:64, 0:QW]),
                                w=[(tE, t_lb)], inc=vE)
                            lb_war[bsl] = v_lbc
                            exp_s[bsl] = 0
                            ow = hq
                            w = [(dG, ost_d[ow])] if ost_d[ow] else []
                            v_o = p.op("vector", lambda E_, hq=hq, ow=ow:
                                       E_.tensor_mul(
                                           ost[ow][:, :], ps_op[hq][0:64, :],
                                           Lb_sb[:, :]), w=w, inc=vE)
                            o_v[hq][0] = v_o
                            ost_d[ow] = dma(
                                "gpsimd", dG,
                                o_dram.ap()[h * 64:(h + 1) * 64,
                                            qb * QW:(qb + 1) * QW],
                                ost[ow][:, :], w=[(vE, v_o)], fence=True)
                    for hq in range(2):
                        h = 2 * pr + hq
                        p.op("gpsimd", lambda E_, h=h: E_.collective_compute(
                            "AllGather", ALU.bypass,
                            replica_groups=[[0, 1, 2, 3], [4, 5, 6, 7]],
                            ins=[o_dram.ap()[h * 64:(h + 1) * 64, :].opt()],
                            outs=[oag_h[h].ap().opt()]),
                            w=[(dG, dG.n)], inc=ccB4[h])
                if cfg.DBG:
                    dma("gpsimd", dG, dbg3, o_dram.ap(), fence=True)
                barrier()
                p.flush()

        psc = ExitStack()
        with psc:
            wpj_sb = psc.enter_context(nc.sbuf_tensor("wpj_sb", [128, DC * D], BF16))
            og = psc.enter_context(nc.sbuf_tensor("og", [128, DC * TL], BF16))
            bpr_f = psc.enter_context(nc.sbuf_tensor("bpr_f", [1, D], F32))
            bpr_b = psc.enter_context(nc.sbuf_tensor("bpr_b", [1, D], BF16))
            onesq = psc.enter_context(nc.sbuf_tensor("onesq", [1, QW], BF16))
            ps_p = [psc.enter_context(nc.psum_tensor(f"ps_p{i}", [128, QW], F32))
                    for i in range(2)]
            dma("sync", dS, wpj_sb[:, :], wpjbf.ap(), fence=True)
            d_pj = dma("sync", dS, bpr_f[:, :], bprin)
            p.op("vector", lambda E_: E_.memset(onesq[:, :], 1.0))
            v_bpr = p.op("vector", lambda E_: E_.tensor_copy(bpr_b[:, :], bpr_f[:, :]),
                         w=[(dS, d_pj)], inc=vE)

            def emit_og(E_):
                last = None
                with E_.register("gr0") as gr0, E_.register("gr1") as gr1:
                    E_.reg_load(gr0, rk_sb[0:1, 0:1])
                    E_.reg_mul(gr0, gr0, TL)
                    for kc in range(DC):
                        r_, hh = kc // 2, kc % 2
                        for hq in range(2):
                            hl_ = hh * 2 + hq
                            E_.reg_mov(gr1, r_ * 64 * SEQ)
                            E_.reg_add(gr1, gr1, gr0)
                            last = E_.dma_start(
                                out=og[hq * 64:(hq + 1) * 64,
                                       kc * TL:(kc + 1) * TL],
                                in_=bass.AP(oag_h[hl_], gr1, [[SEQ, 64], [1, TL]]))
                            last.then_inc(dG.h, 16)
                return last
            fence_w = [(dG, dG.n)] if dG.n else []
            dG.n += 16 * DC * 2
            p.op("gpsimd", emit_og,
                 w=fence_w + [(s, 1) for s in ccB4] + [(dS, d_pj)])
            d_og = dG.n
            padd_v = [0, 0]
            for tb in range(NTB):
                for dcb in range(DC):
                    i = (tb * DC + dcb) % 2
                    w = [(dG, d_og), (dS, d_pj), (vE, v_bpr)]
                    if padd_v[i]:
                        w.append((vE, padd_v[i]))
                    p.op("tensor", lambda E_, dcb=dcb, i=i: E_.matmul(
                        ps_p[i][:, :], bpr_b[0:1, dcb * 128:(dcb + 1) * 128],
                        onesq[:, :], start=True, stop=False), w=w)
                    for kc in range(DC):
                        p.op("tensor", lambda E_, kc=kc, dcb=dcb, tb=tb, i=i:
                             E_.matmul(
                                 ps_p[i][:, :],
                                 wpj_sb[:, kc * D + dcb * 128: kc * D + (dcb + 1) * 128],
                                 og[:, kc * TL + tb * QW: kc * TL + (tb + 1) * QW],
                                 start=False, stop=(kc == DC - 1)),
                             inc=(tE if kc == DC - 1 else None))
                    t_p = tE.n
                    xs = x_sb[:, dcb * TL + tb * QW: dcb * TL + (tb + 1) * QW]
                    padd_v[i] = p.op("vector", lambda E_, i=i, xs=xs: E_.tensor_add(
                        xs, ps_p[i][:, :], xs), w=[(tE, t_p)], inc=vE)
            barrier()
            p.flush()
        if cfg.DBG:
            for dc in range(DC):
                dma("gpsimd", dG, dbg2[dc * 128:(dc + 1) * 128, :],
                    x_sb[:, dc * TL:(dc + 1) * TL], fence=(dc == 0))
            barrier()

        emit_ffn("B")

        for dc in range(DC):
            dma("gpsimd", dG, yout[dc * 128:(dc + 1) * 128, :],
                x_sb[:, dc * TL:(dc + 1) * TL], fence=(dc == 0))
        p.q["gpsimd"].append((((dG, dG.n),), None, None))
        p.flush()
    return nc


def make_inputs(cfg, inputs):
    TL, SEQ, QW, DIAG = cfg.TL, cfg.SEQ, cfg.QW, cfg.DIAG
    x = np.asarray(inputs["L0_attn_x"], np.float32)
    w_qkv = np.asarray(inputs["L0_attn_w_qkv"], np.float32)
    w_proj = np.asarray(inputs["L0_attn_w_proj"], np.float32)
    b_proj = np.asarray(inputs["L0_attn_b_proj"], np.float32)
    gw = np.asarray(inputs["L0_ffn_gate_w"], np.float32)
    gb = np.asarray(inputs["L0_ffn_gate_b"], np.float32)
    w1 = np.ascontiguousarray(np.asarray(inputs["L0_ffn_w1"], np.float32))
    b1 = np.asarray(inputs["L0_ffn_b1"], np.float32)
    w2 = np.ascontiguousarray(np.asarray(inputs["L0_ffn_w2"], np.float32))
    b2 = np.ascontiguousarray(np.asarray(inputs["L0_ffn_b2"], np.float32))

    b1l = np.ascontiguousarray(
        b1.reshape(8, 32, 128).transpose(2, 0, 1).reshape(128, 256))
    gwl = np.ascontiguousarray(
        gw.reshape(8, 128, 8).transpose(1, 0, 2).reshape(128, 64))
    gbl = np.ascontiguousarray(gb.reshape(1, 8))
    wpjt = np.ascontiguousarray(
        w_proj.T.reshape(8, 128, 1024).transpose(1, 0, 2).reshape(128, 8192))
    bpr = np.ascontiguousarray(b_proj.reshape(1, 1024))
    idb = np.eye(128, dtype=np.float32).astype(BFNP)
    masks = np.zeros((128, DIAG * QW), np.float32)
    ii = np.arange(128)[:, None]
    jj = np.arange(QW)[None, :]
    for ri in range(DIAG):
        masks[:, ri * QW:(ri + 1) * QW] = (jj >= ii + ri * 128)
    masks = masks.astype(BFNP)
    ebasis = np.zeros((8, 1024), np.float32)
    for e in range(8):
        ebasis[e, e * 128:(e + 1) * 128] = 1.0
    ebasis = ebasis.astype(BFNP)

    maps = []
    for c in range(8):
        g, r = c // 4, c % 4
        xl = np.ascontiguousarray(x[g, r * TL:(r + 1) * TL, :].T)
        hs = slice(4 * r, 4 * r + 4)
        wq = np.ascontiguousarray(
            w_qkv[hs, 0:64, :].transpose(2, 0, 1).reshape(1024, 256))
        wk = np.ascontiguousarray(
            w_qkv[hs, 64:128, :].transpose(2, 0, 1).reshape(1024, 256))
        wv = np.ascontiguousarray(
            w_qkv[hs, 128:192, :].transpose(2, 0, 1).reshape(1024, 256))
        wqkvl = np.concatenate([wq, wk, wv], axis=1)
        wqkvl = np.ascontiguousarray(
            wqkvl.reshape(8, 128, 768).transpose(1, 0, 2).reshape(128, 8 * 768))
        maps.append({
            "x": xl, "w1": w1, "w2": w2, "b1l": b1l, "b2": b2, "gwl": gwl,
            "gb": gbl, "wqkvl": wqkvl, "wpjt": wpjt, "bpr": bpr,
            "idb": idb, "masks": masks, "ebasis": ebasis,
            "rank": np.array([[r]], dtype=np.uint32),
        })
    return maps


def kernel(**inputs):
    cfg = CFG()
    maps = make_inputs(cfg, inputs)
    nc = build_program(cfg)
    res = run_bass_kernel_spmd(nc, maps, core_ids=list(range(8)))
    out = np.empty((2, 8192, 1024), np.float32)
    for c in range(8):
        g, r = c // 4, c % 4
        out[g, r * cfg.TL:(r + 1) * cfg.TL, :] = res.results[c]["out"].T
    return out


# version 17
# speedup vs baseline: 1.0958x; 1.0785x over previous
import numpy as np
import ml_dtypes
import concourse.bass as bass
import concourse.mybir as mybir
from concourse.bass_utils import run_bass_kernel_spmd
from contextlib import ExitStack

BFNP = ml_dtypes.bfloat16
F32 = mybir.dt.float32
BF16 = mybir.dt.bfloat16
U32 = mybir.dt.uint32
AF = mybir.ActivationFunctionType
ALU = mybir.AluOpType
AX = mybir.AxisListType

ENGS = ("sync", "tensor", "vector", "scalar", "gpsimd")


class Sem:
    __slots__ = ("name", "n", "h")

    def __init__(self, name):
        self.name, self.n, self.h = name, 0, None


class P:

    def __init__(self, nc):
        self.nc = nc
        self.q = {e: [] for e in ENGS}
        self.last_wait = {e: {} for e in ENGS}
        self.serialize = {}

    def op(self, eng, fn=None, w=(), inc=None, inc_n=1, chain=True):
        own = self.serialize.get(eng)
        if own is not None and fn is not None:
            if chain:
                w = list(w) + [(own, own.n)]
            if inc is None:
                inc = own
                inc_n = 1
        if inc is not None:
            inc.n += inc_n
        self.q[eng].append((tuple(w), fn, (inc, inc_n) if inc is not None else None))
        return inc.n if inc is not None else None

    def flush(self):
        if not any(self.q[e] for e in ENGS):
            return
        with self.nc.Block() as block:
            for eng in ENGS:
                ops = self.q[eng]
                if not ops:
                    continue
                lastw = self.last_wait[eng]

                def body(E_, ops=ops, lastw=lastw):
                    for waits, fn, inc in ops:
                        for (s, v) in waits:
                            if lastw.get(id(s), 0) < v:
                                E_.wait_ge(s.h, v)
                                lastw[id(s)] = v
                        ins = fn(E_) if fn is not None else None
                        if inc is not None:
                            ins.then_inc(inc[0].h, inc[1])
                getattr(block, eng)(body)
        self.q = {e: [] for e in ENGS}


class CFG:
    def __init__(self, TL=2048, SEQ=8192, QW=512):
        self.TL, self.SEQ, self.QW = TL, SEQ, QW
        self.NTB = TL // QW
        self.NTT = TL // 128
        self.NQB = SEQ // QW
        self.DIAG = QW // 128
        self.D, self.F, self.E, self.DC, self.HL = 1024, 4096, 8, 8, 4
        self.DBG = False
        self.FH = self.F // 2
        self.NFK = self.FH // 128


def build_program(cfg: CFG):
    nc = bass.Bass("TRN2", target_bir_lowering=False)
    TL, SEQ, QW, D, E, DC = cfg.TL, cfg.SEQ, cfg.QW, cfg.D, cfg.E, cfg.DC
    NTB, NTT, NQB, DIAG = cfg.NTB, cfg.NTT, cfg.NQB, cfg.DIAG
    NFK, FH, HL, F = cfg.NFK, cfg.FH, cfg.HL, cfg.F
    NKT = SEQ // 128
    HVS = 2 if cfg.NTB >= 2 else 1

    xin = nc.dram_tensor("x", [D, TL], F32, kind="ExternalInput").ap()
    w1in = nc.dram_tensor("w1", [E, D, F], F32, kind="ExternalInput").ap()
    w2in = nc.dram_tensor("w2", [E, F, D], F32, kind="ExternalInput").ap()
    b1in = nc.dram_tensor("b1l", [128, 2 * E * NFK], F32, kind="ExternalInput").ap()
    b2in = nc.dram_tensor("b2", [E, D], F32, kind="ExternalInput").ap()
    gwin = nc.dram_tensor("gwl", [128, DC * E], F32, kind="ExternalInput").ap()
    gbin = nc.dram_tensor("gb", [1, E], F32, kind="ExternalInput").ap()
    wqkvin = nc.dram_tensor("wqkvl", [128, DC * 768], F32, kind="ExternalInput").ap()
    wpjin = nc.dram_tensor("wpjt", [128, DC * D], F32, kind="ExternalInput").ap()
    bprin = nc.dram_tensor("bpr", [1, D], F32, kind="ExternalInput").ap()
    idbin = nc.dram_tensor("idb", [128, 128], BF16, kind="ExternalInput").ap()
    mskin = nc.dram_tensor("masks", [128, DIAG * QW], BF16, kind="ExternalInput").ap()
    ebin = nc.dram_tensor("ebasis", [E, E * 128], BF16, kind="ExternalInput").ap()
    rkin = nc.dram_tensor("rank", [1, 1], U32, kind="ExternalInput").ap()
    yout = nc.dram_tensor("out", [D, TL], F32, kind="ExternalOutput").ap()
    if cfg.DBG:
        dbg1 = nc.dram_tensor("dbg1", [D, TL], F32, kind="ExternalOutput").ap()
        dbg2 = nc.dram_tensor("dbg2", [D, TL], F32, kind="ExternalOutput").ap()
        dbg3 = nc.dram_tensor("dbg3", [HL * 64, SEQ], BF16, kind="ExternalOutput").ap()
        dbg4 = nc.dram_tensor("dbg4", [E, TL], BF16, kind="ExternalOutput").ap()
        dbg5 = nc.dram_tensor("dbg5", [128, 128], F32, kind="ExternalOutput").ap()
        dbg6 = nc.dram_tensor("dbg6", [128, TL], BF16, kind="ExternalOutput").ap()

    w1bf = nc.dram_tensor("w1bf", [E, 128, DC * F], BF16)
    w2bf = nc.dram_tensor("w2bf", [E, 128, (F // 128) * D], BF16)
    wqkvbf = nc.dram_tensor("wqkvbf", [128, DC * 768], BF16)
    wpjbf = nc.dram_tensor("wpjbf", [128, DC * D], BF16)
    ag_in2 = [nc.dram_tensor(f"ag_in{i}", [D, TL // HVS], BF16) for i in range(HVS)]
    ag_out2 = [nc.dram_tensor(f"ag_out{i}", [4 * D, TL // HVS], BF16)
               for i in range(HVS)]
    o_dram = nc.dram_tensor("o_dram", [HL * 64, SEQ], BF16)
    oag_h = [nc.dram_tensor(f"oag_h{i}", [4 * 64, SEQ], BF16) for i in range(HL)]

    p = P(nc)
    dS, dG = Sem("dS"), Sem("dG")
    tE, vE, sE = Sem("tE"), Sem("vE"), Sem("sE")
    p.serialize = {"vector": vE, "scalar": sE}
    ccA, ccB = Sem("ccA"), Sem("ccB")
    ccA2 = [Sem("ccA0"), Sem("ccA1")]
    ccB4 = [Sem(f"ccB{i}") for i in range(4)]
    dP = [Sem("dP0"), Sem("dP1")]
    dQ = [Sem("dQ0"), Sem("dQ1")]
    gE = Sem("gE")
    sems = [dS, dG, tE, vE, sE, ccA, ccB, gE] + dP + dQ + ccA2 + ccB4
    own = {"sync": {dS, dP[0], dP[1]},
           "gpsimd": {dG, ccA, ccB, dQ[0], dQ[1], gE, *ccA2, *ccB4},
           "tensor": {tE}, "vector": {vE}, "scalar": {sE}}

    def barrier():
        snap = [(s, s.n) for s in sems if s.n > 0]
        for eng in ENGS:
            w = tuple((s, v) for (s, v) in snap if s not in own[eng])
            if w:
                p.q[eng].append((w, None, None))

    def dma(eng, sem, out, in_, w=(), fence=False):
        if fence and sem.n:
            w = [(sem, sem.n)] + list(w)
        return p.op(eng, lambda E_, o=out, i=in_: E_.dma_start(out=o, in_=i),
                    w=w, inc=sem, inc_n=16)

    st = ExitStack()
    with st:
        for s in sems:
            s.h = st.enter_context(nc.semaphore(s.name))
        SB = lambda name, shape, dt: st.enter_context(nc.sbuf_tensor(name, shape, dt))
        x_sb = SB("x_sb", [128, DC * TL], F32)
        ones_cf = SB("ones_cf", [128, 1], F32)
        ones_cb = SB("ones_cb", [128, 1], BF16)
        ones_rf = SB("ones_rf", [1, 128], F32)
        ones_rb = SB("ones_rb", [1, 128], BF16)
        id_sb = SB("id_sb", [128, 128], BF16)
        eb_sb = SB("eb_sb", [E, E * 128], BF16)
        b1_sb = SB("b1_sb", [128, 2 * E * NFK], F32)
        b2_sb = SB("b2_sb", [E, D], BF16)
        gw_sb = SB("gw_sb", [128, DC * E], BF16)
        gb_sb = SB("gb_sb", [1, E], BF16)
        rk_sb = SB("rk_sb", [1, 2], U32)

        p.op("vector", lambda E_: E_.memset(ones_cf[:, :], 1.0))
        p.op("vector", lambda E_: E_.memset(ones_cb[:, :], 1.0))
        p.op("vector", lambda E_: E_.memset(ones_rf[:, :], 1.0))
        p.op("vector", lambda E_: E_.memset(ones_rb[:, :], 1.0), inc=vE)

        for dc in range(DC):
            dma("sync", dS, x_sb[:, dc * TL:(dc + 1) * TL],
                xin[dc * 128:(dc + 1) * 128, :])
        dma("sync", dS, id_sb[:, :], idbin)
        dma("sync", dS, eb_sb[:, :], ebin)
        dma("sync", dS, b1_sb[:, :], b1in)
        dma("sync", dS, rk_sb[:, 0:1], rkin)

        stageA = SB("stageA", [128, 1024], F32)
        stageB = SB("stageB", [128, 1024], F32)
        chkA = SB("chkA", [128, 1024], BF16)
        chkB = SB("chkB", [128, 1024], BF16)
        stg, chks = [stageA, stageB], [chkA, chkB]
        cstate = {"i": 0, "conv_v": [0, 0]}

        def conv(dst_dram, src_dram, ncols, scales=None):
            i = cstate["i"] % 2
            cstate["i"] += 1
            w = [(gE, cstate["conv_v"][i])] if cstate["conv_v"][i] else []
            dv = p.op("sync", lambda E_, o=stg[i][:, 0:ncols], s=src_dram:
                      E_.dma_start(out=o, in_=s), w=w, inc=dP[i], inc_n=16)
            w2_ = [(dP[i], dv), (dQ[i], dQ[i].n)]
            parts = scales or [(0, ncols, None)]
            for k, (lo, hi, fac) in enumerate(parts):
                lastp = (k == len(parts) - 1)
                if fac is None:
                    p.op("gpsimd", lambda E_, i=i, lo=lo, hi=hi: E_.tensor_copy(
                        chks[i][:, lo:hi], stg[i][:, lo:hi]),
                        w=(w2_ if k == 0 else ()), inc=(gE if lastp else None))
                else:
                    p.op("gpsimd", lambda E_, i=i, lo=lo, hi=hi, fac=fac:
                         E_.tensor_scalar_mul(chks[i][:, lo:hi], stg[i][:, lo:hi], fac),
                         w=(w2_ if k == 0 else ()), inc=(gE if lastp else None))
            cstate["conv_v"][i] = gE.n
            p.op("gpsimd", lambda E_, o=dst_dram, s=chks[i][:, 0:ncols]:
                 E_.dma_start(out=o, in_=s), w=[(gE, gE.n)], inc=dQ[i], inc_n=16)

        def conv_expert(e, part=0, nparts=1):
            jobs = []
            for dc in range(DC):
                for hf in range(4):
                    jobs.append((w1bf.ap()[e, :, dc * F + hf * 1024:
                                           dc * F + (hf + 1) * 1024],
                                 w1in[e, dc * 128:(dc + 1) * 128,
                                      hf * 1024:(hf + 1) * 1024]))
            for fk in range(F // 128):
                jobs.append((w2bf.ap()[e, :, fk * D:(fk + 1) * D],
                             w2in[e, fk * 128:(fk + 1) * 128, :]))
            n = len(jobs)
            lo = n * part // nparts
            hi = n * (part + 1) // nparts
            for dst, s_ in jobs[lo:hi]:
                conv(dst, s_, 1024)
            if hi == n:
                conv_done[e] = (dQ[0].n, dQ[1].n)

        conv_done = {}
        d1 = dma("sync", dS, stageA[0:1, 0:E], gbin, fence=True)
        g1 = p.op("gpsimd", lambda E_: E_.tensor_copy(gb_sb[:, :], stageA[0:1, 0:E]),
                  w=[(dS, d1)], inc=gE)
        cstate["conv_v"][0] = g1
        d2 = dma("sync", dS, stageB[:, 0:DC * E], gwin, fence=True)
        g2 = p.op("gpsimd", lambda E_: E_.tensor_copy(gw_sb[:, :], stageB[:, 0:DC * E]),
                  w=[(dS, d2)], inc=gE)
        cstate["conv_v"][1] = g2
        d3 = dma("sync", dS, stageA[0:E, 0:D], b2in,
                 w=[(gE, cstate["conv_v"][0])], fence=True)
        g3 = p.op("gpsimd", lambda E_: E_.tensor_copy(b2_sb[:, :], stageA[0:E, 0:D]),
                  w=[(dS, d3)], inc=gE)
        cstate["conv_v"][0] = g3
        conv_expert(0)
        for dc in range(DC):
            conv(wqkvbf.ap()[:, dc * 768:(dc + 1) * 768],
                 wqkvin[:, dc * 768:(dc + 1) * 768], 768,
                 scales=[(0, 256, 0.125), (256, 768, None)])
        for j in range(DC * D // 1024):
            conv(wpjbf.ap()[:, j * 1024:(j + 1) * 1024],
                 wpjin[:, j * 1024:(j + 1) * 1024], 1024)
        barrier()
        p.flush()

        def emit_ln(xn_sb, s_):
            (ps_sx, ps_sq, ps_bc, rowm, rowa, rowb, rowc, Rb, C2b, sq_sb) = s_
            last_sq_t = [0, 0]
            last_norm_v = 0
            tb_done = []
            for tb in range(NTB):
                for dc in range(DC):
                    i = dc % 2
                    xs = x_sb[:, dc * TL + tb * QW: dc * TL + (tb + 1) * QW]
                    w = [(tE, last_sq_t[i])] if last_sq_t[i] else []
                    sv = p.op("scalar", lambda E_, i=i, xs=xs: E_.activation(
                        sq_sb[:, i * QW:(i + 1) * QW], xs, AF.Square), w=w, inc=sE)
                    p.op("tensor", lambda E_, xs=xs, dc=dc: E_.matmul(
                        ps_sx[:, :], ones_cf[:, :], xs,
                        start=(dc == 0), stop=(dc == DC - 1)))
                    tv = p.op("tensor", lambda E_, i=i, dc=dc: E_.matmul(
                        ps_sq[:, :], ones_cb[:, :], sq_sb[:, i * QW:(i + 1) * QW],
                        start=(dc == 0), stop=(dc == DC - 1)),
                        w=[(sE, sv)], inc=tE)
                    last_sq_t[i] = tv
                t_stats = tE.n
                p.op("scalar", lambda E_: E_.mul(rowm[:, :], ps_sx[0:1, :], 1.0 / D),
                     w=[(tE, t_stats)])
                s_rows = p.op("scalar", lambda E_: E_.mul(rowa[:, :], ps_sq[0:1, :],
                                                          1.0 / D), inc=sE)
                p.op("vector", lambda E_: E_.tensor_mul(rowb[:, :], rowm[:, :],
                                                        rowm[:, :]), w=[(sE, s_rows)])
                p.op("vector", lambda E_: E_.tensor_sub(rowc[:, :], rowa[:, :],
                                                        rowb[:, :]))
                v_var = p.op("vector", lambda E_: E_.tensor_scalar_add(
                    rowb[:, :], rowc[:, :], 1e-5), inc=vE)
                s_std = p.op("scalar", lambda E_: E_.activation(
                    rowa[:, :], rowb[:, :], AF.Sqrt),
                    w=[(vE, v_var)], inc=sE)
                p.op("vector", lambda E_: E_.reciprocal(rowb[:, :], rowa[:, :]),
                     w=[(sE, s_std)])
                v_c2 = p.op("vector", lambda E_: E_.scalar_tensor_tensor(
                    rowa[:, :], rowm[:, :], -1.0, rowb[:, :], ALU.mult, ALU.mult),
                    inc=vE)
                w = [(vE, v_c2)]
                if last_norm_v:
                    w.append((vE, last_norm_v))
                t_b1 = p.op("tensor", lambda E_: E_.matmul(
                    ps_bc[:, :], ones_rf[:, :], rowb[:, :], start=True, stop=True),
                    w=w, inc=tE)
                v_rb = p.op("vector", lambda E_: E_.tensor_copy(Rb[:, :], ps_bc[:, :]),
                            w=[(tE, t_b1)], inc=vE)
                t_b2 = p.op("tensor", lambda E_: E_.matmul(
                    ps_bc[:, :], ones_rf[:, :], rowa[:, :], start=True, stop=True),
                    w=[(vE, v_rb)], inc=tE)
                v_cb2 = p.op("vector", lambda E_: E_.tensor_copy(C2b[:, :], ps_bc[:, :]),
                             w=[(tE, t_b2)], inc=vE)
                for dc in range(DC):
                    xs = x_sb[:, dc * TL + tb * QW: dc * TL + (tb + 1) * QW]
                    xns = xn_sb[:, dc * TL + tb * QW: dc * TL + (tb + 1) * QW]
                    p.op("vector", lambda E_, xs=xs: E_.tensor_mul(
                        ps_bc[:, :], xs, Rb[:, :]),
                        w=([(vE, v_cb2)] if dc == 0 else ()))
                    last_norm_v = p.op("vector", lambda E_, xns=xns: E_.tensor_add(
                        xns, ps_bc[:, :], C2b[:, :]), inc=vE)
                tb_done.append(vE.n)
            return tb_done

        def mk_ln_scope(tag, stx):
            return (stx.enter_context(nc.psum_tensor("ps_sx" + tag, [1, QW], F32)),
                    stx.enter_context(nc.psum_tensor("ps_sq" + tag, [1, QW], F32)),
                    stx.enter_context(nc.psum_tensor("ps_bc" + tag, [128, QW], F32)),
                    stx.enter_context(nc.sbuf_tensor("rowm" + tag, [1, QW], F32)),
                    stx.enter_context(nc.sbuf_tensor("rowa" + tag, [1, QW], F32)),
                    stx.enter_context(nc.sbuf_tensor("rowb" + tag, [1, QW], F32)),
                    stx.enter_context(nc.sbuf_tensor("rowc" + tag, [1, QW], F32)),
                    stx.enter_context(nc.sbuf_tensor("Rb" + tag, [128, QW], F32)),
                    stx.enter_context(nc.sbuf_tensor("C2b" + tag, [128, QW], F32)),
                    stx.enter_context(nc.sbuf_tensor("sq" + tag, [128, 2 * QW], BF16)))

        def emit_ffn(tag, pre_expert=None):
            sc = ExitStack()
            with sc:
                xn_sb = sc.enter_context(
                    nc.sbuf_tensor("xn" + tag, [128, DC * TL], BF16))
                rfm8 = sc.enter_context(nc.sbuf_tensor("rfm8" + tag, [E, TL], BF16))
                lsc = ExitStack()
                with lsc:
                    lns = mk_ln_scope(tag, lsc)
                    ps_bc = lns[2]
                    ps_g = lsc.enter_context(nc.psum_tensor("ps_g" + tag, [128, E], F32))
                    ps_tr = lsc.enter_context(nc.psum_tensor("ps_tr" + tag, [E, 128], BF16))
                    lg = lsc.enter_context(nc.sbuf_tensor("lg" + tag, [128, E], F32))
                    gsc = lsc.enter_context(nc.sbuf_tensor("gsc" + tag, [128, 96], F32))
                    rtm = lsc.enter_context(nc.sbuf_tensor("rtm" + tag, [128, E], BF16))
                    v_xn = emit_ln(xn_sb, lns)[-1]
                    last_g_v = 0
                    last_tr_s = 0
                    for tt in range(NTT):
                        w = [(vE, v_xn)]
                        if last_g_v:
                            w.append((vE, last_g_v))
                        p.op("tensor", lambda E_: E_.matmul(
                            ps_g[:, :], ones_rb[:, :], gb_sb[:, :],
                            start=True, stop=False), w=w)
                        for dc in range(DC):
                            p.op("tensor", lambda E_, dc=dc, tt=tt: E_.matmul(
                                ps_g[:, :],
                                xn_sb[:, dc * TL + tt * 128: dc * TL + (tt + 1) * 128],
                                gw_sb[:, dc * E:(dc + 1) * E],
                                start=False, stop=(dc == DC - 1)),
                                inc=(tE if dc == DC - 1 else None))
                        t_g = tE.n
                        v_lg = p.op("vector", lambda E_: E_.tensor_copy(
                            lg[:, :], ps_g[:, :]), w=[(tE, t_g)], inc=vE)
                        last_g_v = v_lg
                        mx = gsc[:, 0:8]
                        d1_, d2_ = gsc[:, 8:16], gsc[:, 16:24]
                        s1_, s2_ = gsc[:, 24:32], gsc[:, 32:40]
                        dm = gsc[:, 40:41]
                        w2c, wmc = gsc[:, 41:42], gsc[:, 42:43]
                        m1c, tm_, ngs = gsc[:, 43:51], gsc[:, 51:59], gsc[:, 59:67]
                        m2c, rmx = gsc[:, 67:75], gsc[:, 75:83]
                        p.op("vector", lambda E_: E_.max(mx, lg[:, :]))
                        p.op("vector", lambda E_: E_.tensor_scalar(
                            d1_, lg[:, :], gsc[:, 0:1], 1e-6, ALU.subtract, ALU.add))
                        p.op("vector", lambda E_: E_.tensor_scalar(
                            d2_, lg[:, :], gsc[:, 1:2], 1e-6, ALU.subtract, ALU.add))
                        v_g1 = p.op("vector", lambda E_: E_.tensor_sub(
                            dm, gsc[:, 1:2], gsc[:, 0:1]), inc=vE)
                        p.op("scalar", lambda E_: E_.activation(s1_, d1_, AF.Sign),
                             w=[(vE, v_g1)])
                        p.op("scalar", lambda E_: E_.activation(s2_, d2_, AF.Sign))
                        p.op("scalar", lambda E_: E_.activation(w2c, dm, AF.Sigmoid))
                        s_g = p.op("scalar", lambda E_: E_.activation(
                            wmc, dm, AF.Sigmoid, scale=-1.0), inc=sE)
                        p.op("vector", lambda E_: E_.tensor_scalar(
                            m1c, s1_, 1.0, 0.5, ALU.add, ALU.mult), w=[(sE, s_g)])
                        p.op("vector", lambda E_: E_.tensor_scalar(
                            tm_, s2_, 1.0, 0.5, ALU.add, ALU.mult))
                        p.op("vector", lambda E_: E_.tensor_sub(m2c, tm_, m1c))
                        p.op("vector", lambda E_: E_.tensor_scalar_mul(
                            rmx, m1c, gsc[:, 42:43]))
                        p.op("vector", lambda E_: E_.tensor_scalar_mul(
                            ngs, m2c, gsc[:, 41:42]))
                        w = [(sE, last_tr_s)] if last_tr_s else []
                        v_rtm = p.op("vector", lambda E_: E_.tensor_add(
                            rtm[:, :], rmx, ngs), w=w, inc=vE)
                        t_tr = p.op("tensor", lambda E_: E_.transpose(
                            ps_tr[:, :], rtm[:, :], id_sb[:, :]),
                            w=[(vE, v_rtm)], inc=tE)
                        last_tr_s = p.op("scalar", lambda E_, tt=tt: E_.mul(
                            rfm8[:, tt * 128:(tt + 1) * 128], ps_tr[:, :], 0.5),
                            w=[(tE, t_tr)], inc=sE)
                    if cfg.DBG and tag == "A":
                        dma("gpsimd", dG, dbg4, rfm8[:, :],
                            w=[(sE, last_tr_s)], fence=True)
                        dma("gpsimd", dG, dbg5[:, 0:8], lg[:, :], w=[(vE, vE.n)])
                        dma("gpsimd", dG, dbg5[:, 8:104], gsc[:, :], w=[(sE, sE.n)])
                        dma("gpsimd", dG, dbg6, xn_sb[:, 0:TL], w=[(vE, vE.n)])
                    barrier()
                    p.flush()
                esc = ExitStack()
                with esc:
                    w1b = esc.enter_context(
                        nc.sbuf_tensor("w1b" + tag, [128, DC * FH], BF16))
                    w2b = esc.enter_context(
                        nc.sbuf_tensor("w2b" + tag, [128, NFK * D], BF16))
                    h_sb = esc.enter_context(
                        nc.sbuf_tensor("h" + tag, [128, NFK * QW], BF16))
                    hraw = esc.enter_context(
                        nc.sbuf_tensor("hraw" + tag, [128, 2 * QW], BF16))
                    web = esc.enter_context(
                        nc.sbuf_tensor("web" + tag, [128, 2 * QW], BF16))
                    ps_h = [esc.enter_context(
                        nc.psum_tensor(f"ps_h{i}" + tag, [128, QW], F32))
                        for i in range(2)]
                    ps_o = [esc.enter_context(
                        nc.psum_tensor(f"ps_o{i}" + tag, [128, QW], F32))
                        for i in range(4)]
                    ps_w = esc.enter_context(
                        nc.psum_tensor("ps_w" + tag, [128, QW], F32))
                    w1_free_t = w2_free_t = 0
                    gelu_s = [0, 0]
                    scale_v = [0, 0]
                    oadd_v = [0, 0, 0, 0]
                    web_v = [0, 0]
                    first_b2 = [[True] * DC for _ in range(NTB)]
                    v_xn_local = vE.n
                    for e in range(E):
                        cw = list(conv_done.get(e, ()))
                        cw = [(dQ[0], cw[0]), (dQ[1], cw[1])] if cw else []
                        for fh in range(2):
                            w = ([(tE, w1_free_t)] if w1_free_t else []) + cw
                            src1 = w1bf.ap()[e, :, :].rearrange(
                                "p (dc f) -> p dc f", dc=DC)[:, :, fh * FH:(fh + 1) * FH]
                            dst1 = w1b[:, :].rearrange("p (dc f) -> p dc f", dc=DC)
                            if dS.n:
                                w = [(dS, dS.n)] + w
                            d_w1 = p.op("sync", lambda E_, s=src1, d=dst1:
                                        E_.dma_start(out=d, in_=s), w=w, inc=dS, inc_n=16)
                            w = ([(tE, w2_free_t)] if w2_free_t else []) + cw
                            src2 = w2bf.ap()[e, :, fh * NFK * D:(fh + 1) * NFK * D]
                            d_w2 = p.op("sync", lambda E_, s=src2, d=w2b:
                                        E_.dma_start(out=d[:, :], in_=s), w=w,
                                        inc=dS, inc_n=16)

                            for tb in range(NTB):
                                if pre_expert is not None:
                                    pre_expert(e, fh * NTB + tb, 2 * NTB)
                                i = tb % 2
                                w = [(vE, web_v[i])] if web_v[i] else []
                                t_w = p.op("tensor", lambda E_, e=e, tb=tb: E_.matmul(
                                    ps_w[:, :], eb_sb[:, e * 128:(e + 1) * 128],
                                    rfm8[:, tb * QW:(tb + 1) * QW],
                                    start=True, stop=True), w=w, inc=tE)
                                web_v[i] = p.op("vector", lambda E_, i=i:
                                                E_.tensor_copy(
                                                    web[:, i * QW:(i + 1) * QW],
                                                    ps_w[:, :]),
                                                w=[(tE, t_w)], inc=vE)
                                wi = tb % 2
                                for fk in range(NFK):
                                    hb = fk % 2
                                    w = [(dS, d_w1), (vE, v_xn_local)]
                                    if gelu_s[hb]:
                                        w.append((sE, gelu_s[hb]))
                                    for dc in range(DC):
                                        p.op("tensor",
                                             lambda E_, fk=fk, dc=dc, hb=hb, tb=tb:
                                             E_.matmul(
                                                 ps_h[hb][:, :],
                                                 w1b[:, dc * FH + fk * 128:
                                                     dc * FH + (fk + 1) * 128],
                                                 xn_sb[:, dc * TL + tb * QW:
                                                       dc * TL + (tb + 1) * QW],
                                                 start=(dc == 0), stop=(dc == DC - 1)),
                                             w=(w if dc == 0 else ()),
                                             inc=(tE if dc == DC - 1 else None))
                                    t_h = tE.n
                                    bcol = e * 2 * NFK + fh * NFK + fk
                                    w = [(tE, t_h)]
                                    if scale_v[hb]:
                                        w.append((vE, scale_v[hb]))
                                    g_s = p.op("scalar", lambda E_, hb=hb, bcol=bcol:
                                               E_.activation(
                                                   hraw[:, hb * QW:(hb + 1) * QW],
                                                   ps_h[hb][:, :], AF.Gelu,
                                                   bias=b1_sb[:, bcol:bcol + 1]),
                                               w=w, inc=sE, chain=False)
                                    gelu_s[hb] = g_s
                                    scale_v[hb] = p.op(
                                        "vector", lambda E_, hb=hb, fk=fk, wi=wi:
                                        E_.tensor_mul(
                                            h_sb[:, fk * QW:(fk + 1) * QW],
                                            hraw[:, hb * QW:(hb + 1) * QW],
                                            web[:, wi * QW:(wi + 1) * QW]),
                                        w=[(sE, g_s), (vE, web_v[wi])], inc=vE,
                                        chain=False)
                                v_h = vE.n
                                if tb == NTB - 1:
                                    w1_free_t = tE.n
                                for dcb in range(DC):
                                    ob = dcb % 4
                                    w = [(vE, v_h), (dS, d_w2)]
                                    if oadd_v[ob]:
                                        w.append((vE, oadd_v[ob]))
                                    use_b2 = first_b2[tb][dcb]
                                    first_b2[tb][dcb] = False
                                    if use_b2:
                                        p.op("tensor", lambda E_, dcb=dcb, tb=tb, ob=ob:
                                             E_.matmul(
                                                 ps_o[ob][:, :],
                                                 b2_sb[:, dcb * 128:(dcb + 1) * 128],
                                                 rfm8[:, tb * QW:(tb + 1) * QW],
                                                 start=True, stop=False), w=w)
                                    for fk in range(NFK):
                                        p.op("tensor",
                                             lambda E_, dcb=dcb, fk=fk, ob=ob,
                                             use_b2=use_b2: E_.matmul(
                                                 ps_o[ob][:, :],
                                                 w2b[:, fk * D + dcb * 128:
                                                     fk * D + (dcb + 1) * 128],
                                                 h_sb[:, fk * QW:(fk + 1) * QW],
                                                 start=(fk == 0 and not use_b2),
                                                 stop=(fk == NFK - 1)),
                                             w=(w if (fk == 0 and not use_b2) else ()),
                                             inc=(tE if fk == NFK - 1 else None))
                                    t_o = tE.n
                                    xs = x_sb[:, dcb * TL + tb * QW:
                                              dcb * TL + (tb + 1) * QW]
                                    oadd_v[ob] = p.op(
                                        "vector", lambda E_, ob=ob, xs=xs:
                                        E_.tensor_add(xs, ps_o[ob][:, :], xs),
                                        w=w + [(tE, t_o)], inc=vE, chain=False)
                            w2_free_t = tE.n
                    barrier()
                    p.flush()

        emit_ffn("A", pre_expert=lambda e, part, nparts: (
            conv_expert(e + 1, part, nparts) if e + 1 < E else None))
        if cfg.DBG:
            for dc in range(DC):
                dma("gpsimd", dG, dbg1[dc * 128:(dc + 1) * 128, :],
                    x_sb[:, dc * TL:(dc + 1) * TL], fence=(dc == 0))
            barrier()

        lsc2 = ExitStack()
        with lsc2:
            xn2 = lsc2.enter_context(nc.sbuf_tensor("xn2", [128, DC * TL], BF16))
            lns2 = mk_ln_scope("L2", lsc2)
            tb_done2 = emit_ln(xn2, lns2)
            TLH = TL // HVS
            for hv in range(HVS):
                v_half = tb_done2[(hv + 1) * NTB // HVS - 1]
                for dc in range(DC):
                    dma("gpsimd", dG,
                        ag_in2[hv].ap()[dc * 128:(dc + 1) * 128, :],
                        xn2[:, dc * TL + hv * TLH: dc * TL + (hv + 1) * TLH],
                        w=([(vE, v_half)] if dc == 0 else ()), fence=(dc == 0))
                d_agin = dG.n
                p.op("gpsimd", lambda E_, hv=hv: E_.collective_compute(
                    "AllGather", ALU.bypass,
                    replica_groups=[[0, 1, 2, 3], [4, 5, 6, 7]],
                    ins=[ag_in2[hv].ap().opt()], outs=[ag_out2[hv].ap().opt()]),
                    w=[(dG, d_agin)], inc=ccA2[hv])
            barrier()
            p.flush()

        asc = ExitStack()
        with asc:
            QT = asc.enter_context(nc.sbuf_tensor("QT", [128, 2 * SEQ], BF16))
            KT = asc.enter_context(nc.sbuf_tensor("KT", [128, 2 * SEQ], BF16))
            VS = asc.enter_context(nc.sbuf_tensor("VS", [128, NKT * 260], BF16))
            qsc = ExitStack()
            with qsc:
                NTBH = NTB // HVS
                TH = TL // HVS
                xg = qsc.enter_context(nc.sbuf_tensor("xg", [128, DC * TH], BF16))
                wqkv_sb = qsc.enter_context(
                    nc.sbuf_tensor("wqkv_sb", [128, DC * 768], BF16))
                ps_q = [qsc.enter_context(
                    nc.psum_tensor(f"ps_q{i}", [128, QW], F32)) for i in range(2)]
                ps_v = [qsc.enter_context(
                    nc.psum_tensor(f"ps_v{i}", [128, 256], F32)) for i in range(2)]
                d_wq = dma("sync", dS, wqkv_sb[:, :], wqkvbf.ap(), fence=True)
                v_vones = p.op("vector", lambda E_: E_.memset(
                    VS[:, :].rearrange("p (t h z) -> p t h z", h=HL, z=65)
                    [:, :, :, 64:65], 1.0), inc=vE)
                qk_s = [0, 0]
                v_v = [0, 0]
                xg_free_t = 0
                for rb in range(4):
                  for hv in range(HVS):
                    w = [(ccA2[hv], 1)]
                    if xg_free_t:
                        w.append((tE, xg_free_t))
                    for dc in range(DC):
                        dma("sync", dS, xg[:, dc * TH:(dc + 1) * TH],
                            ag_out2[hv].ap()[rb * D + dc * 128:
                                             rb * D + (dc + 1) * 128, :],
                            w=(w if dc == 0 else ()), fence=(dc == 0))
                    d_xg = dS.n
                    for pr in range(2):
                        for tbh in range(NTBH):
                            tg = rb * NTB + hv * NTBH + tbh
                            for wh, base in (("q", 0), ("k", 256)):
                                i = (pr * NTBH + tbh) % 2
                                w = [(dS, d_xg), (dS, d_wq)]
                                if qk_s[i]:
                                    w.append((sE, qk_s[i]))
                                for dc in range(DC):
                                    p.op("tensor",
                                         lambda E_, dc=dc, pr=pr, tbh=tbh, i=i, base=base:
                                         E_.matmul(
                                             ps_q[i][:, :],
                                             wqkv_sb[:, dc * 768 + base + pr * 128:
                                                     dc * 768 + base + (pr + 1) * 128],
                                             xg[:, dc * TH + tbh * QW:
                                                dc * TH + (tbh + 1) * QW],
                                             start=(dc == 0), stop=(dc == DC - 1)),
                                         w=(w if dc == 0 else ()),
                                         inc=(tE if dc == DC - 1 else None))
                                t_qk = tE.n
                                dst = QT if wh == "q" else KT
                                qk_s[i] = p.op("scalar",
                                               lambda E_, dst=dst, pr=pr, tg=tg, i=i:
                                               E_.copy(
                                                   dst[:, pr * SEQ + tg * QW:
                                                       pr * SEQ + (tg + 1) * QW],
                                                   ps_q[i][:, :]),
                                               w=[(tE, t_qk)], inc=sE, chain=False)
                    for tt in range(TH // 128):
                        i = tt % 2
                        ttg = rb * (TL // 128) + hv * (TH // 128) + tt
                        w = [(dS, d_xg), (dS, d_wq), (vE, v_vones)]
                        if v_v[i]:
                            w.append((vE, v_v[i]))
                        for dc in range(DC):
                            p.op("tensor", lambda E_, dc=dc, tt=tt, i=i: E_.matmul(
                                ps_v[i][:, :],
                                xg[:, dc * TH + tt * 128: dc * TH + (tt + 1) * 128],
                                wqkv_sb[:, dc * 768 + 512: dc * 768 + 768],
                                start=(dc == 0), stop=(dc == DC - 1)),
                                w=(w if dc == 0 else ()),
                                inc=(tE if dc == DC - 1 else None))
                        t_v = tE.n
                        for hl in range(HL):
                            vv = p.op("vector", lambda E_, ttg=ttg, hl=hl, i=i:
                                      E_.tensor_copy(
                                          VS[:, ttg * 260 + hl * 65:
                                             ttg * 260 + hl * 65 + 64],
                                          ps_v[i][:, hl * 64:(hl + 1) * 64]),
                                      w=([(tE, t_v), (vE, v_v[i])] if hl == 0
                                         else [(tE, t_v)]),
                                      inc=(vE if hl == HL - 1 else None),
                                      chain=False)
                        v_v[i] = vv
                    xg_free_t = tE.n
                barrier()
                p.flush()
            fsc = ExitStack()
            with fsc:
                FSB = lambda n_, s_, d_: fsc.enter_context(nc.sbuf_tensor(n_, s_, d_))
                FPS = lambda n_, s_, d_: fsc.enter_context(nc.psum_tensor(n_, s_, d_))
                masks_sb = FSB("masks_sb", [128, DIAG * QW], BF16)
                cb_sb = FSB("cb_sb", [128, HL * NQB], F32)
                NB = HL * NQB
                rows_at = FSB("rows_at", [1, 2 * QW + 4 * NB], F32)
                sqn = FSB("sqn", [64, 2 * QW], BF16)
                pe = [FSB(f"pe{i}", [128, 2 * QW], BF16) for i in range(3)]
                pm = [FSB(f"pm{i}", [128, 2 * QW], BF16) for i in range(3)]
                Lb_sb = FSB("Lb_sb", [64, QW], F32)
                ost = [FSB(f"ost{i}", [64, QW], BF16) for i in range(2)]
                nsc = ExitStack()
                ps_n = [nsc.enter_context(nc.psum_tensor(f"ps_n{i}", [1, QW], F32))
                        for i in range(2)]
                ps_cb = nsc.enter_context(nc.psum_tensor("ps_cb", [128, HL * NQB], F32))
                d_msk = dma("sync", dS, masks_sb[:, :], mskin, fence=True)
                nrow = rows_at[0:1, 0:QW]
                nrow2 = rows_at[0:1, QW:2 * QW]
                bq_r = rows_at[0:1, 2 * QW:2 * QW + NB]
                bk_r = rows_at[0:1, 2 * QW + NB:2 * QW + 2 * NB]
                pk_r = rows_at[0:1, 2 * QW + 2 * NB:2 * QW + 3 * NB]
                nc_r = rows_at[0:1, 2 * QW + 3 * NB:2 * QW + 4 * NB]
                sq_done_t = [0, 0]
                red_done_v = [0, 0]
                for src_, dstr in ((QT, bq_r), (KT, bk_r)):
                    for h in range(HL):
                        pr, s2 = h // 2, h % 2
                        for qb in range(NQB):
                            it = h * NQB + qb
                            sl = it % 2
                            ssrc = src_[s2 * 64:s2 * 64 + 64,
                                        pr * SEQ + qb * QW: pr * SEQ + (qb + 1) * QW]
                            w = [(tE, sq_done_t[sl])] if sq_done_t[sl] else []
                            s_sq = p.op("scalar", lambda E_, ssrc=ssrc, sl=sl:
                                        E_.activation(sqn[:, sl * QW:(sl + 1) * QW],
                                                      ssrc, AF.Square), w=w, inc=sE, chain=False)
                            w = [(sE, s_sq)]
                            if red_done_v[sl]:
                                w.append((vE, red_done_v[sl]))
                            tv = p.op("tensor", lambda E_, sl=sl: E_.matmul(
                                ps_n[sl][:, :], ones_cb[0:64, :],
                                sqn[:, sl * QW:(sl + 1) * QW],
                                start=True, stop=True), w=w, inc=tE)
                            sq_done_t[sl] = tv
                            red_done_v[sl] = p.op(
                                "vector", lambda E_, dstr=dstr, it=it, sl=sl:
                                E_.reduce_max(dstr[0:1, it:it + 1],
                                              ps_n[sl][0:1, :], axis=AX.X),
                                w=[(tE, tv)], inc=vE)
                for h in range(HL):
                    p.op("vector", lambda E_, h=h: E_.tensor_tensor_scan(
                        pk_r[0:1, h * NQB:(h + 1) * NQB],
                        bk_r[0:1, h * NQB:(h + 1) * NQB],
                        bk_r[0:1, h * NQB:(h + 1) * NQB],
                        -1e30, ALU.max, ALU.bypass))
                v_pq = p.op("vector", lambda E_: E_.tensor_mul(nc_r, bq_r, pk_r),
                            inc=vE)
                for pr_ in range(2):
                    a0 = nc_r[0:1, (2 * pr_) * NQB:(2 * pr_ + 1) * NQB]
                    a1 = nc_r[0:1, (2 * pr_ + 1) * NQB:(2 * pr_ + 2) * NQB]
                    p.op("vector", lambda E_, a0=a0, a1=a1: E_.tensor_max(a0, a0, a1))
                    p.op("vector", lambda E_, a0=a0, a1=a1: E_.tensor_copy(a1, a0))
                s_cr = p.op("scalar", lambda E_: E_.activation(
                    nrow2[0:1, 0:NB], nc_r, AF.Sqrt), w=[(vE, vE.n)], inc=sE)
                v_nc = p.op("vector", lambda E_: E_.tensor_scalar_mul(
                    nc_r, nrow2[0:1, 0:NB], -1.0), w=[(sE, s_cr)], inc=vE)
                t_cb = p.op("tensor", lambda E_: E_.matmul(
                    ps_cb[:, :], ones_rf[:, :], nc_r, start=True, stop=True),
                    w=[(vE, v_nc)], inc=tE)
                v_cb = p.op("vector", lambda E_: E_.tensor_copy(
                    cb_sb[:, :], ps_cb[:, :]), w=[(tE, t_cb)], inc=vE)
                p.flush()
                nsc.close()
                NSL = 3
                ps_sp = [FPS(f"ps_sp{i}", [128, 2 * QW], F32) for i in range(NSL)]
                ps_op = [FPS(f"ps_op{i}", [65, QW], F32) for i in range(2)]
                exp_s = [0, 0, 0]
                lb_war = [0, 0, 0]
                pv_done = [[], [], []]
                o_v = [[0, 0], [0, 0]]
                ost_d = [0, 0]
                NSL = 3
                for pr in range(2):
                    for qb in range(NQB):
                        nkb = DIAG * qb + DIAG
                        pends = []

                        def emit_pv(kb, sl, mwaits, e_s, qb_=qb, pr_=pr, nkb_=nkb):
                            for hq in range(2):
                                diag = kb >= DIAG * qb_
                                if diag:
                                    w = [(vE, mwaits[hq])]
                                    rhs = pm[sl][:, hq * QW:(hq + 1) * QW]
                                else:
                                    w = [(sE, e_s)]
                                    rhs = pe[sl][:, hq * QW:(hq + 1) * QW]
                                if kb == 0 and o_v[hq][0]:
                                    w.append((vE, o_v[hq][0]))
                                hl_ = 2 * pr_ + hq
                                tv = p.op("tensor",
                                          lambda E_, kb=kb, hl_=hl_, hq=hq,
                                          nkb_=nkb_, rhs=rhs: E_.matmul(
                                              ps_op[hq][:, :],
                                              VS[:, kb * 260 + hl_ * 65:
                                                 kb * 260 + hl_ * 65 + 65],
                                              rhs,
                                              start=(kb == 0),
                                              stop=(kb == nkb_ - 1)),
                                          w=w, inc=tE)
                                pv_done[sl].append((tE, tv))

                        for kb in range(nkb):
                            sl = kb % NSL
                            w = []
                            if exp_s[sl]:
                                w.append((sE, exp_s[sl]))
                            if lb_war[sl]:
                                w.append((vE, lb_war[sl]))
                            for s2 in range(2):
                                p.op("tensor",
                                     lambda E_, s2=s2, pr=pr, kb=kb, qb=qb, sl=sl:
                                     E_.matmul(
                                         ps_sp[sl][:, s2 * QW:(s2 + 1) * QW],
                                         KT[s2 * 64:s2 * 64 + 64,
                                            pr * SEQ + kb * 128:
                                            pr * SEQ + (kb + 1) * 128],
                                         QT[s2 * 64:s2 * 64 + 64,
                                            pr * SEQ + qb * QW:
                                            pr * SEQ + (qb + 1) * QW],
                                         start=True, stop=True),
                                     w=(w if s2 == 0 else ()),
                                     inc=(tE if s2 == 1 else None))
                            t_st = tE.n
                            if len(pends) >= 2:
                                kb_, sl_, mw_, es_ = pends.pop(0)
                                emit_pv(kb_, sl_, mw_, es_)
                            w = [(tE, t_st), (vE, v_cb)] + pv_done[sl]
                            pv_done[sl] = []
                            cidx = (2 * pr) * NQB + qb
                            e_s = p.op("scalar",
                                       lambda E_, sl=sl, cidx=cidx:
                                       E_.activation(
                                           pe[sl][:, 0:2 * QW],
                                           ps_sp[sl][:, 0:2 * QW], AF.Exp,
                                           bias=cb_sb[:, cidx:cidx + 1]),
                                       w=w, inc=sE, chain=False)
                            exp_s[sl] = e_s
                            mwaits = {}
                            if kb >= DIAG * qb:
                                r = kb - DIAG * qb
                                for hq in range(2):
                                    v_m = p.op("vector",
                                               lambda E_, sl=sl, hq=hq, r=r:
                                               E_.tensor_mul(
                                                   pm[sl][:, hq * QW:(hq + 1) * QW],
                                                   pe[sl][:, hq * QW:(hq + 1) * QW],
                                                   masks_sb[:, r * QW:(r + 1) * QW]),
                                               w=[(sE, e_s), (dS, d_msk)], inc=vE)
                                    mwaits[hq] = v_m
                            pends.append((kb, sl, mwaits, e_s))
                        lastsl = (nkb - 1) % NSL
                        for (kb_, sl_, mw_, es_) in pends:
                            emit_pv(kb_, sl_, mw_, es_)
                        t_all = tE.n
                        for hq in range(2):
                            h = 2 * pr + hq
                            bsl = (lastsl + 1 + hq) % NSL
                            p.op("vector", lambda E_, hq=hq: E_.tensor_copy(
                                nrow, ps_op[hq][64:65, :]), w=[(tE, t_all)])
                            v_li = p.op("vector", lambda E_: E_.reciprocal(
                                nrow2, nrow), inc=vE)
                            w = [(vE, v_li)]
                            if exp_s[bsl]:
                                w.append((sE, exp_s[bsl]))
                            t_lb = p.op("tensor", lambda E_, bsl=bsl: E_.matmul(
                                ps_sp[bsl][0:64, 0:QW], ones_rf[0:1, 0:64], nrow2,
                                start=True, stop=True), w=w, inc=tE)
                            v_lbc = p.op("vector", lambda E_, bsl=bsl: E_.tensor_copy(
                                Lb_sb[:, :], ps_sp[bsl][0:64, 0:QW]),
                                w=[(tE, t_lb)], inc=vE)
                            lb_war[bsl] = v_lbc
                            exp_s[bsl] = 0
                            ow = hq
                            w = [(dG, ost_d[ow])] if ost_d[ow] else []
                            v_o = p.op("vector", lambda E_, hq=hq, ow=ow:
                                       E_.tensor_mul(
                                           ost[ow][:, :], ps_op[hq][0:64, :],
                                           Lb_sb[:, :]), w=w, inc=vE)
                            o_v[hq][0] = v_o
                            ost_d[ow] = dma(
                                "gpsimd", dG,
                                o_dram.ap()[h * 64:(h + 1) * 64,
                                            qb * QW:(qb + 1) * QW],
                                ost[ow][:, :], w=[(vE, v_o)], fence=True)
                    for hq in range(2):
                        h = 2 * pr + hq
                        p.op("gpsimd", lambda E_, h=h: E_.collective_compute(
                            "AllGather", ALU.bypass,
                            replica_groups=[[0, 1, 2, 3], [4, 5, 6, 7]],
                            ins=[o_dram.ap()[h * 64:(h + 1) * 64, :].opt()],
                            outs=[oag_h[h].ap().opt()]),
                            w=[(dG, dG.n)], inc=ccB4[h])
                if cfg.DBG:
                    dma("gpsimd", dG, dbg3, o_dram.ap(), fence=True)
                barrier()
                p.flush()

        psc = ExitStack()
        with psc:
            wpj_sb = psc.enter_context(nc.sbuf_tensor("wpj_sb", [128, DC * D], BF16))
            og = psc.enter_context(nc.sbuf_tensor("og", [128, DC * TL], BF16))
            bpr_f = psc.enter_context(nc.sbuf_tensor("bpr_f", [1, D], F32))
            bpr_b = psc.enter_context(nc.sbuf_tensor("bpr_b", [1, D], BF16))
            onesq = psc.enter_context(nc.sbuf_tensor("onesq", [1, QW], BF16))
            ps_p = [psc.enter_context(nc.psum_tensor(f"ps_p{i}", [128, QW], F32))
                    for i in range(2)]
            dma("sync", dS, wpj_sb[:, :], wpjbf.ap(), fence=True)
            d_pj = dma("sync", dS, bpr_f[:, :], bprin)
            p.op("vector", lambda E_: E_.memset(onesq[:, :], 1.0))
            v_bpr = p.op("vector", lambda E_: E_.tensor_copy(bpr_b[:, :], bpr_f[:, :]),
                         w=[(dS, d_pj)], inc=vE)

            def emit_og(E_):
                last = None
                with E_.register("gr0") as gr0, E_.register("gr1") as gr1:
                    E_.reg_load(gr0, rk_sb[0:1, 0:1])
                    E_.reg_mul(gr0, gr0, TL)
                    for kc in range(DC):
                        r_, hh = kc // 2, kc % 2
                        for hq in range(2):
                            hl_ = hh * 2 + hq
                            E_.reg_mov(gr1, r_ * 64 * SEQ)
                            E_.reg_add(gr1, gr1, gr0)
                            last = E_.dma_start(
                                out=og[hq * 64:(hq + 1) * 64,
                                       kc * TL:(kc + 1) * TL],
                                in_=bass.AP(oag_h[hl_], gr1, [[SEQ, 64], [1, TL]]))
                            last.then_inc(dG.h, 16)
                return last
            fence_w = [(dG, dG.n)] if dG.n else []
            dG.n += 16 * DC * 2
            p.op("gpsimd", emit_og,
                 w=fence_w + [(s, 1) for s in ccB4] + [(dS, d_pj)])
            d_og = dG.n
            padd_v = [0, 0]
            for tb in range(NTB):
                for dcb in range(DC):
                    i = (tb * DC + dcb) % 2
                    w = [(dG, d_og), (dS, d_pj), (vE, v_bpr)]
                    if padd_v[i]:
                        w.append((vE, padd_v[i]))
                    p.op("tensor", lambda E_, dcb=dcb, i=i: E_.matmul(
                        ps_p[i][:, :], bpr_b[0:1, dcb * 128:(dcb + 1) * 128],
                        onesq[:, :], start=True, stop=False), w=w)
                    for kc in range(DC):
                        p.op("tensor", lambda E_, kc=kc, dcb=dcb, tb=tb, i=i:
                             E_.matmul(
                                 ps_p[i][:, :],
                                 wpj_sb[:, kc * D + dcb * 128: kc * D + (dcb + 1) * 128],
                                 og[:, kc * TL + tb * QW: kc * TL + (tb + 1) * QW],
                                 start=False, stop=(kc == DC - 1)),
                             inc=(tE if kc == DC - 1 else None))
                    t_p = tE.n
                    xs = x_sb[:, dcb * TL + tb * QW: dcb * TL + (tb + 1) * QW]
                    padd_v[i] = p.op("vector", lambda E_, i=i, xs=xs: E_.tensor_add(
                        xs, ps_p[i][:, :], xs), w=[(tE, t_p)], inc=vE)
            barrier()
            p.flush()
        if cfg.DBG:
            for dc in range(DC):
                dma("gpsimd", dG, dbg2[dc * 128:(dc + 1) * 128, :],
                    x_sb[:, dc * TL:(dc + 1) * TL], fence=(dc == 0))
            barrier()

        emit_ffn("B")

        for dc in range(DC):
            dma("gpsimd", dG, yout[dc * 128:(dc + 1) * 128, :],
                x_sb[:, dc * TL:(dc + 1) * TL], fence=(dc == 0))
        p.q["gpsimd"].append((((dG, dG.n),), None, None))
        p.flush()
    return nc


def make_inputs(cfg, inputs):
    TL, SEQ, QW, DIAG = cfg.TL, cfg.SEQ, cfg.QW, cfg.DIAG
    x = np.asarray(inputs["L0_attn_x"], np.float32)
    w_qkv = np.asarray(inputs["L0_attn_w_qkv"], np.float32)
    w_proj = np.asarray(inputs["L0_attn_w_proj"], np.float32)
    b_proj = np.asarray(inputs["L0_attn_b_proj"], np.float32)
    gw = np.asarray(inputs["L0_ffn_gate_w"], np.float32)
    gb = np.asarray(inputs["L0_ffn_gate_b"], np.float32)
    w1 = np.ascontiguousarray(np.asarray(inputs["L0_ffn_w1"], np.float32))
    b1 = np.asarray(inputs["L0_ffn_b1"], np.float32)
    w2 = np.ascontiguousarray(np.asarray(inputs["L0_ffn_w2"], np.float32))
    b2 = np.ascontiguousarray(np.asarray(inputs["L0_ffn_b2"], np.float32))

    b1l = np.ascontiguousarray(
        b1.reshape(8, 32, 128).transpose(2, 0, 1).reshape(128, 256))
    gwl = np.ascontiguousarray(
        gw.reshape(8, 128, 8).transpose(1, 0, 2).reshape(128, 64))
    gbl = np.ascontiguousarray(gb.reshape(1, 8))
    wpjt = np.ascontiguousarray(
        w_proj.T.reshape(8, 128, 1024).transpose(1, 0, 2).reshape(128, 8192))
    bpr = np.ascontiguousarray(b_proj.reshape(1, 1024))
    idb = np.eye(128, dtype=np.float32).astype(BFNP)
    masks = np.zeros((128, DIAG * QW), np.float32)
    ii = np.arange(128)[:, None]
    jj = np.arange(QW)[None, :]
    for ri in range(DIAG):
        masks[:, ri * QW:(ri + 1) * QW] = (jj >= ii + ri * 128)
    masks = masks.astype(BFNP)
    ebasis = np.zeros((8, 1024), np.float32)
    for e in range(8):
        ebasis[e, e * 128:(e + 1) * 128] = 1.0
    ebasis = ebasis.astype(BFNP)

    maps = []
    for c in range(8):
        g, r = c // 4, c % 4
        xl = np.ascontiguousarray(x[g, r * TL:(r + 1) * TL, :].T)
        hs = slice(4 * r, 4 * r + 4)
        wq = np.ascontiguousarray(
            w_qkv[hs, 0:64, :].transpose(2, 0, 1).reshape(1024, 256))
        wk = np.ascontiguousarray(
            w_qkv[hs, 64:128, :].transpose(2, 0, 1).reshape(1024, 256))
        wv = np.ascontiguousarray(
            w_qkv[hs, 128:192, :].transpose(2, 0, 1).reshape(1024, 256))
        wqkvl = np.concatenate([wq, wk, wv], axis=1)
        wqkvl = np.ascontiguousarray(
            wqkvl.reshape(8, 128, 768).transpose(1, 0, 2).reshape(128, 8 * 768))
        maps.append({
            "x": xl, "w1": w1, "w2": w2, "b1l": b1l, "b2": b2, "gwl": gwl,
            "gb": gbl, "wqkvl": wqkvl, "wpjt": wpjt, "bpr": bpr,
            "idb": idb, "masks": masks, "ebasis": ebasis,
            "rank": np.array([[r]], dtype=np.uint32),
        })
    return maps


def kernel(**inputs):
    cfg = CFG()
    maps = make_inputs(cfg, inputs)
    nc = build_program(cfg)
    res = run_bass_kernel_spmd(nc, maps, core_ids=list(range(8)))
    out = np.empty((2, 8192, 1024), np.float32)
    for c in range(8):
        g, r = c // 4, c % 4
        out[g, r * cfg.TL:(r + 1) * cfg.TL, :] = res.results[c]["out"].T
    return out


# version 18
# speedup vs baseline: 1.1053x; 1.0086x over previous
import numpy as np
import ml_dtypes
import concourse.bass as bass
import concourse.mybir as mybir
from concourse.bass_utils import run_bass_kernel_spmd
from contextlib import ExitStack

BFNP = ml_dtypes.bfloat16
F32 = mybir.dt.float32
BF16 = mybir.dt.bfloat16
U32 = mybir.dt.uint32
AF = mybir.ActivationFunctionType
ALU = mybir.AluOpType
AX = mybir.AxisListType

ENGS = ("sync", "tensor", "vector", "scalar", "gpsimd")


class Sem:
    __slots__ = ("name", "n", "h")

    def __init__(self, name):
        self.name, self.n, self.h = name, 0, None


class P:

    def __init__(self, nc):
        self.nc = nc
        self.q = {e: [] for e in ENGS}
        self.last_wait = {e: {} for e in ENGS}
        self.serialize = {}

    def op(self, eng, fn=None, w=(), inc=None, inc_n=1, chain=True):
        own = self.serialize.get(eng)
        if own is not None and fn is not None:
            if chain:
                w = list(w) + [(own, own.n)]
            if inc is None:
                inc = own
                inc_n = 1
        if inc is not None:
            inc.n += inc_n
        self.q[eng].append((tuple(w), fn, (inc, inc_n) if inc is not None else None))
        return inc.n if inc is not None else None

    def flush(self):
        if not any(self.q[e] for e in ENGS):
            return
        with self.nc.Block() as block:
            for eng in ENGS:
                ops = self.q[eng]
                if not ops:
                    continue
                lastw = self.last_wait[eng]

                def body(E_, ops=ops, lastw=lastw):
                    for waits, fn, inc in ops:
                        for (s, v) in waits:
                            if lastw.get(id(s), 0) < v:
                                E_.wait_ge(s.h, v)
                                lastw[id(s)] = v
                        ins = fn(E_) if fn is not None else None
                        if inc is not None:
                            ins.then_inc(inc[0].h, inc[1])
                getattr(block, eng)(body)
        self.q = {e: [] for e in ENGS}


class CFG:
    def __init__(self, TL=2048, SEQ=8192, QW=512):
        self.TL, self.SEQ, self.QW = TL, SEQ, QW
        self.NTB = TL // QW
        self.NTT = TL // 128
        self.NQB = SEQ // QW
        self.DIAG = QW // 128
        self.D, self.F, self.E, self.DC, self.HL = 1024, 4096, 8, 8, 4
        self.DBG = False
        self.FH = self.F // 2
        self.NFK = self.FH // 128


def build_program(cfg: CFG):
    nc = bass.Bass("TRN2", target_bir_lowering=False)
    TL, SEQ, QW, D, E, DC = cfg.TL, cfg.SEQ, cfg.QW, cfg.D, cfg.E, cfg.DC
    NTB, NTT, NQB, DIAG = cfg.NTB, cfg.NTT, cfg.NQB, cfg.DIAG
    NFK, FH, HL, F = cfg.NFK, cfg.FH, cfg.HL, cfg.F
    NKT = SEQ // 128
    HVS = 2 if cfg.NTB >= 2 else 1

    xin = nc.dram_tensor("x", [D, TL], F32, kind="ExternalInput").ap()
    w1in = nc.dram_tensor("w1", [E, D, F], F32, kind="ExternalInput").ap()
    w2in = nc.dram_tensor("w2", [E, F, D], F32, kind="ExternalInput").ap()
    b1in = nc.dram_tensor("b1l", [128, 2 * E * NFK], F32, kind="ExternalInput").ap()
    b2in = nc.dram_tensor("b2", [E, D], F32, kind="ExternalInput").ap()
    gwin = nc.dram_tensor("gwl", [128, DC * E], F32, kind="ExternalInput").ap()
    gbin = nc.dram_tensor("gb", [1, E], F32, kind="ExternalInput").ap()
    wqkvin = nc.dram_tensor("wqkvl", [128, DC * 768], F32, kind="ExternalInput").ap()
    wpjin = nc.dram_tensor("wpjt", [128, DC * D], F32, kind="ExternalInput").ap()
    bprin = nc.dram_tensor("bpr", [1, D], F32, kind="ExternalInput").ap()
    idbin = nc.dram_tensor("idb", [128, 128], BF16, kind="ExternalInput").ap()
    mskin = nc.dram_tensor("masks", [128, DIAG * QW], BF16, kind="ExternalInput").ap()
    ebin = nc.dram_tensor("ebasis", [E, E * 128], BF16, kind="ExternalInput").ap()
    rkin = nc.dram_tensor("rank", [1, 1], U32, kind="ExternalInput").ap()
    yout = nc.dram_tensor("out", [D, TL], F32, kind="ExternalOutput").ap()
    if cfg.DBG:
        dbg1 = nc.dram_tensor("dbg1", [D, TL], F32, kind="ExternalOutput").ap()
        dbg2 = nc.dram_tensor("dbg2", [D, TL], F32, kind="ExternalOutput").ap()
        dbg3 = nc.dram_tensor("dbg3", [HL * 64, SEQ], BF16, kind="ExternalOutput").ap()
        dbg4 = nc.dram_tensor("dbg4", [E, TL], BF16, kind="ExternalOutput").ap()
        dbg5 = nc.dram_tensor("dbg5", [128, 128], F32, kind="ExternalOutput").ap()
        dbg6 = nc.dram_tensor("dbg6", [128, TL], BF16, kind="ExternalOutput").ap()

    w1bf = nc.dram_tensor("w1bf", [E, 128, DC * F], BF16)
    w2bf = nc.dram_tensor("w2bf", [E, 128, (F // 128) * D], BF16)
    wqkvbf = nc.dram_tensor("wqkvbf", [128, DC * 768], BF16)
    wpjbf = nc.dram_tensor("wpjbf", [128, DC * D], BF16)
    ag_in2 = [nc.dram_tensor(f"ag_in{i}", [D, TL // HVS], BF16) for i in range(HVS)]
    ag_out2 = [nc.dram_tensor(f"ag_out{i}", [4 * D, TL // HVS], BF16)
               for i in range(HVS)]
    o_dram = nc.dram_tensor("o_dram", [HL * 64, SEQ], BF16)
    oag_h = [nc.dram_tensor(f"oag_h{i}", [4 * 64, SEQ], BF16) for i in range(HL)]

    p = P(nc)
    dS, dG = Sem("dS"), Sem("dG")
    tE, vE, sE = Sem("tE"), Sem("vE"), Sem("sE")
    p.serialize = {"vector": vE, "scalar": sE}
    ccA, ccB = Sem("ccA"), Sem("ccB")
    ccA2 = [Sem("ccA0"), Sem("ccA1")]
    ccB4 = [Sem(f"ccB{i}") for i in range(4)]
    dP = [Sem(f"dP{i}") for i in range(3)]
    dQ = [Sem("dQ0"), Sem("dQ1")]
    gE = Sem("gE")
    sems = [dS, dG, tE, vE, sE, ccA, ccB, gE] + dP + dQ + ccA2 + ccB4
    own = {"sync": {dS, *dP},
           "gpsimd": {dG, ccA, ccB, dQ[0], dQ[1], gE, *ccA2, *ccB4},
           "tensor": {tE}, "vector": {vE}, "scalar": {sE}}

    def barrier():
        snap = [(s, s.n) for s in sems if s.n > 0]
        for eng in ENGS:
            w = tuple((s, v) for (s, v) in snap if s not in own[eng])
            if w:
                p.q[eng].append((w, None, None))

    def dma(eng, sem, out, in_, w=(), fence=False):
        if fence and sem.n:
            w = [(sem, sem.n)] + list(w)
        return p.op(eng, lambda E_, o=out, i=in_: E_.dma_start(out=o, in_=i),
                    w=w, inc=sem, inc_n=16)

    st = ExitStack()
    with st:
        for s in sems:
            s.h = st.enter_context(nc.semaphore(s.name))
        SB = lambda name, shape, dt: st.enter_context(nc.sbuf_tensor(name, shape, dt))
        x_sb = SB("x_sb", [128, DC * TL], F32)
        ones_cf = SB("ones_cf", [128, 1], F32)
        ones_cb = SB("ones_cb", [128, 1], BF16)
        ones_rf = SB("ones_rf", [1, 128], F32)
        ones_rb = SB("ones_rb", [1, 128], BF16)
        id_sb = SB("id_sb", [128, 128], BF16)
        eb_sb = SB("eb_sb", [E, E * 128], BF16)
        b1_sb = SB("b1_sb", [128, 2 * E * NFK], F32)
        b2_sb = SB("b2_sb", [E, D], BF16)
        gw_sb = SB("gw_sb", [128, DC * E], BF16)
        gb_sb = SB("gb_sb", [1, E], BF16)
        rk_sb = SB("rk_sb", [1, 2], U32)

        p.op("vector", lambda E_: E_.memset(ones_cf[:, :], 1.0))
        p.op("vector", lambda E_: E_.memset(ones_cb[:, :], 1.0))
        p.op("vector", lambda E_: E_.memset(ones_rf[:, :], 1.0))
        p.op("vector", lambda E_: E_.memset(ones_rb[:, :], 1.0), inc=vE)

        for dc in range(DC):
            dma("sync", dS, x_sb[:, dc * TL:(dc + 1) * TL],
                xin[dc * 128:(dc + 1) * 128, :])
        dma("sync", dS, id_sb[:, :], idbin)
        dma("sync", dS, eb_sb[:, :], ebin)
        dma("sync", dS, b1_sb[:, :], b1in)
        dma("sync", dS, rk_sb[:, 0:1], rkin)

        cvsc = ExitStack()
        stg = [cvsc.enter_context(nc.sbuf_tensor(f"stage{i}", [128, 1024], F32))
               for i in range(3)]
        chks = [cvsc.enter_context(nc.sbuf_tensor(f"chk{i}", [128, 1024], BF16))
                for i in range(2)]
        stageA, stageB = stg[0], stg[1]
        cstate = {"i": 0, "conv_v": [0, 0, 0], "chk_v": [0, 0]}

        def conv(dst_dram, src_dram, ncols, scales=None):
            li = cstate["i"] % 3
            ci = cstate["i"] % 2
            cstate["i"] += 1
            w = [(gE, cstate["conv_v"][li])] if cstate["conv_v"][li] else []
            dv = p.op("sync", lambda E_, o=stg[li][:, 0:ncols], s=src_dram:
                      E_.dma_start(out=o, in_=s), w=w, inc=dP[li], inc_n=16)
            w2_ = [(dP[li], dv), (dQ[ci], dQ[ci].n)]
            parts = scales or [(0, ncols, None)]
            for k, (lo, hi, fac) in enumerate(parts):
                lastp = (k == len(parts) - 1)
                if fac is None:
                    p.op("gpsimd", lambda E_, li=li, ci=ci, lo=lo, hi=hi:
                         E_.tensor_copy(chks[ci][:, lo:hi], stg[li][:, lo:hi]),
                         w=(w2_ if k == 0 else ()), inc=(gE if lastp else None))
                else:
                    p.op("gpsimd", lambda E_, li=li, ci=ci, lo=lo, hi=hi, fac=fac:
                         E_.tensor_scalar_mul(chks[ci][:, lo:hi], stg[li][:, lo:hi],
                                              fac),
                         w=(w2_ if k == 0 else ()), inc=(gE if lastp else None))
            cstate["conv_v"][li] = gE.n
            p.op("gpsimd", lambda E_, o=dst_dram, s=chks[ci][:, 0:ncols]:
                 E_.dma_start(out=o, in_=s), w=[(gE, gE.n)], inc=dQ[ci], inc_n=16)

        def conv_expert(e, part=0, nparts=1):
            jobs = []
            for dc in range(DC):
                for hf in range(4):
                    jobs.append((w1bf.ap()[e, :, dc * F + hf * 1024:
                                           dc * F + (hf + 1) * 1024],
                                 w1in[e, dc * 128:(dc + 1) * 128,
                                      hf * 1024:(hf + 1) * 1024]))
            for fk in range(F // 128):
                jobs.append((w2bf.ap()[e, :, fk * D:(fk + 1) * D],
                             w2in[e, fk * 128:(fk + 1) * 128, :]))
            n = len(jobs)
            lo = n * part // nparts
            hi = n * (part + 1) // nparts
            for dst, s_ in jobs[lo:hi]:
                conv(dst, s_, 1024)
            if hi == n:
                conv_done[e] = (dQ[0].n, dQ[1].n)

        conv_done = {}
        d1 = dma("sync", dS, stageA[0:1, 0:E], gbin, fence=True)
        g1 = p.op("gpsimd", lambda E_: E_.tensor_copy(gb_sb[:, :], stageA[0:1, 0:E]),
                  w=[(dS, d1)], inc=gE)
        cstate["conv_v"][0] = g1
        d2 = dma("sync", dS, stageB[:, 0:DC * E], gwin, fence=True)
        g2 = p.op("gpsimd", lambda E_: E_.tensor_copy(gw_sb[:, :], stageB[:, 0:DC * E]),
                  w=[(dS, d2)], inc=gE)
        cstate["conv_v"][1] = g2
        d3 = dma("sync", dS, stageA[0:E, 0:D], b2in,
                 w=[(gE, cstate["conv_v"][0])], fence=True)
        g3 = p.op("gpsimd", lambda E_: E_.tensor_copy(b2_sb[:, :], stageA[0:E, 0:D]),
                  w=[(dS, d3)], inc=gE)
        cstate["conv_v"][0] = g3
        conv_expert(0)
        for dc in range(DC):
            conv(wqkvbf.ap()[:, dc * 768:(dc + 1) * 768],
                 wqkvin[:, dc * 768:(dc + 1) * 768], 768,
                 scales=[(0, 256, 0.125), (256, 768, None)])
        for j in range(DC * D // 1024):
            conv(wpjbf.ap()[:, j * 1024:(j + 1) * 1024],
                 wpjin[:, j * 1024:(j + 1) * 1024], 1024)
        barrier()
        p.flush()

        def emit_ln(xn_sb, s_):
            (ps_sx, ps_sq, ps_bc, rowm, rowa, rowb, rowc, Rb, C2b, sq_sb) = s_
            last_sq_t = [0, 0]
            last_norm_v = 0
            tb_done = []
            for tb in range(NTB):
                for dc in range(DC):
                    i = dc % 2
                    xs = x_sb[:, dc * TL + tb * QW: dc * TL + (tb + 1) * QW]
                    w = [(tE, last_sq_t[i])] if last_sq_t[i] else []
                    sv = p.op("scalar", lambda E_, i=i, xs=xs: E_.activation(
                        sq_sb[:, i * QW:(i + 1) * QW], xs, AF.Square), w=w, inc=sE)
                    p.op("tensor", lambda E_, xs=xs, dc=dc: E_.matmul(
                        ps_sx[:, :], ones_cf[:, :], xs,
                        start=(dc == 0), stop=(dc == DC - 1)))
                    tv = p.op("tensor", lambda E_, i=i, dc=dc: E_.matmul(
                        ps_sq[:, :], ones_cb[:, :], sq_sb[:, i * QW:(i + 1) * QW],
                        start=(dc == 0), stop=(dc == DC - 1)),
                        w=[(sE, sv)], inc=tE)
                    last_sq_t[i] = tv
                t_stats = tE.n
                p.op("scalar", lambda E_: E_.mul(rowm[:, :], ps_sx[0:1, :], 1.0 / D),
                     w=[(tE, t_stats)])
                s_rows = p.op("scalar", lambda E_: E_.mul(rowa[:, :], ps_sq[0:1, :],
                                                          1.0 / D), inc=sE)
                p.op("vector", lambda E_: E_.tensor_mul(rowb[:, :], rowm[:, :],
                                                        rowm[:, :]), w=[(sE, s_rows)])
                p.op("vector", lambda E_: E_.tensor_sub(rowc[:, :], rowa[:, :],
                                                        rowb[:, :]))
                v_var = p.op("vector", lambda E_: E_.tensor_scalar_add(
                    rowb[:, :], rowc[:, :], 1e-5), inc=vE)
                s_std = p.op("scalar", lambda E_: E_.activation(
                    rowa[:, :], rowb[:, :], AF.Sqrt),
                    w=[(vE, v_var)], inc=sE)
                p.op("vector", lambda E_: E_.reciprocal(rowb[:, :], rowa[:, :]),
                     w=[(sE, s_std)])
                v_c2 = p.op("vector", lambda E_: E_.scalar_tensor_tensor(
                    rowa[:, :], rowm[:, :], -1.0, rowb[:, :], ALU.mult, ALU.mult),
                    inc=vE)
                w = [(vE, v_c2)]
                if last_norm_v:
                    w.append((vE, last_norm_v))
                t_b1 = p.op("tensor", lambda E_: E_.matmul(
                    ps_bc[:, :], ones_rf[:, :], rowb[:, :], start=True, stop=True),
                    w=w, inc=tE)
                v_rb = p.op("vector", lambda E_: E_.tensor_copy(Rb[:, :], ps_bc[:, :]),
                            w=[(tE, t_b1)], inc=vE)
                t_b2 = p.op("tensor", lambda E_: E_.matmul(
                    ps_bc[:, :], ones_rf[:, :], rowa[:, :], start=True, stop=True),
                    w=[(vE, v_rb)], inc=tE)
                v_cb2 = p.op("vector", lambda E_: E_.tensor_copy(C2b[:, :], ps_bc[:, :]),
                             w=[(tE, t_b2)], inc=vE)
                for dc in range(DC):
                    xs = x_sb[:, dc * TL + tb * QW: dc * TL + (tb + 1) * QW]
                    xns = xn_sb[:, dc * TL + tb * QW: dc * TL + (tb + 1) * QW]
                    p.op("vector", lambda E_, xs=xs: E_.tensor_mul(
                        ps_bc[:, :], xs, Rb[:, :]),
                        w=([(vE, v_cb2)] if dc == 0 else ()))
                    last_norm_v = p.op("vector", lambda E_, xns=xns: E_.tensor_add(
                        xns, ps_bc[:, :], C2b[:, :]), inc=vE)
                tb_done.append(vE.n)
            return tb_done

        def mk_ln_scope(tag, stx):
            return (stx.enter_context(nc.psum_tensor("ps_sx" + tag, [1, QW], F32)),
                    stx.enter_context(nc.psum_tensor("ps_sq" + tag, [1, QW], F32)),
                    stx.enter_context(nc.psum_tensor("ps_bc" + tag, [128, QW], F32)),
                    stx.enter_context(nc.sbuf_tensor("rowm" + tag, [1, QW], F32)),
                    stx.enter_context(nc.sbuf_tensor("rowa" + tag, [1, QW], F32)),
                    stx.enter_context(nc.sbuf_tensor("rowb" + tag, [1, QW], F32)),
                    stx.enter_context(nc.sbuf_tensor("rowc" + tag, [1, QW], F32)),
                    stx.enter_context(nc.sbuf_tensor("Rb" + tag, [128, QW], F32)),
                    stx.enter_context(nc.sbuf_tensor("C2b" + tag, [128, QW], F32)),
                    stx.enter_context(nc.sbuf_tensor("sq" + tag, [128, 2 * QW], BF16)))

        def emit_ffn(tag, pre_expert=None):
            sc = ExitStack()
            with sc:
                xn_sb = sc.enter_context(
                    nc.sbuf_tensor("xn" + tag, [128, DC * TL], BF16))
                rfm8 = sc.enter_context(nc.sbuf_tensor("rfm8" + tag, [E, TL], BF16))
                lsc = ExitStack()
                with lsc:
                    lns = mk_ln_scope(tag, lsc)
                    ps_bc = lns[2]
                    ps_g = lsc.enter_context(nc.psum_tensor("ps_g" + tag, [128, E], F32))
                    ps_tr = lsc.enter_context(nc.psum_tensor("ps_tr" + tag, [E, 128], BF16))
                    lg = lsc.enter_context(nc.sbuf_tensor("lg" + tag, [128, E], F32))
                    gsc = lsc.enter_context(nc.sbuf_tensor("gsc" + tag, [128, 96], F32))
                    rtm = lsc.enter_context(nc.sbuf_tensor("rtm" + tag, [128, E], BF16))
                    v_xn = emit_ln(xn_sb, lns)[-1]
                    last_g_v = 0
                    last_tr_s = 0
                    for tt in range(NTT):
                        w = [(vE, v_xn)]
                        if last_g_v:
                            w.append((vE, last_g_v))
                        p.op("tensor", lambda E_: E_.matmul(
                            ps_g[:, :], ones_rb[:, :], gb_sb[:, :],
                            start=True, stop=False), w=w)
                        for dc in range(DC):
                            p.op("tensor", lambda E_, dc=dc, tt=tt: E_.matmul(
                                ps_g[:, :],
                                xn_sb[:, dc * TL + tt * 128: dc * TL + (tt + 1) * 128],
                                gw_sb[:, dc * E:(dc + 1) * E],
                                start=False, stop=(dc == DC - 1)),
                                inc=(tE if dc == DC - 1 else None))
                        t_g = tE.n
                        v_lg = p.op("vector", lambda E_: E_.tensor_copy(
                            lg[:, :], ps_g[:, :]), w=[(tE, t_g)], inc=vE)
                        last_g_v = v_lg
                        mx = gsc[:, 0:8]
                        d1_, d2_ = gsc[:, 8:16], gsc[:, 16:24]
                        s1_, s2_ = gsc[:, 24:32], gsc[:, 32:40]
                        dm = gsc[:, 40:41]
                        w2c, wmc = gsc[:, 41:42], gsc[:, 42:43]
                        m1c, tm_, ngs = gsc[:, 43:51], gsc[:, 51:59], gsc[:, 59:67]
                        m2c, rmx = gsc[:, 67:75], gsc[:, 75:83]
                        p.op("vector", lambda E_: E_.max(mx, lg[:, :]))
                        p.op("vector", lambda E_: E_.tensor_scalar(
                            d1_, lg[:, :], gsc[:, 0:1], 1e-6, ALU.subtract, ALU.add))
                        p.op("vector", lambda E_: E_.tensor_scalar(
                            d2_, lg[:, :], gsc[:, 1:2], 1e-6, ALU.subtract, ALU.add))
                        v_g1 = p.op("vector", lambda E_: E_.tensor_sub(
                            dm, gsc[:, 1:2], gsc[:, 0:1]), inc=vE)
                        p.op("scalar", lambda E_: E_.activation(s1_, d1_, AF.Sign),
                             w=[(vE, v_g1)])
                        p.op("scalar", lambda E_: E_.activation(s2_, d2_, AF.Sign))
                        p.op("scalar", lambda E_: E_.activation(w2c, dm, AF.Sigmoid))
                        s_g = p.op("scalar", lambda E_: E_.activation(
                            wmc, dm, AF.Sigmoid, scale=-1.0), inc=sE)
                        p.op("vector", lambda E_: E_.tensor_scalar(
                            m1c, s1_, 1.0, 0.5, ALU.add, ALU.mult), w=[(sE, s_g)])
                        p.op("vector", lambda E_: E_.tensor_scalar(
                            tm_, s2_, 1.0, 0.5, ALU.add, ALU.mult))
                        p.op("vector", lambda E_: E_.tensor_sub(m2c, tm_, m1c))
                        p.op("vector", lambda E_: E_.tensor_scalar_mul(
                            rmx, m1c, gsc[:, 42:43]))
                        p.op("vector", lambda E_: E_.tensor_scalar_mul(
                            ngs, m2c, gsc[:, 41:42]))
                        w = [(sE, last_tr_s)] if last_tr_s else []
                        v_rtm = p.op("vector", lambda E_: E_.tensor_add(
                            rtm[:, :], rmx, ngs), w=w, inc=vE)
                        t_tr = p.op("tensor", lambda E_: E_.transpose(
                            ps_tr[:, :], rtm[:, :], id_sb[:, :]),
                            w=[(vE, v_rtm)], inc=tE)
                        last_tr_s = p.op("scalar", lambda E_, tt=tt: E_.mul(
                            rfm8[:, tt * 128:(tt + 1) * 128], ps_tr[:, :], 0.5),
                            w=[(tE, t_tr)], inc=sE)
                    if cfg.DBG and tag == "A":
                        dma("gpsimd", dG, dbg4, rfm8[:, :],
                            w=[(sE, last_tr_s)], fence=True)
                        dma("gpsimd", dG, dbg5[:, 0:8], lg[:, :], w=[(vE, vE.n)])
                        dma("gpsimd", dG, dbg5[:, 8:104], gsc[:, :], w=[(sE, sE.n)])
                        dma("gpsimd", dG, dbg6, xn_sb[:, 0:TL], w=[(vE, vE.n)])
                    barrier()
                    p.flush()
                esc = ExitStack()
                with esc:
                    w1b = esc.enter_context(
                        nc.sbuf_tensor("w1b" + tag, [128, DC * FH], BF16))
                    w2b = esc.enter_context(
                        nc.sbuf_tensor("w2b" + tag, [128, NFK * D], BF16))
                    h_sb = esc.enter_context(
                        nc.sbuf_tensor("h" + tag, [128, NFK * QW], BF16))
                    hraw = esc.enter_context(
                        nc.sbuf_tensor("hraw" + tag, [128, 2 * QW], BF16))
                    web = esc.enter_context(
                        nc.sbuf_tensor("web" + tag, [128, 2 * QW], BF16))
                    ps_h = [esc.enter_context(
                        nc.psum_tensor(f"ps_h{i}" + tag, [128, QW], F32))
                        for i in range(2)]
                    ps_o = [esc.enter_context(
                        nc.psum_tensor(f"ps_o{i}" + tag, [128, QW], F32))
                        for i in range(4)]
                    ps_w = esc.enter_context(
                        nc.psum_tensor("ps_w" + tag, [128, QW], F32))
                    w1_free_t = w2_free_t = 0
                    gelu_s = [0, 0]
                    scale_v = [0, 0]
                    oadd_v = [0, 0, 0, 0]
                    web_v = [0, 0]
                    first_b2 = [[True] * DC for _ in range(NTB)]
                    v_xn_local = vE.n
                    for e in range(E):
                        cw = list(conv_done.get(e, ()))
                        cw = [(dQ[0], cw[0]), (dQ[1], cw[1])] if cw else []
                        for fh in range(2):
                            w = ([(tE, w1_free_t)] if w1_free_t else []) + cw
                            src1 = w1bf.ap()[e, :, :].rearrange(
                                "p (dc f) -> p dc f", dc=DC)[:, :, fh * FH:(fh + 1) * FH]
                            dst1 = w1b[:, :].rearrange("p (dc f) -> p dc f", dc=DC)
                            if dS.n:
                                w = [(dS, dS.n)] + w
                            d_w1 = p.op("sync", lambda E_, s=src1, d=dst1:
                                        E_.dma_start(out=d, in_=s), w=w, inc=dS, inc_n=16)
                            w = ([(tE, w2_free_t)] if w2_free_t else []) + cw
                            src2 = w2bf.ap()[e, :, fh * NFK * D:(fh + 1) * NFK * D]
                            d_w2 = p.op("sync", lambda E_, s=src2, d=w2b:
                                        E_.dma_start(out=d[:, :], in_=s), w=w,
                                        inc=dS, inc_n=16)

                            for tb in range(NTB):
                                if pre_expert is not None:
                                    pre_expert(e, fh * NTB + tb, 2 * NTB)
                                i = tb % 2
                                w = [(vE, web_v[i])] if web_v[i] else []
                                t_w = p.op("tensor", lambda E_, e=e, tb=tb: E_.matmul(
                                    ps_w[:, :], eb_sb[:, e * 128:(e + 1) * 128],
                                    rfm8[:, tb * QW:(tb + 1) * QW],
                                    start=True, stop=True), w=w, inc=tE)
                                web_v[i] = p.op("vector", lambda E_, i=i:
                                                E_.tensor_copy(
                                                    web[:, i * QW:(i + 1) * QW],
                                                    ps_w[:, :]),
                                                w=[(tE, t_w)], inc=vE)
                                wi = tb % 2
                                for fk in range(NFK):
                                    hb = fk % 2
                                    w = [(dS, d_w1), (vE, v_xn_local)]
                                    if gelu_s[hb]:
                                        w.append((sE, gelu_s[hb]))
                                    for dc in range(DC):
                                        p.op("tensor",
                                             lambda E_, fk=fk, dc=dc, hb=hb, tb=tb:
                                             E_.matmul(
                                                 ps_h[hb][:, :],
                                                 w1b[:, dc * FH + fk * 128:
                                                     dc * FH + (fk + 1) * 128],
                                                 xn_sb[:, dc * TL + tb * QW:
                                                       dc * TL + (tb + 1) * QW],
                                                 start=(dc == 0), stop=(dc == DC - 1)),
                                             w=(w if dc == 0 else ()),
                                             inc=(tE if dc == DC - 1 else None))
                                    t_h = tE.n
                                    bcol = e * 2 * NFK + fh * NFK + fk
                                    w = [(tE, t_h)]
                                    if scale_v[hb]:
                                        w.append((vE, scale_v[hb]))
                                    g_s = p.op("scalar", lambda E_, hb=hb, bcol=bcol:
                                               E_.activation(
                                                   hraw[:, hb * QW:(hb + 1) * QW],
                                                   ps_h[hb][:, :], AF.Gelu,
                                                   bias=b1_sb[:, bcol:bcol + 1]),
                                               w=w, inc=sE, chain=False)
                                    gelu_s[hb] = g_s
                                    scale_v[hb] = p.op(
                                        "vector", lambda E_, hb=hb, fk=fk, wi=wi:
                                        E_.tensor_mul(
                                            h_sb[:, fk * QW:(fk + 1) * QW],
                                            hraw[:, hb * QW:(hb + 1) * QW],
                                            web[:, wi * QW:(wi + 1) * QW]),
                                        w=[(sE, g_s), (vE, web_v[wi])], inc=vE,
                                        chain=False)
                                v_h = vE.n
                                if tb == NTB - 1:
                                    w1_free_t = tE.n
                                for dcb in range(DC):
                                    ob = dcb % 4
                                    w = [(vE, v_h), (dS, d_w2)]
                                    if oadd_v[ob]:
                                        w.append((vE, oadd_v[ob]))
                                    use_b2 = first_b2[tb][dcb]
                                    first_b2[tb][dcb] = False
                                    if use_b2:
                                        p.op("tensor", lambda E_, dcb=dcb, tb=tb, ob=ob:
                                             E_.matmul(
                                                 ps_o[ob][:, :],
                                                 b2_sb[:, dcb * 128:(dcb + 1) * 128],
                                                 rfm8[:, tb * QW:(tb + 1) * QW],
                                                 start=True, stop=False), w=w)
                                    for fk in range(NFK):
                                        p.op("tensor",
                                             lambda E_, dcb=dcb, fk=fk, ob=ob,
                                             use_b2=use_b2: E_.matmul(
                                                 ps_o[ob][:, :],
                                                 w2b[:, fk * D + dcb * 128:
                                                     fk * D + (dcb + 1) * 128],
                                                 h_sb[:, fk * QW:(fk + 1) * QW],
                                                 start=(fk == 0 and not use_b2),
                                                 stop=(fk == NFK - 1)),
                                             w=(w if (fk == 0 and not use_b2) else ()),
                                             inc=(tE if fk == NFK - 1 else None))
                                    t_o = tE.n
                                    xs = x_sb[:, dcb * TL + tb * QW:
                                              dcb * TL + (tb + 1) * QW]
                                    oadd_v[ob] = p.op(
                                        "vector", lambda E_, ob=ob, xs=xs:
                                        E_.tensor_add(xs, ps_o[ob][:, :], xs),
                                        w=w + [(tE, t_o)], inc=vE, chain=False)
                            w2_free_t = tE.n
                    barrier()
                    p.flush()

        emit_ffn("A", pre_expert=lambda e, part, nparts: (
            conv_expert(e + 1, part, nparts) if e + 1 < E else None))
        cvsc.close()
        if cfg.DBG:
            for dc in range(DC):
                dma("gpsimd", dG, dbg1[dc * 128:(dc + 1) * 128, :],
                    x_sb[:, dc * TL:(dc + 1) * TL], fence=(dc == 0))
            barrier()

        lsc2 = ExitStack()
        with lsc2:
            xn2 = lsc2.enter_context(nc.sbuf_tensor("xn2", [128, DC * TL], BF16))
            lns2 = mk_ln_scope("L2", lsc2)
            tb_done2 = emit_ln(xn2, lns2)
            TLH = TL // HVS
            for hv in range(HVS):
                v_half = tb_done2[(hv + 1) * NTB // HVS - 1]
                for dc in range(DC):
                    dma("gpsimd", dG,
                        ag_in2[hv].ap()[dc * 128:(dc + 1) * 128, :],
                        xn2[:, dc * TL + hv * TLH: dc * TL + (hv + 1) * TLH],
                        w=([(vE, v_half)] if dc == 0 else ()), fence=(dc == 0))
                d_agin = dG.n
                p.op("gpsimd", lambda E_, hv=hv: E_.collective_compute(
                    "AllGather", ALU.bypass,
                    replica_groups=[[0, 1, 2, 3], [4, 5, 6, 7]],
                    ins=[ag_in2[hv].ap().opt()], outs=[ag_out2[hv].ap().opt()]),
                    w=[(dG, d_agin)], inc=ccA2[hv])
            barrier()
            p.flush()

        asc = ExitStack()
        with asc:
            QT = asc.enter_context(nc.sbuf_tensor("QT", [128, 2 * SEQ], BF16))
            KT = asc.enter_context(nc.sbuf_tensor("KT", [128, 2 * SEQ], BF16))
            VS = asc.enter_context(nc.sbuf_tensor("VS", [128, NKT * 260], BF16))
            qsc = ExitStack()
            with qsc:
                NTBH = NTB // HVS
                TH = TL // HVS
                xg = qsc.enter_context(nc.sbuf_tensor("xg", [128, DC * TH], BF16))
                wqkv_sb = qsc.enter_context(
                    nc.sbuf_tensor("wqkv_sb", [128, DC * 768], BF16))
                ps_q = [qsc.enter_context(
                    nc.psum_tensor(f"ps_q{i}", [128, QW], F32)) for i in range(2)]
                ps_v = [qsc.enter_context(
                    nc.psum_tensor(f"ps_v{i}", [128, 256], F32)) for i in range(2)]
                d_wq = dma("sync", dS, wqkv_sb[:, :], wqkvbf.ap(), fence=True)
                v_vones = p.op("vector", lambda E_: E_.memset(
                    VS[:, :].rearrange("p (t h z) -> p t h z", h=HL, z=65)
                    [:, :, :, 64:65], 1.0), inc=vE)
                qk_s = [0, 0]
                v_v = [0, 0]
                xg_free_t = 0
                for rb in range(4):
                  for hv in range(HVS):
                    w = [(ccA2[hv], 1)]
                    if xg_free_t:
                        w.append((tE, xg_free_t))
                    for dc in range(DC):
                        dma("sync", dS, xg[:, dc * TH:(dc + 1) * TH],
                            ag_out2[hv].ap()[rb * D + dc * 128:
                                             rb * D + (dc + 1) * 128, :],
                            w=(w if dc == 0 else ()), fence=(dc == 0))
                    d_xg = dS.n
                    for pr in range(2):
                        for tbh in range(NTBH):
                            tg = rb * NTB + hv * NTBH + tbh
                            for wh, base in (("q", 0), ("k", 256)):
                                i = (pr * NTBH + tbh) % 2
                                w = [(dS, d_xg), (dS, d_wq)]
                                if qk_s[i]:
                                    w.append((sE, qk_s[i]))
                                for dc in range(DC):
                                    p.op("tensor",
                                         lambda E_, dc=dc, pr=pr, tbh=tbh, i=i, base=base:
                                         E_.matmul(
                                             ps_q[i][:, :],
                                             wqkv_sb[:, dc * 768 + base + pr * 128:
                                                     dc * 768 + base + (pr + 1) * 128],
                                             xg[:, dc * TH + tbh * QW:
                                                dc * TH + (tbh + 1) * QW],
                                             start=(dc == 0), stop=(dc == DC - 1)),
                                         w=(w if dc == 0 else ()),
                                         inc=(tE if dc == DC - 1 else None))
                                t_qk = tE.n
                                dst = QT if wh == "q" else KT
                                qk_s[i] = p.op("scalar",
                                               lambda E_, dst=dst, pr=pr, tg=tg, i=i:
                                               E_.copy(
                                                   dst[:, pr * SEQ + tg * QW:
                                                       pr * SEQ + (tg + 1) * QW],
                                                   ps_q[i][:, :]),
                                               w=[(tE, t_qk)], inc=sE, chain=False)
                    for tt in range(TH // 128):
                        i = tt % 2
                        ttg = rb * (TL // 128) + hv * (TH // 128) + tt
                        w = [(dS, d_xg), (dS, d_wq), (vE, v_vones)]
                        if v_v[i]:
                            w.append((vE, v_v[i]))
                        for dc in range(DC):
                            p.op("tensor", lambda E_, dc=dc, tt=tt, i=i: E_.matmul(
                                ps_v[i][:, :],
                                xg[:, dc * TH + tt * 128: dc * TH + (tt + 1) * 128],
                                wqkv_sb[:, dc * 768 + 512: dc * 768 + 768],
                                start=(dc == 0), stop=(dc == DC - 1)),
                                w=(w if dc == 0 else ()),
                                inc=(tE if dc == DC - 1 else None))
                        t_v = tE.n
                        for hl in range(HL):
                            vv = p.op("vector", lambda E_, ttg=ttg, hl=hl, i=i:
                                      E_.tensor_copy(
                                          VS[:, ttg * 260 + hl * 65:
                                             ttg * 260 + hl * 65 + 64],
                                          ps_v[i][:, hl * 64:(hl + 1) * 64]),
                                      w=([(tE, t_v), (vE, v_v[i])] if hl == 0
                                         else [(tE, t_v)]),
                                      inc=(vE if hl == HL - 1 else None),
                                      chain=False)
                        v_v[i] = vv
                    xg_free_t = tE.n
                barrier()
                p.flush()
            fsc = ExitStack()
            with fsc:
                FSB = lambda n_, s_, d_: fsc.enter_context(nc.sbuf_tensor(n_, s_, d_))
                FPS = lambda n_, s_, d_: fsc.enter_context(nc.psum_tensor(n_, s_, d_))
                masks_sb = FSB("masks_sb", [128, DIAG * QW], BF16)
                cb_sb = FSB("cb_sb", [128, HL * NQB], F32)
                NB = HL * NQB
                rows_at = FSB("rows_at", [1, 2 * QW + 4 * NB], F32)
                sqn = FSB("sqn", [64, 2 * QW], BF16)
                pe = [FSB(f"pe{i}", [128, 2 * QW], BF16) for i in range(3)]
                pm = [FSB(f"pm{i}", [128, 2 * QW], BF16) for i in range(3)]
                Lb_sb = FSB("Lb_sb", [64, QW], F32)
                ost = [FSB(f"ost{i}", [64, QW], BF16) for i in range(2)]
                nsc = ExitStack()
                ps_n = [nsc.enter_context(nc.psum_tensor(f"ps_n{i}", [1, QW], F32))
                        for i in range(2)]
                ps_cb = nsc.enter_context(nc.psum_tensor("ps_cb", [128, HL * NQB], F32))
                d_msk = dma("sync", dS, masks_sb[:, :], mskin, fence=True)
                nrow = rows_at[0:1, 0:QW]
                nrow2 = rows_at[0:1, QW:2 * QW]
                bq_r = rows_at[0:1, 2 * QW:2 * QW + NB]
                bk_r = rows_at[0:1, 2 * QW + NB:2 * QW + 2 * NB]
                pk_r = rows_at[0:1, 2 * QW + 2 * NB:2 * QW + 3 * NB]
                nc_r = rows_at[0:1, 2 * QW + 3 * NB:2 * QW + 4 * NB]
                sq_done_t = [0, 0]
                red_done_v = [0, 0]
                for src_, dstr in ((QT, bq_r), (KT, bk_r)):
                    for h in range(HL):
                        pr, s2 = h // 2, h % 2
                        for qb in range(NQB):
                            it = h * NQB + qb
                            sl = it % 2
                            ssrc = src_[s2 * 64:s2 * 64 + 64,
                                        pr * SEQ + qb * QW: pr * SEQ + (qb + 1) * QW]
                            w = [(tE, sq_done_t[sl])] if sq_done_t[sl] else []
                            s_sq = p.op("scalar", lambda E_, ssrc=ssrc, sl=sl:
                                        E_.activation(sqn[:, sl * QW:(sl + 1) * QW],
                                                      ssrc, AF.Square), w=w, inc=sE, chain=False)
                            w = [(sE, s_sq)]
                            if red_done_v[sl]:
                                w.append((vE, red_done_v[sl]))
                            tv = p.op("tensor", lambda E_, sl=sl: E_.matmul(
                                ps_n[sl][:, :], ones_cb[0:64, :],
                                sqn[:, sl * QW:(sl + 1) * QW],
                                start=True, stop=True), w=w, inc=tE)
                            sq_done_t[sl] = tv
                            red_done_v[sl] = p.op(
                                "vector", lambda E_, dstr=dstr, it=it, sl=sl:
                                E_.reduce_max(dstr[0:1, it:it + 1],
                                              ps_n[sl][0:1, :], axis=AX.X),
                                w=[(tE, tv)], inc=vE)
                for h in range(HL):
                    p.op("vector", lambda E_, h=h: E_.tensor_tensor_scan(
                        pk_r[0:1, h * NQB:(h + 1) * NQB],
                        bk_r[0:1, h * NQB:(h + 1) * NQB],
                        bk_r[0:1, h * NQB:(h + 1) * NQB],
                        -1e30, ALU.max, ALU.bypass))
                v_pq = p.op("vector", lambda E_: E_.tensor_mul(nc_r, bq_r, pk_r),
                            inc=vE)
                for pr_ in range(2):
                    a0 = nc_r[0:1, (2 * pr_) * NQB:(2 * pr_ + 1) * NQB]
                    a1 = nc_r[0:1, (2 * pr_ + 1) * NQB:(2 * pr_ + 2) * NQB]
                    p.op("vector", lambda E_, a0=a0, a1=a1: E_.tensor_max(a0, a0, a1))
                    p.op("vector", lambda E_, a0=a0, a1=a1: E_.tensor_copy(a1, a0))
                s_cr = p.op("scalar", lambda E_: E_.activation(
                    nrow2[0:1, 0:NB], nc_r, AF.Sqrt), w=[(vE, vE.n)], inc=sE)
                v_nc = p.op("vector", lambda E_: E_.tensor_scalar_mul(
                    nc_r, nrow2[0:1, 0:NB], -1.0), w=[(sE, s_cr)], inc=vE)
                t_cb = p.op("tensor", lambda E_: E_.matmul(
                    ps_cb[:, :], ones_rf[:, :], nc_r, start=True, stop=True),
                    w=[(vE, v_nc)], inc=tE)
                v_cb = p.op("vector", lambda E_: E_.tensor_copy(
                    cb_sb[:, :], ps_cb[:, :]), w=[(tE, t_cb)], inc=vE)
                p.flush()
                nsc.close()
                NSL = 3
                ps_sp = [FPS(f"ps_sp{i}", [128, 2 * QW], F32) for i in range(NSL)]
                ps_op = [FPS(f"ps_op{i}", [65, QW], F32) for i in range(2)]
                exp_s = [0, 0, 0]
                lb_war = [0, 0, 0]
                pv_done = [[], [], []]
                o_v = [[0, 0], [0, 0]]
                ost_d = [0, 0]
                NSL = 3
                for pr in range(2):
                    for qb in range(NQB):
                        nkb = DIAG * qb + DIAG
                        pends = []

                        def emit_pv(kb, sl, mwaits, e_s, qb_=qb, pr_=pr, nkb_=nkb):
                            for hq in range(2):
                                diag = kb >= DIAG * qb_
                                if diag:
                                    w = [(vE, mwaits[hq])]
                                    rhs = pm[sl][:, hq * QW:(hq + 1) * QW]
                                else:
                                    w = [(sE, e_s)]
                                    rhs = pe[sl][:, hq * QW:(hq + 1) * QW]
                                if kb == 0 and o_v[hq][0]:
                                    w.append((vE, o_v[hq][0]))
                                hl_ = 2 * pr_ + hq
                                tv = p.op("tensor",
                                          lambda E_, kb=kb, hl_=hl_, hq=hq,
                                          nkb_=nkb_, rhs=rhs: E_.matmul(
                                              ps_op[hq][:, :],
                                              VS[:, kb * 260 + hl_ * 65:
                                                 kb * 260 + hl_ * 65 + 65],
                                              rhs,
                                              start=(kb == 0),
                                              stop=(kb == nkb_ - 1)),
                                          w=w, inc=tE)
                                pv_done[sl].append((tE, tv))

                        for kb in range(nkb):
                            sl = kb % NSL
                            w = []
                            if exp_s[sl]:
                                w.append((sE, exp_s[sl]))
                            if lb_war[sl]:
                                w.append((vE, lb_war[sl]))
                            for s2 in range(2):
                                p.op("tensor",
                                     lambda E_, s2=s2, pr=pr, kb=kb, qb=qb, sl=sl:
                                     E_.matmul(
                                         ps_sp[sl][:, s2 * QW:(s2 + 1) * QW],
                                         KT[s2 * 64:s2 * 64 + 64,
                                            pr * SEQ + kb * 128:
                                            pr * SEQ + (kb + 1) * 128],
                                         QT[s2 * 64:s2 * 64 + 64,
                                            pr * SEQ + qb * QW:
                                            pr * SEQ + (qb + 1) * QW],
                                         start=True, stop=True),
                                     w=(w if s2 == 0 else ()),
                                     inc=(tE if s2 == 1 else None))
                            t_st = tE.n
                            if len(pends) >= 2:
                                kb_, sl_, mw_, es_ = pends.pop(0)
                                emit_pv(kb_, sl_, mw_, es_)
                            w = [(tE, t_st), (vE, v_cb)] + pv_done[sl]
                            pv_done[sl] = []
                            cidx = (2 * pr) * NQB + qb
                            e_s = p.op("scalar",
                                       lambda E_, sl=sl, cidx=cidx:
                                       E_.activation(
                                           pe[sl][:, 0:2 * QW],
                                           ps_sp[sl][:, 0:2 * QW], AF.Exp,
                                           bias=cb_sb[:, cidx:cidx + 1]),
                                       w=w, inc=sE, chain=False)
                            exp_s[sl] = e_s
                            mwaits = {}
                            if kb >= DIAG * qb:
                                r = kb - DIAG * qb
                                for hq in range(2):
                                    v_m = p.op("vector",
                                               lambda E_, sl=sl, hq=hq, r=r:
                                               E_.tensor_mul(
                                                   pm[sl][:, hq * QW:(hq + 1) * QW],
                                                   pe[sl][:, hq * QW:(hq + 1) * QW],
                                                   masks_sb[:, r * QW:(r + 1) * QW]),
                                               w=[(sE, e_s), (dS, d_msk)], inc=vE)
                                    mwaits[hq] = v_m
                            pends.append((kb, sl, mwaits, e_s))
                        lastsl = (nkb - 1) % NSL
                        for (kb_, sl_, mw_, es_) in pends:
                            emit_pv(kb_, sl_, mw_, es_)
                        t_all = tE.n
                        for hq in range(2):
                            h = 2 * pr + hq
                            bsl = (lastsl + 1 + hq) % NSL
                            p.op("vector", lambda E_, hq=hq: E_.tensor_copy(
                                nrow, ps_op[hq][64:65, :]), w=[(tE, t_all)])
                            v_li = p.op("vector", lambda E_: E_.reciprocal(
                                nrow2, nrow), inc=vE)
                            w = [(vE, v_li)]
                            if exp_s[bsl]:
                                w.append((sE, exp_s[bsl]))
                            t_lb = p.op("tensor", lambda E_, bsl=bsl: E_.matmul(
                                ps_sp[bsl][0:64, 0:QW], ones_rf[0:1, 0:64], nrow2,
                                start=True, stop=True), w=w, inc=tE)
                            v_lbc = p.op("vector", lambda E_, bsl=bsl: E_.tensor_copy(
                                Lb_sb[:, :], ps_sp[bsl][0:64, 0:QW]),
                                w=[(tE, t_lb)], inc=vE)
                            lb_war[bsl] = v_lbc
                            exp_s[bsl] = 0
                            ow = hq
                            w = [(dG, ost_d[ow])] if ost_d[ow] else []
                            v_o = p.op("vector", lambda E_, hq=hq, ow=ow:
                                       E_.tensor_mul(
                                           ost[ow][:, :], ps_op[hq][0:64, :],
                                           Lb_sb[:, :]), w=w, inc=vE)
                            o_v[hq][0] = v_o
                            ost_d[ow] = dma(
                                "gpsimd", dG,
                                o_dram.ap()[h * 64:(h + 1) * 64,
                                            qb * QW:(qb + 1) * QW],
                                ost[ow][:, :], w=[(vE, v_o)], fence=True)
                    for hq in range(2):
                        h = 2 * pr + hq
                        p.op("gpsimd", lambda E_, h=h: E_.collective_compute(
                            "AllGather", ALU.bypass,
                            replica_groups=[[0, 1, 2, 3], [4, 5, 6, 7]],
                            ins=[o_dram.ap()[h * 64:(h + 1) * 64, :].opt()],
                            outs=[oag_h[h].ap().opt()]),
                            w=[(dG, dG.n)], inc=ccB4[h])
                if cfg.DBG:
                    dma("gpsimd", dG, dbg3, o_dram.ap(), fence=True)
                barrier()
                p.flush()

        psc = ExitStack()
        with psc:
            wpj_sb = psc.enter_context(nc.sbuf_tensor("wpj_sb", [128, DC * D], BF16))
            og = psc.enter_context(nc.sbuf_tensor("og", [128, DC * TL], BF16))
            bpr_f = psc.enter_context(nc.sbuf_tensor("bpr_f", [1, D], F32))
            bpr_b = psc.enter_context(nc.sbuf_tensor("bpr_b", [1, D], BF16))
            onesq = psc.enter_context(nc.sbuf_tensor("onesq", [1, QW], BF16))
            ps_p = [psc.enter_context(nc.psum_tensor(f"ps_p{i}", [128, QW], F32))
                    for i in range(2)]
            dma("sync", dS, wpj_sb[:, :], wpjbf.ap(), fence=True)
            d_pj = dma("sync", dS, bpr_f[:, :], bprin)
            p.op("vector", lambda E_: E_.memset(onesq[:, :], 1.0))
            v_bpr = p.op("vector", lambda E_: E_.tensor_copy(bpr_b[:, :], bpr_f[:, :]),
                         w=[(dS, d_pj)], inc=vE)

            def emit_og(E_):
                last = None
                with E_.register("gr0") as gr0, E_.register("gr1") as gr1:
                    E_.reg_load(gr0, rk_sb[0:1, 0:1])
                    E_.reg_mul(gr0, gr0, TL)
                    for kc in range(DC):
                        r_, hh = kc // 2, kc % 2
                        for hq in range(2):
                            hl_ = hh * 2 + hq
                            E_.reg_mov(gr1, r_ * 64 * SEQ)
                            E_.reg_add(gr1, gr1, gr0)
                            last = E_.dma_start(
                                out=og[hq * 64:(hq + 1) * 64,
                                       kc * TL:(kc + 1) * TL],
                                in_=bass.AP(oag_h[hl_], gr1, [[SEQ, 64], [1, TL]]))
                            last.then_inc(dG.h, 16)
                return last
            fence_w = [(dG, dG.n)] if dG.n else []
            dG.n += 16 * DC * 2
            p.op("gpsimd", emit_og,
                 w=fence_w + [(s, 1) for s in ccB4] + [(dS, d_pj)])
            d_og = dG.n
            padd_v = [0, 0]
            for tb in range(NTB):
                for dcb in range(DC):
                    i = (tb * DC + dcb) % 2
                    w = [(dG, d_og), (dS, d_pj), (vE, v_bpr)]
                    if padd_v[i]:
                        w.append((vE, padd_v[i]))
                    p.op("tensor", lambda E_, dcb=dcb, i=i: E_.matmul(
                        ps_p[i][:, :], bpr_b[0:1, dcb * 128:(dcb + 1) * 128],
                        onesq[:, :], start=True, stop=False), w=w)
                    for kc in range(DC):
                        p.op("tensor", lambda E_, kc=kc, dcb=dcb, tb=tb, i=i:
                             E_.matmul(
                                 ps_p[i][:, :],
                                 wpj_sb[:, kc * D + dcb * 128: kc * D + (dcb + 1) * 128],
                                 og[:, kc * TL + tb * QW: kc * TL + (tb + 1) * QW],
                                 start=False, stop=(kc == DC - 1)),
                             inc=(tE if kc == DC - 1 else None))
                    t_p = tE.n
                    xs = x_sb[:, dcb * TL + tb * QW: dcb * TL + (tb + 1) * QW]
                    padd_v[i] = p.op("vector", lambda E_, i=i, xs=xs: E_.tensor_add(
                        xs, ps_p[i][:, :], xs), w=[(tE, t_p)], inc=vE)
            barrier()
            p.flush()
        if cfg.DBG:
            for dc in range(DC):
                dma("gpsimd", dG, dbg2[dc * 128:(dc + 1) * 128, :],
                    x_sb[:, dc * TL:(dc + 1) * TL], fence=(dc == 0))
            barrier()

        emit_ffn("B")

        for dc in range(DC):
            dma("gpsimd", dG, yout[dc * 128:(dc + 1) * 128, :],
                x_sb[:, dc * TL:(dc + 1) * TL], fence=(dc == 0))
        p.q["gpsimd"].append((((dG, dG.n),), None, None))
        p.flush()
    return nc


def make_inputs(cfg, inputs):
    TL, SEQ, QW, DIAG = cfg.TL, cfg.SEQ, cfg.QW, cfg.DIAG
    x = np.asarray(inputs["L0_attn_x"], np.float32)
    w_qkv = np.asarray(inputs["L0_attn_w_qkv"], np.float32)
    w_proj = np.asarray(inputs["L0_attn_w_proj"], np.float32)
    b_proj = np.asarray(inputs["L0_attn_b_proj"], np.float32)
    gw = np.asarray(inputs["L0_ffn_gate_w"], np.float32)
    gb = np.asarray(inputs["L0_ffn_gate_b"], np.float32)
    w1 = np.ascontiguousarray(np.asarray(inputs["L0_ffn_w1"], np.float32))
    b1 = np.asarray(inputs["L0_ffn_b1"], np.float32)
    w2 = np.ascontiguousarray(np.asarray(inputs["L0_ffn_w2"], np.float32))
    b2 = np.ascontiguousarray(np.asarray(inputs["L0_ffn_b2"], np.float32))

    b1l = np.ascontiguousarray(
        b1.reshape(8, 32, 128).transpose(2, 0, 1).reshape(128, 256))
    gwl = np.ascontiguousarray(
        gw.reshape(8, 128, 8).transpose(1, 0, 2).reshape(128, 64))
    gbl = np.ascontiguousarray(gb.reshape(1, 8))
    wpjt = np.ascontiguousarray(
        w_proj.T.reshape(8, 128, 1024).transpose(1, 0, 2).reshape(128, 8192))
    bpr = np.ascontiguousarray(b_proj.reshape(1, 1024))
    idb = np.eye(128, dtype=np.float32).astype(BFNP)
    masks = np.zeros((128, DIAG * QW), np.float32)
    ii = np.arange(128)[:, None]
    jj = np.arange(QW)[None, :]
    for ri in range(DIAG):
        masks[:, ri * QW:(ri + 1) * QW] = (jj >= ii + ri * 128)
    masks = masks.astype(BFNP)
    ebasis = np.zeros((8, 1024), np.float32)
    for e in range(8):
        ebasis[e, e * 128:(e + 1) * 128] = 1.0
    ebasis = ebasis.astype(BFNP)

    maps = []
    for c in range(8):
        g, r = c // 4, c % 4
        xl = np.ascontiguousarray(x[g, r * TL:(r + 1) * TL, :].T)
        hs = slice(4 * r, 4 * r + 4)
        wq = np.ascontiguousarray(
            w_qkv[hs, 0:64, :].transpose(2, 0, 1).reshape(1024, 256))
        wk = np.ascontiguousarray(
            w_qkv[hs, 64:128, :].transpose(2, 0, 1).reshape(1024, 256))
        wv = np.ascontiguousarray(
            w_qkv[hs, 128:192, :].transpose(2, 0, 1).reshape(1024, 256))
        wqkvl = np.concatenate([wq, wk, wv], axis=1)
        wqkvl = np.ascontiguousarray(
            wqkvl.reshape(8, 128, 768).transpose(1, 0, 2).reshape(128, 8 * 768))
        maps.append({
            "x": xl, "w1": w1, "w2": w2, "b1l": b1l, "b2": b2, "gwl": gwl,
            "gb": gbl, "wqkvl": wqkvl, "wpjt": wpjt, "bpr": bpr,
            "idb": idb, "masks": masks, "ebasis": ebasis,
            "rank": np.array([[r]], dtype=np.uint32),
        })
    return maps


def kernel(**inputs):
    cfg = CFG()
    maps = make_inputs(cfg, inputs)
    nc = build_program(cfg)
    res = run_bass_kernel_spmd(nc, maps, core_ids=list(range(8)))
    out = np.empty((2, 8192, 1024), np.float32)
    for c in range(8):
        g, r = c // 4, c % 4
        out[g, r * cfg.TL:(r + 1) * cfg.TL, :] = res.results[c]["out"].T
    return out
